# version 01
# baseline (speedup 1.0000x reference)
import math
from contextlib import ExitStack

import numpy as np
import ml_dtypes

import concourse.bass as bass
import concourse.mybir as mybir
from concourse.bass import IndirectOffsetOnAxis
from concourse.bass_utils import run_bass_kernel_spmd

F32 = mybir.dt.float32
BF16 = mybir.dt.bfloat16
U32 = mybir.dt.uint32
AF = mybir.ActivationFunctionType
ALU = mybir.AluOpType
AXX = mybir.AxisListType.X

NCORE = 8
HD = 64
QCH = 512
EPS = 1e-5


class Cfg:
    def __init__(self, D=1024, TB=8192, H=16, E=8, FFM=4):
        self.B = 2
        self.D, self.TB, self.H, self.E = D, TB, H, E
        self.FF = FFM * D
        assert D == H * HD and E == NCORE
        self.N = self.B * TB
        self.TOK = self.N // NCORE
        self.GS = NCORE // self.B
        self.UN = self.B * H // NCORE
        self.PAIRS = self.UN // 2
        self.DCH = D // 128
        self.TT = self.TOK // 128
        self.NQC = TB // QCH
        self.NKT = TB // 128
        self.CAP = int(math.ceil(2.0 * self.N / E))
        self.FT = self.FF // 128
        self.NSW = self.CAP // 512
        self.CAPT = self.CAP // 128
        self.SR = self.PAIRS * 128
        assert self.TOK % QCH == 0 and self.CAP % 512 == 0


FULL = Cfg()
MINI = Cfg(D=512, TB=2048, H=8)


class Sem:
    def __init__(self, h):
        self.h = h
        self.n = 0

    def inc(self, k=1):
        self.n += k
        return self.n


_EPIR = {"EPIs", "EPIr", "EPIrr", "EPIb1", "EPIb2", "EPIo", "ROUT"}
ALIAS = {
    "QT": {"PIN", "EIT"}, "KT": {"PIN", "EIT"}, "VV": {"WP", "EIT"},
    "XS0": {"SQ", "EIT"}, "XS1": {"SQ", "EIT"},
    "PIN": {"QT", "KT", "EIT"}, "WP": {"VV", "EIT"},
    "SQ": {"XS0", "XS1", "EIT"},
    "EIT": {"QT", "KT", "VV", "XS0", "XS1", "PIN", "WP", "SQ"},
    "H2": {"MID", "LHA"}, "MID": {"H2", "LHA"}, "LHA": {"H2", "MID"},
    "WQKV": {"W1B"}, "W1B": {"WQKV"},
    "LB": {"EIG", "MO1"}, "EIG": {"LB"}, "MO1": {"LB"},
    "GB2": {"CUM"}, "CUM": {"GB2"},
    "GBB": {"PT", "LHAP", "W2B"},
    "LHAP": {"PT", "GBB", "W2B"}, "PT": {"LHAP", "GBB", "W2B"},
    "MKT": {"CSL0", "CSL1", "W2B"}, "CSL0": {"MKT", "W2B"},
    "CSL1": {"MKT", "W2B"},
    "W2B": {"PT", "CSL0", "CSL1", "MKT", "LHAP", "GBB"},
}
for _r in _EPIR:
    ALIAS[_r] = (ALIAS.get(_r, set()) | _EPIR) - {_r}


class EM:
    ESEM = {"tensor": "eT", "scalar": "eS", "vector": "eV", "gpsimd": "eG",
            "sync": "eY"}

    def __init__(self, blk, sems, pools):
        self.blk = blk
        self.s = sems
        self.pools = pools
        self.pidx = {q: 0 for q in pools}
        self.w = {}
        self.r = {}

    def _pick(self, q):
        pool = self.pools[q]
        i = self.pidx[q]
        self.pidx[q] = (i + 1) % len(pool)
        return pool[i]

    def _expand(self, regs):
        out = set(regs)
        for reg in regs:
            out |= ALIAS.get(reg, set())
        return out

    def _waits(self, rd, wr):
        acc = {}

        def add(pair):
            if pair is None:
                return
            s, n = pair
            if acc.get(s, 0) < n:
                acc[s] = n
        for reg in rd:
            add(self.w.get(reg))
        for reg in self._expand(wr):
            add(self.w.get(reg))
            for s, n in self.r.get(reg, {}).items():
                add((s, n))
        return acc

    def op(self, eng, fn, rd=(), wr=(), dma=False, cc=False):
        if dma:
            sem = self._pick({"sync": "qsy", "scalar": "qsc",
                              "gpsimd": "qg"}[eng])
            step = 16
        elif cc:
            sem = self._pick("qcc")
            step = 1
        else:
            sem = self.s[self.ESEM[eng]]
            step = 1
        waits = dict(self._waits(rd, wr))
        if (dma or cc) and sem.n > 0:
            if waits.get(sem, 0) < sem.n:
                waits[sem] = sem.n
        cnt = [0]

        class _DmaProxy:
            def __init__(self, e):
                self._e = e

            def dma_start(self, *a, **k):
                inst = self._e.dma_start(*a, **k)
                inst.then_inc(sem.h, step)
                cnt[0] += step
                return inst

            def indirect_dma_start(self, *a, **k):
                inst = self._e.indirect_dma_start(*a, **k)
                inst.then_inc(sem.h, step)
                cnt[0] += step
                return inst

            def collective_compute(self, *a, **k):
                inst = self._e.collective_compute(*a, **k)
                inst.then_inc(sem.h, step)
                cnt[0] += step
                return inst

            def __getattr__(self, name):
                return getattr(self._e, name)

        def body(e):
            for s, n in waits.items():
                e.wait_ge(s.h, n)
            if dma or cc:
                fn(_DmaProxy(e))
            else:
                inst = fn(e)
                assert inst is not None, "op fn must return its last instruction"
                inst.then_inc(sem.h, 1)
                cnt[0] = 1
        getattr(self.blk, eng)(body)
        n = sem.inc(cnt[0])
        for reg in self._expand(wr):
            self.w[reg] = (sem, n)
            self.r[reg] = {}
        for reg in rd:
            self.r.setdefault(reg, {})[sem] = n
        return sem, n

    def note_write(self, reg, sem, n):
        for rg in self._expand((reg,)):
            self.w[rg] = (sem, n)
            self.r[rg] = {}

    def note_read(self, reg, sem, n):
        self.r.setdefault(reg, {})[sem] = n

    def need(self, *regs):
        acc = {}
        for reg in regs:
            p = self.w.get(reg)
            if p and acc.get(p[0], 0) < p[1]:
                acc[p[0]] = p[1]
        return list(acc.items())

    def need_w(self, *regs):
        acc = {}
        for reg in self._expand(regs):
            pairs = []
            if self.w.get(reg):
                pairs.append(self.w[reg])
            pairs += list(self.r.get(reg, {}).items())
            for s, n in pairs:
                if acc.get(s, 0) < n:
                    acc[s] = n
        return list(acc.items())


class K:
    def __init__(self, cfg=FULL, dbg=()):
        self.c = cfg
        self.dbg = set(dbg)
        self.nc = bass.Bass("TRN2", target_bir_lowering=False)
        self.d = {}
        self.b = {}
        self.s = {}
        self.dbg_keys = []

    def declare(self):
        nc, c, d = self.nc, self.c, self.d

        def inp(name, shape, dt=F32):
            d[name] = nc.dram_tensor(name, shape, dt, kind="ExternalInput").ap()

        inp("xin", [c.TOK, c.D])
        for p in range(c.PAIRS):
            for lay in ("o", "i"):
                for knd in "qkv":
                    inp(f"{lay}_w{knd}{p}", [128, c.DCH * 128])
            for knd in "qkv":
                inp(f"o_b{knd}{p}", [128])
                inp(f"o_B{knd}{p}", [8, 128])
        inp("o_A", [128, c.DCH * 8])
        for nm in ("i_g1", "i_b1", "i_g2", "i_b2", "o_bp", "b2e"):
            inp(nm, [c.D])
        inp("b1e", [c.FF])
        for lay in ("o", "i"):
            inp(f"{lay}_wp", [128, c.DCH * c.D])
        inp("o_Ap", [128, c.DCH * 8])
        inp("o_Bp", [8, c.D])
        inp("wgate", [128, c.DCH * 8])
        inp("w1e", [c.FT, 128, c.DCH * 128])
        inp("w2e", [c.DCH, 128, c.FT * 128])
        inp("ident_f", [128, 128])
        inp("cc_tab", [128, c.TB])
        inp("ss_tab", [128, c.TB])
        inp("cmask", [128, 4 * QCH])
        inp("tvals", [128, c.TT])
        inp("iot", [128, 1])
        inp("io8", [128, 8])
        inp("pfx", [8])
        inp("ebd", [128, 1])
        inp("mb", [128, 1])

        d["xout"] = nc.dram_tensor("xout", [c.TOK, c.D], F32,
                                   kind="ExternalOutput").ap()

        def internal(name, shape, dt):
            d[name] = nc.dram_tensor(name, shape, dt, kind="Internal").ap()

        internal("ag_in", [c.D, c.TOK], BF16)
        internal("ag_out", [c.GS * c.D, c.TOK], BF16)
        for m in range(c.GS):
            internal(f"st{m}", [c.SR, c.TOK], BF16)
        internal("ao_all", [c.GS * c.GS * c.SR, c.TOK], BF16)
        internal("flat_st", [c.TOK, c.D], BF16)
        internal("flat_all", [c.N, c.D], BF16)
        internal("cnt_in", [1, 8], F32)
        internal("cnt_all", [8, 8], F32)
        internal("tmap_in", [c.E * c.CAP, 1], F32)
        internal("tmap_all", [c.E * c.CAP, 1], F32)
        internal("eo_st", [c.CAP, c.D], BF16)
        internal("eo_all", [c.E * c.CAP, c.D], BF16)
        internal("sp_x1", [c.TOK, c.D], F32)
        internal("sp_y", [c.TOK, c.D], BF16)

    def alloc(self, st):
        nc, c, b = self.nc, self.c, self.b

        def sb(name, shape, dt):
            b[name] = st.enter_context(nc.sbuf_tensor(name, shape, dt))

        sb("R", [128, c.TT * c.D], F32)
        sb("GB2", [128, 2 * c.D], F32)
        sb("SM", [128, 256], F32)
        sb("ROUT", [128, 3072], F32)
        sb("AR1", [128, 32768], BF16)
        sb("AR2", [128, 8192], BF16)
        sb("AR3", [128, 20480], BF16)
        sb("ONES", [128, 1], BF16)
        sb("ONESF", [128, 128], F32)
        sb("IDB", [128, 128], BF16)
        sb("IDF", [128, 128], F32)
        sb("WG", [128, c.DCH * 8], F32)
        sb("EIDX", [128, max(c.CAPT, 24)], U32)
        sb("CIX", [128, 2 * c.TT], U32)
        sb("GTS", [128, 14 * c.TT], F32)
        sb("TVS", [128, c.TT], F32)
        sb("CNT", [8, 8], F32)
        sb("PFXS", [8, 1], F32)
        b["PS6"] = st.enter_context(nc.psum_tensor("PS6", [128, 3072], F32))
        b["PVB"] = st.enter_context(nc.psum_tensor("PVB", [128, 512], F32))
        b["RSB"] = st.enter_context(nc.psum_tensor("RSB", [128, 512], F32))

        A1, A2, A3 = b["AR1"], b["AR2"], b["AR3"]
        v = self.v = {}
        v["QT"] = A1[:, 0:8192]
        v["KT"] = A1[:, 8192:16384]
        v["VV"] = A1[:, 16384:24576]
        v["XS"] = A1[:, 24576:32768]
        v["PIN"] = A1[:, 0:16384]
        v["WP"] = A1[:, 16384:24576]
        v["SQ"] = A1[:, 24576:32768]
        v["EIT"] = A1[:, 0:c.DCH * c.CAP]
        v["PT"] = A2[:, 0:2048]
        v["CS"] = A2[:, 2048:4096]
        v["CM"] = A2[:, 4096:6144]
        v["GBB"] = A2[:, 0:2048]
        v["TPS"] = A2[:, 6144:8192]
        v["H2"] = A3[:, 0:16384]
        v["WQKV"] = A3[:, 16384:18432]
        v["LB"] = A3[:, 18432:20480]
        v["MID"] = A3[:, 0:16384]
        v["W1B"] = A3[:, 16384:18432]
        v["W2B"] = A2[:, 0:c.FT * 128]
        v["EIG"] = A3[:, 18432:19456]
        v["MO1"] = A3[:, 19456:20480]
        b["EPI"] = b["ROUT"][:, 0:3072]
        b["CUM"] = b["GB2"][:, 0:c.TOK]
        b["MKT"] = A2[:, 2048:2048 + c.TOK]
        b["LHA"] = A3[:, 0:c.TB]
        b["LHAP"] = A2[:, 0:c.TOK]
        b["ONEBR"] = A2[:, 2048:2176]
        b["IO8"] = b["SM"][:, 144:152]
        b["B2S"] = b["SM"][:, 152:160]
        b["BPS"] = b["SM"][:, 160:168]
        b["BQB"] = b["SM"][:, 168:174]
        b["B1S"] = b["SM"][:, 176:176 + c.FT]
        b["IOTS"] = b["SM"][:, 224:225]
        b["EBD"] = b["SM"][:, 225:226]
        b["MBS"] = b["SM"][:, 226:227]
        b["EPSC"] = b["SM"][:, 227:228]

    def build(self):
        self.declare()
        with ExitStack() as st:
            self.alloc(st)
            for nm in ("smm", "sex", "spv", "smsk",
                       "eT", "eS", "eV", "eG", "eY"):
                self.s[nm] = Sem(st.enter_context(self.nc.semaphore(nm)))
            pools = {}
            for q, npool in (("qsy", 24), ("qsc", 12), ("qg", 24), ("qcc", 8)):
                pools[q] = [Sem(st.enter_context(self.nc.semaphore(f"{q}{i}")))
                            for i in range(npool)]
            with self.nc.Block() as blk:
                self.blk = blk
                self.em = EM(blk, self.s, pools)
                self.emit()
        return self.nc

    def _fence_pairs(self):
        pairs = [(s, s.n) for s in self.s.values() if s.n]
        for pool in self.em.pools.values():
            pairs += [(s, s.n) for s in pool if s.n]
        return pairs

    def dbg_sbuf(self, name, view, regs=()):
        if name not in self.dbg:
            return
        h = self.nc.dram_tensor("dbg_" + name, [128, view.shape[-1]],
                                view.tensor.dtype, kind="ExternalOutput").ap()
        self.dbg_keys.append("dbg_" + name)
        pairs = self._fence_pairs()

        def f(sy):
            for s, n in pairs:
                sy.wait_ge(s.h, n)
            return sy.dma_start(out=h, in_=view)
        self.em.op("sync", f, rd=tuple(regs), dma=True)

    def dbg_dram(self, name, key):
        if name not in self.dbg:
            return
        src = self.d[key]
        h = self.nc.dram_tensor("dbg_" + name, list(src.tensor.shape),
                                src.tensor.dtype, kind="ExternalOutput").ap()
        self.dbg_keys.append("dbg_" + name)
        pairs = self._fence_pairs()

        def f(sy):
            for s, n in pairs:
                sy.wait_ge(s.h, n)
            return sy.dma_start(out=h, in_=src)
        self.em.op("sync", f, rd=(key,), dma=True)

    def emit(self):
        c = self.c
        self.p0_load()
        self.ln_stage_xln()
        self.allgather_group("ag_in", "ag_out")
        self.dbg_dram("xlnT", "ag_out")
        for p in range(c.PAIRS):
            self.qkv_pair("o", p)
            self.dbg_sbuf(f"o_qt{p}", self.v["QT"][:, 0:c.TB], regs=("QT",))
            self.dbg_sbuf(f"o_kt{p}", self.v["KT"][:, 0:c.TB], regs=("KT",))
            self.dbg_sbuf(f"o_vv{p}", self.v["VV"][:, 0:c.TB], regs=("VV",))
            self.bounds_pair("o", p)
            self.attention_pair("o", p, causal=False)
        self.chunk_ags()
        self.proj_and_residual("o")
        self.dbg_sbuf("x1", self.b["R"][:, 0:c.TT * c.D], regs=("R",))
        self.h2_u_stage()
        self.allgather_group("ag_in", "ag_out")
        self.dbg_dram("uT", "ag_out")
        for p in range(c.PAIRS):
            self.qkv_pair("i", p)
            self.dbg_sbuf(f"i_qt{p}", self.v["QT"][:, 0:c.TB], regs=("QT",))
            self.bounds_pair("i", p)
            self.attention_pair("i", p, causal=True)
        self.chunk_ags()
        self.proj_and_residual("i")
        self.dbg_sbuf("ybuf", self.v["H2"][:, 0:c.TT * c.D], regs=("H2",))
        self.spill_and_flat()
        self.allgather_global("flat_st", "flat_all")
        self.routing()
        self.expert_ffn()
        self.combine_final()

    def p0_load(self):
        c, b, v, d, em = self.c, self.b, self.v, self.d, self.em

        def fm(g):
            g.memset(b["SM"][:, :], 0.0)
            g.memset(b["ROUT"][:, :], 0.0)
            g.drain()
            g.memset(b["EPSC"][:, 0:1], float(EPS))
            g.memset(b["ONES"][:, :], 1.0)
            g.memset(b["ONEBR"][0:1, :], 1.0)
            return g.memset(b["ONESF"][:, :], 1.0)
        em.op("gpsimd", fm, wr=("consts2", "SM", "CSL0", "ROUT"))

        def f1(sy):
            xr = d["xin"].rearrange("(i p) dd -> i p dd", p=128)
            for i in range(c.TT):
                sy.dma_start(out=b["R"][:, i * c.D:(i + 1) * c.D], in_=xr[i, :, :])
            sy.dma_start(out=b["IDF"][:, :], in_=d["ident_f"])
            sy.dma_start(out=b["PFXS"][:, :],
                         in_=d["pfx"].rearrange("(a o) -> a o", o=1))
            sy.dma_start(out=b["EBD"][:, 0:1], in_=d["ebd"])
            sy.dma_start(out=b["MBS"][:, 0:1], in_=d["mb"])
            sy.dma_start(out=b["IOTS"][:, 0:1], in_=d["iot"])
            sy.dma_start(out=b["SM"][:, 144:152], in_=d["io8"])
            sy.dma_start(out=b["TVS"][:, :], in_=d["tvals"])
            sy.dma_start(out=b["B1S"][:, 0:c.FT],
                         in_=d["b1e"].rearrange("(f p) -> p f", p=128), allow_slow_non_contiguous=True)
            sy.dma_start(out=b["B2S"][:, 0:c.DCH],
                         in_=d["b2e"].rearrange("(t p) -> p t", p=128), allow_slow_non_contiguous=True)
            sy.dma_start(out=b["BPS"][:, 0:c.DCH],
                         in_=d["o_bp"].rearrange("(t p) -> p t", p=128), allow_slow_non_contiguous=True)
            sy.dma_start(out=b["WG"][:, :], in_=d["wgate"])
            for p in range(c.PAIRS):
                for ki, knd in enumerate("qkv"):
                    sy.dma_start(out=b["BQB"][:, 3 * p + ki:3 * p + ki + 1],
                                 in_=d[f"o_b{knd}{p}"].rearrange("(p o) -> p o", o=1), allow_slow_non_contiguous=True)
            sy.dma_start(out=b["ROUT"][0:1, 0:c.D],
                         in_=d["i_g2"].rearrange("(a x) -> a x", a=1))
            return sy.dma_start(out=b["ROUT"][0:1, c.D:2 * c.D],
                                in_=d["i_b2"].rearrange("(a x) -> a x", a=1))
        em.op("sync", f1, wr=("R", "consts", "ROUT", "SM"), dma=True)

        def fg(g):
            g.dma_start(out=b["IDB"][:, :], in_=d["ident_f"])
            return g.dma_start(out=v["CM"], in_=d["cmask"])
        em.op("gpsimd", fg, wr=("CM", "consts"), dma=True)

        for j in range(2 * c.DCH):
            half = j // c.DCH
            col = (j % c.DCH) * 128

            def fb(t, half=half, col=col):
                src = b["ROUT"][0:1, half * c.D + col:half * c.D + col + 128]
                t.matmul(b["PS6"][:, 0:128], b["ONESF"][0:1, :], src,
                         start=True, stop=True)
                return t.drain()
            em.op("tensor", fb, rd=("ROUT", "consts2"), wr=("PS0",))

            def fe(ve, j=j):
                return ve.tensor_copy(b["GB2"][:, j * 128:(j + 1) * 128],
                                      b["PS6"][:, 0:128])
            em.op("vector", fe, rd=("PS0",), wr=("GB2", "PS0"))

    def ln_stats(self, src_view, rd_regs):
        c, b, em = self.c, self.b, self.em

        def fv(ve):
            sm = b["SM"]
            for j in range(c.D // 512):
                ve.bn_stats(sm[:, 6 * j:6 * j + 6], src_view[:, 512 * j:512 * (j + 1)])
            ve.drain()
            ve.bn_aggr(sm[:, 48:50], sm[:, 0:6 * (c.D // 512)])
            return ve.drain()
        em.op("vector", fv, rd=rd_regs, wr=("SM",))

        def fs(s):
            sm = b["SM"]
            s.activation(sm[:, 50:51], sm[:, 49:50], AF.Ln, bias=b["EPSC"][:, 0:1])
            s.drain()
            s.activation(sm[:, 51:52], sm[:, 50:51], AF.Exp, scale=-0.5)
            return s.drain()
        em.op("scalar", fs, rd=("SM",), wr=("SM",))

    def ln_stage_xln(self):
        c, b, v, em = self.c, self.b, self.v, self.em
        for i in range(c.TT):
            self.ln_stats(b["R"][:, i * c.D:(i + 1) * c.D], ("R",))

            def fa(ve, i=i):
                sm = b["SM"]
                return ve.tensor_scalar(v["TPS"][:, 0:c.D],
                                        b["R"][:, i * c.D:(i + 1) * c.D],
                                        sm[:, 48:49], sm[:, 51:52],
                                        op0=ALU.subtract, op1=ALU.mult)
            em.op("vector", fa, rd=("R", "SM"), wr=("TPS",))
            self.transpose_stage_tile(i, "ag_in")

    def transpose_stage_tile(self, i, dram):
        c, b, v, em = self.c, self.b, self.v, self.em
        for g0 in range(0, c.DCH, 4):
            def ft(t, g0=g0):
                pb = b["PVB"].ap().bitcast(BF16)
                for k in range(4):
                    cch = g0 + k
                    t.transpose(pb[:, 128 * k:128 * (k + 1)],
                                v["TPS"][:, cch * 128:(cch + 1) * 128],
                                b["IDB"][:, :])
                return t.drain()
            em.op("tensor", ft, rd=("TPS", "consts"), wr=("PVB",))

            def fe(ve):
                pb = b["PVB"].ap().bitcast(BF16)
                return ve.tensor_copy(v["TPS"][:, 1024:1536], pb[:, 0:512])
            em.op("vector", fe, rd=("PVB",), wr=("TPS2", "PVB"))

            def fd(sy, i=i, g0=g0):
                last = None
                for k in range(4):
                    cch = g0 + k
                    last = sy.dma_start(
                        out=self.d[dram][cch * 128:(cch + 1) * 128,
                                         i * 128:(i + 1) * 128],
                        in_=v["TPS"][:, 1024 + 128 * k:1024 + 128 * (k + 1)])
                return last
            em.op("sync", fd, rd=("TPS2",), wr=(dram,), dma=True)

    def allgather_group(self, src, dst):
        c, em = self.c, self.em

        def fg(g):
            rg = [list(range(g0, g0 + c.GS)) for g0 in range(0, NCORE, c.GS)]
            g.collective_compute("AllGather", ALU.bypass, replica_groups=rg,
                                 ins=[self.d[src]], outs=[self.d[dst]])
        em.op("gpsimd", fg, rd=(src,), wr=(dst,), cc=True)

    def allgather_global(self, src, dst, op="AllGather", alu=ALU.bypass):
        em = self.em

        def fg(g):
            g.collective_compute(op, alu, replica_groups=[list(range(NCORE))],
                                 ins=[self.d[src]], outs=[self.d[dst]])
        em.op("gpsimd", fg, rd=(src,), wr=(dst,), cc=True)

    def chunk_ags(self):
        c, em = self.c, self.em
        for m in range(c.GS):
            def fg(g, m=m):
                rg = [list(range(g0, g0 + c.GS))
                      for g0 in range(0, NCORE, c.GS)]
                g.collective_compute(
                    "AllGather", ALU.bypass, replica_groups=rg,
                    ins=[self.d[f"st{m}"]],
                    outs=[self.d["ao_all"][m * c.GS * c.SR:(m + 1) * c.GS * c.SR, :]])
            em.op("gpsimd", fg, rd=(f"st{m}",), wr=(f"aoall{m}",), cc=True)

    def qkv_pair(self, lay, p):
        c, b, v, d, em = self.c, self.b, self.v, self.d, self.em
        do_lora = lay == "o"
        do_hA = do_lora and p == 0

        if do_lora:
            def fwl(g, p=p):
                g.dma_start(out=v["LB"][0:8, 0:128], in_=d[f"o_Bq{p}"])
                g.dma_start(out=v["LB"][0:8, 128:256], in_=d[f"o_Bk{p}"])
                g.dma_start(out=v["LB"][0:8, 256:384], in_=d[f"o_Bv{p}"])
                return g.dma_start(out=v["LB"][:, 1024:1024 + c.DCH * 8],
                                   in_=d["o_A"])
            em.op("gpsimd", fwl, wr=("LB",), dma=True)

        if do_hA:
            for tc in range(c.NQC):
                par = tc % 2
                self._qkv_load_chunk(tc, par)

                def fha(t, tc=tc, par=par):
                    for cch in range(c.DCH):
                        t.matmul(b["PS6"][0:8, 2560:3072],
                                 v["LB"][:, 1024 + cch * 8:1024 + (cch + 1) * 8],
                                 v["XS"][:, 4096 * par + cch * 512:4096 * par + (cch + 1) * 512],
                                 start=(cch == 0), stop=(cch == c.DCH - 1))
                    return t.drain()
                em.op("tensor", fha, rd=(f"XS{par}", "LB"), wr=("PS5",))

                def fhe(s, tc=tc):
                    return s.activation(b["LHA"][0:8, tc * QCH:(tc + 1) * QCH],
                                        b["PS6"][0:8, 2560:3072], AF.Copy,
                                        scale=2.0)
                em.op("scalar", fhe, rd=("PS5",), wr=("LHA", "PS5"))

        for ki, knd in enumerate("qkv"):
            def fw(g, ki=ki, knd=knd, lay=lay, p=p):
                return g.dma_start(out=v["WQKV"][:, 0:c.DCH * 128],
                                   in_=d[f"{lay}_w{knd}{p}"])
            em.op("gpsimd", fw, wr=("WQKV",), dma=True)
            dstv = (v["QT"], v["KT"], v["VV"])[ki]
            dreg = ("QT", "KT", "VV")[ki]
            for tc in range(c.NQC):
                par = tc % 2
                self._qkv_load_chunk(tc, par)
                if lay == "i" and ki == 0:
                    bt = (tc * QCH) % c.TB

                    def ftab(g, par=par, bt=bt):
                        g.dma_start(out=v["CS"][:, par * 1024:par * 1024 + 512],
                                    in_=d["cc_tab"][:, bt:bt + QCH])
                        return g.dma_start(
                            out=v["CS"][:, par * 1024 + 512:par * 1024 + 1024],
                            in_=d["ss_tab"][:, bt:bt + QCH])
                    em.op("gpsimd", ftab, wr=(f"CSL{par}",), dma=True)
                bank = tc % 2

                def fmm(t, tc=tc, par=par, ki=ki, bank=bank):
                    for cch in range(c.DCH):
                        t.matmul(b["PS6"][:, bank * 512:bank * 512 + 512],
                                 v["WQKV"][:, cch * 128:(cch + 1) * 128],
                                 v["XS"][:, 4096 * par + cch * 512:4096 * par + (cch + 1) * 512],
                                 start=(cch == 0),
                                 stop=(not do_lora and cch == c.DCH - 1))
                    if do_lora:
                        t.matmul(b["PS6"][:, bank * 512:bank * 512 + 512],
                                 v["LB"][0:8, ki * 128:(ki + 1) * 128],
                                 b["LHA"][0:8, tc * QCH:(tc + 1) * QCH],
                                 start=False, stop=True)
                    return t.drain()
                rdl = (f"XS{par}", "WQKV", "LB", "LHA") if do_lora else \
                      (f"XS{par}", "WQKV")
                em.op("tensor", fmm, rd=rdl, wr=(f"PS{bank}",))

                def fev(s, tc=tc, ki=ki, bank=bank, p=p, lay=lay, dstv=dstv):
                    if lay == "o":
                        return s.activation(
                            dstv[:, tc * QCH:(tc + 1) * QCH],
                            b["PS6"][:, bank * 512:bank * 512 + 512],
                            AF.Identity,
                            bias=b["BQB"][:, 3 * p + ki:3 * p + ki + 1])
                    return s.copy(dstv[:, tc * QCH:(tc + 1) * QCH],
                                  b["PS6"][:, bank * 512:bank * 512 + 512])
                em.op("scalar", fev, rd=(f"PS{bank}", "consts"),
                      wr=(dreg, f"PS{bank}"))

                if lay == "i" and ki < 2:
                    def frope(ve, tc=tc, par=par, dstv=dstv, dreg=dreg):
                        cs = v["CS"][:, par * 1024:par * 1024 + 512]
                        ss = v["CS"][:, par * 1024 + 512:par * 1024 + 1024]
                        q = dstv[:, tc * QCH:(tc + 1) * QCH]
                        sw = v["PT"][:, 0:512]
                        for hh in (0, 64):
                            ve.tensor_copy(sw[hh:hh + 32, :], q[hh + 32:hh + 64, :])
                            ve.tensor_copy(sw[hh + 32:hh + 64, :], q[hh:hh + 32, :])
                        ve.drain()
                        ve.tensor_tensor(q, q, cs, op=ALU.mult)
                        ve.tensor_tensor(sw[:, :], sw[:, :], ss, op=ALU.mult)
                        ve.drain()
                        ve.tensor_tensor(q, q, sw[:, :], op=ALU.add)
                        return ve.drain()
                    em.op("vector", frope, rd=(dreg, f"CSL{par}"),
                          wr=(dreg, "PT"))

        NB = c.TB // 128
        for j0 in range(0, NB, 4):
            def ftv(t, j0=j0):
                pb = b["RSB"].ap().bitcast(BF16)
                for k in range(4):
                    j = j0 + k
                    t.transpose(pb[:, 128 * k:128 * (k + 1)],
                                v["VV"][:, j * 128:(j + 1) * 128], b["IDB"][:, :])
                return t.drain()
            em.op("tensor", ftv, rd=("VV", "consts"), wr=("RSB",))

            def fev2(ve, j0=j0):
                pb = b["RSB"].ap().bitcast(BF16)
                last = None
                for k in range(4):
                    j = j0 + k
                    last = ve.tensor_copy(v["VV"][:, j * 128:(j + 1) * 128],
                                          pb[:, 128 * k:128 * (k + 1)])
                return last
            em.op("vector", fev2, rd=("RSB",), wr=("VV", "RSB"))

    def _qkv_load_chunk(self, tc, par):
        c, b, v, d, em = self.c, self.b, self.v, self.d, self.em
        m = (tc * QCH) // c.TOK
        lt = (tc * QCH) % c.TOK

        def fld(s, par=par, m=m, lt=lt):
            last = None
            for cch in range(c.DCH):
                last = s.dma_start(
                    out=v["XS"][:, 4096 * par + cch * 512:4096 * par + (cch + 1) * 512],
                    in_=d["ag_out"][m * c.D + cch * 128:m * c.D + (cch + 1) * 128,
                                    lt:lt + QCH])
            return last
        em.op("scalar", fld, rd=("ag_out",), wr=(f"XS{par}",), dma=True)

    def bounds_pair(self, lay, p):
        c, b, v, em = self.c, self.b, self.v, self.em
        SM = b["SM"]
        for t_i, (srcv, sreg) in enumerate(((v["QT"], "QT"), (v["KT"], "KT"))):
            def fsq(ve, srcv=srcv):
                return ve.tensor_tensor(v["SQ"][:, 0:c.TB], srcv[:, 0:c.TB],
                                        srcv[:, 0:c.TB], op=ALU.mult)
            em.op("vector", fsq, rd=(sreg,), wr=("SQ",))
            for ch in range(c.TB // 512):
                def fm(t, ch=ch):
                    t.matmul(b["RSB"][0:1, 0:512], b["ONES"][0:64, 0:1],
                             v["SQ"][0:64, ch * 512:(ch + 1) * 512],
                             start=True, stop=True, tile_position=(0, 0))
                    t.matmul(b["PVB"][0:1, 0:512], b["ONES"][64:128, 0:1],
                             v["SQ"][64:128, ch * 512:(ch + 1) * 512],
                             start=True, stop=True, tile_position=(64, 0))
                    return t.drain()
                em.op("tensor", fm, rd=("SQ", "consts2"), wr=("RSB", "PVB"))

                def facc(ve, ch=ch, t_i=t_i):
                    col = 96 + 2 * t_i
                    last = None
                    for q, ps in ((0, b["RSB"]), (1, b["PVB"])):
                        if ch == 0:
                            last = ve.tensor_scalar(
                                b["EPI"][0:1, 0:512], ps[0:1, 0:512], 1.0, None,
                                op0=ALU.mult, op1=ALU.max,
                                accum_out=SM[0:1, col + q:col + q + 1])
                        else:
                            last = ve.tensor_scalar(
                                b["EPI"][0:1, 0:512], ps[0:1, 0:512], 1.0,
                                SM[0:1, 110 + 2 * t_i + q:111 + 2 * t_i + q],
                                op0=ALU.mult, op1=ALU.max,
                                accum_out=SM[0:1, col + q:col + q + 1])
                        ve.drain()
                    return last
                em.op("vector", facc, rd=("RSB", "PVB", "SM"),
                      wr=("SM", "EPIs", "RSB", "PVB"))

                if ch < c.TB // 512 - 1:
                    def fcp(ve, t_i=t_i):
                        col = 96 + 2 * t_i
                        ve.tensor_copy(SM[0:1, 110 + 2 * t_i:111 + 2 * t_i],
                                       SM[0:1, col:col + 1])
                        ve.drain()
                        return ve.tensor_copy(
                            SM[0:1, 111 + 2 * t_i:112 + 2 * t_i],
                            SM[0:1, col + 1:col + 2])
                    em.op("vector", fcp, rd=("SM",), wr=("SM",))

        def fc(ve):
            ve.tensor_tensor(SM[0:1, 104:105], SM[0:1, 96:97], SM[0:1, 97:98],
                             op=ALU.max)
            ve.drain()
            ve.tensor_tensor(SM[0:1, 105:106], SM[0:1, 98:99], SM[0:1, 99:100],
                             op=ALU.max)
            ve.drain()
            return ve.tensor_tensor(SM[0:1, 106:107], SM[0:1, 104:105],
                                    SM[0:1, 105:106], op=ALU.mult)
        em.op("vector", fc, rd=("SM",), wr=("SM",))

        def fsr(s):
            return s.activation(SM[0:1, 107:108], SM[0:1, 106:107], AF.Sqrt)
        em.op("scalar", fsr, rd=("SM",), wr=("SM",))

        def fneg(ve):
            return ve.tensor_scalar(SM[0:1, 108:109], SM[0:1, 107:108],
                                    -0.125, None, op0=ALU.mult)
        em.op("vector", fneg, rd=("SM",), wr=("SM",))

        def fbc(t):
            t.matmul(b["RSB"][:, 0:1], b["ONESF"][0:1, :], SM[0:1, 108:109],
                     start=True, stop=True)
            return t.drain()
        em.op("tensor", fbc, rd=("SM", "consts2"), wr=("RSB",))

        def fcb(ve):
            return ve.tensor_copy(SM[:, 80:81], b["RSB"][:, 0:1])
        em.op("vector", fcb, rd=("RSB",), wr=("SMcb", "RSB"))

    def attention_pair(self, lay, p, causal):
        c, b, v, em = self.c, self.b, self.v, self.em
        smm, sex, spv, smsk = (self.s[x] for x in ("smm", "sex", "spv", "smsk"))
        eV = self.s["eV"]
        cb = b["SM"][:, 80:81]
        QT, KTv, VV, PT = v["QT"], v["KT"], v["VV"], v["PT"]
        PS6, PVB, RSB = b["PS6"], b["PVB"], b["RSB"]
        w_pe = em.need("QT", "KT", "VV", "SMcb") + \
            em.need_w("PVB", "RSB", "PS0", "PS1", "PS2", "PS3", "PS4", "PS5")
        w_act = em.need("SMcb") + em.need_w("PT")
        w_dve = em.need("CM")
        a0 = sex.n
        first = True
        for qc in range(c.NQC):
            kmax = c.NKT if not causal else 4 * (qc + 1)
            diag0 = kmax - 4 if causal else kmax + 1
            smm0, spv0, sex0, smsk0 = smm.n, spv.n, sex.n, smsk.n
            nep = eV.n

            def fpe(t, qc=qc, kmax=kmax, diag0=diag0, a0=a0, spv0=spv0,
                    sex0=sex0, smsk0=smsk0, nep=nep, first=first, causal=causal):
                if first:
                    for s, n in w_pe:
                        t.wait_ge(s.h, n)
                t.wait_ge(eV.h, nep)
                for kt in range(kmax + 1):
                    if kt < kmax:
                        a = a0 + kt
                        j3 = a % 3
                        if kt >= 3:
                            t.wait_ge(sex.h, sex0 + kt - 2)
                        elif sex0 > 0:
                            t.wait_ge(sex.h, sex0)
                        t.matmul(PS6[:, j3 * 512:(j3 + 1) * 512],
                                 KTv[0:64, kt * 128:(kt + 1) * 128],
                                 QT[0:64, qc * 512:(qc + 1) * 512],
                                 start=True, stop=True, tile_position=(0, 0))
                        t.matmul(PS6[:, (3 + j3) * 512:(4 + j3) * 512],
                                 KTv[64:128, kt * 128:(kt + 1) * 128],
                                 QT[64:128, qc * 512:(qc + 1) * 512],
                                 start=True, stop=True, tile_position=(64, 0)
                                 ).then_inc(smm.h, 1)
                    if kt >= 1:
                        kp = kt - 1
                        ap = a0 + kp
                        t.wait_ge(sex.h, sex0 + kp + 1)
                        if causal and kp >= diag0:
                            t.wait_ge(smsk.h, smsk0 + (kp - diag0) + 1)
                        par = ap % 2
                        stt = kp == 0
                        t.matmul(PVB[0:64, :], VV[:, kp * 128:kp * 128 + 64],
                                 PT[:, par * 1024:par * 1024 + 512],
                                 start=stt, stop=False, tile_position=(0, 0),
                                 skip_group_check=True)
                        t.matmul(PVB[64:128, :],
                                 VV[:, kp * 128 + 64:kp * 128 + 128],
                                 PT[:, par * 1024 + 512:par * 1024 + 1024],
                                 start=stt, stop=False, tile_position=(0, 64),
                                 skip_group_check=True)
                        t.matmul(RSB[0:1, :], b["ONES"][:, 0:1],
                                 PT[:, par * 1024:par * 1024 + 512],
                                 start=stt, stop=False, tile_position=(0, 0),
                                 skip_group_check=True)
                        t.matmul(RSB[64:65, :], b["ONES"][:, 0:1],
                                 PT[:, par * 1024 + 512:par * 1024 + 1024],
                                 start=stt, stop=False, tile_position=(0, 64),
                                 skip_group_check=True).then_inc(spv.h, 1)
            self.blk.tensor(fpe)
            smm.inc(kmax)
            spv.inc(kmax)

            def fex(s, kmax=kmax, a0=a0, smm0=smm0, spv0=spv0, first=first):
                if first:
                    for ss, n in w_act:
                        s.wait_ge(ss.h, n)
                for kt in range(kmax):
                    a = a0 + kt
                    j3, par = a % 3, a % 2
                    s.wait_ge(smm.h, smm0 + kt + 1)
                    if kt >= 2:
                        s.wait_ge(spv.h, spv0 + kt - 1)
                    elif spv0 > 0:
                        s.wait_ge(spv.h, spv0)
                    s.activation(PT[:, par * 1024:par * 1024 + 512],
                                 PS6[:, j3 * 512:(j3 + 1) * 512],
                                 AF.Exp, bias=cb, scale=0.125)
                    s.activation(PT[:, par * 1024 + 512:par * 1024 + 1024],
                                 PS6[:, (3 + j3) * 512:(4 + j3) * 512],
                                 AF.Exp, bias=cb, scale=0.125
                                 ).then_inc(sex.h, 1)
            self.blk.scalar(fex)
            sex.inc(kmax)

            if causal:
                def fmk(ve, kmax=kmax, diag0=diag0, a0=a0, sex0=sex0,
                        first=first):
                    if first:
                        for ss, n in w_dve:
                            ve.wait_ge(ss.h, n)
                    for j in range(4):
                        kt = diag0 + j
                        par = (a0 + kt) % 2
                        ve.wait_ge(sex.h, sex0 + kt + 1)
                        ve.tensor_tensor(PT[:, par * 1024:par * 1024 + 512],
                                         PT[:, par * 1024:par * 1024 + 512],
                                         v["CM"][:, j * 512:(j + 1) * 512],
                                         op=ALU.mult)
                        ve.drain()
                        ve.tensor_tensor(
                            PT[:, par * 1024 + 512:par * 1024 + 1024],
                            PT[:, par * 1024 + 512:par * 1024 + 1024],
                            v["CM"][:, j * 512:(j + 1) * 512],
                            op=ALU.mult)
                        ve.drain().then_inc(smsk.h, 1)
                self.blk.vector(fmk)
                smsk.inc(4)

            npv = spv.n

            def ep1(ve, npv=npv):
                ve.wait_ge(spv.h, npv)
                ve.memset(b["EPI"][:, 0:512], 1.0)
                ve.drain()
                ve.tensor_copy(b["EPI"][0:1, 0:512], RSB[0:1, :])
                ve.tensor_copy(b["EPI"][64:65, 0:512], RSB[64:65, :])
                ve.drain()
                return ve.tensor_copy(b["EPI"][:, 2560:3072], PVB[:, :])
            em.op("vector", ep1, rd=(), wr=("EPIs", "EPIo", "RSB", "PVB"))

            def ep2(t):
                for k in range(4):
                    t.transpose(RSB[:, 128 * k:128 * (k + 1)],
                                b["EPI"][:, 128 * k:128 * (k + 1)],
                                b["IDF"][:, :])
                return t.drain()
            em.op("tensor", ep2, rd=("EPIs", "consts"), wr=("RSB",))

            def ep3(ve):
                return ve.reciprocal(b["EPI"][:, 512:1024], RSB[:, :])
            em.op("vector", ep3, rd=("RSB",), wr=("EPIr", "RSB"))

            def ep4(t):
                for k in range(4):
                    t.transpose(RSB[:, 128 * k:128 * (k + 1)],
                                b["EPI"][:, 512 + 128 * k:512 + 128 * (k + 1)],
                                b["IDF"][:, :])
                return t.drain()
            em.op("tensor", ep4, rd=("EPIr", "consts"), wr=("RSB",))

            def ep5(ve):
                return ve.tensor_copy(b["EPI"][:, 1024:1536], RSB[:, :])
            em.op("vector", ep5, rd=("RSB",), wr=("EPIrr", "RSB"))

            def ep6(t):
                t.matmul(RSB[:, :], b["ONESF"][0:1, :], b["EPI"][0:1, 1024:1536],
                         start=True, stop=True)
                return t.drain()
            em.op("tensor", ep6, rd=("EPIrr", "consts2"), wr=("RSB",))

            def ep7(ve):
                return ve.tensor_copy(b["EPI"][:, 1536:2048], RSB[:, :])
            em.op("vector", ep7, rd=("RSB",), wr=("EPIb1", "RSB"))

            def ep8(t):
                t.matmul(RSB[:, :], b["ONESF"][64:65, :],
                         b["EPI"][64:65, 1024:1536], start=True, stop=True)
                return t.drain()
            em.op("tensor", ep8, rd=("EPIrr", "consts2"), wr=("RSB",))

            def ep9(ve):
                ve.tensor_copy(b["EPI"][:, 2048:2560], RSB[:, :])
                ve.drain()
                ve.tensor_tensor(v["TPS"][0:64, 0:512],
                                 b["EPI"][0:64, 2560:3072],
                                 b["EPI"][0:64, 1536:2048], op=ALU.mult)
                ve.drain()
                return ve.tensor_tensor(v["TPS"][64:128, 0:512],
                                        b["EPI"][64:128, 2560:3072],
                                        b["EPI"][64:128, 2048:2560],
                                        op=ALU.mult)
            em.op("vector", ep9, rd=("EPIb1", "EPIb2", "EPIo"),
                  wr=("TPS", "RSB", "EPIb2"))

            m = (qc * QCH) // c.TOK
            lt = (qc * QCH) % c.TOK

            def fst(sy, m=m, lt=lt, p=p):
                return sy.dma_start(
                    out=self.d[f"st{m}"][p * 128:(p + 1) * 128, lt:lt + QCH],
                    in_=v["TPS"][:, 0:512])
            em.op("sync", fst, rd=("TPS",), wr=(f"st{m}",), dma=True)
            a0 += kmax
            first = False
        em.note_write("PT", sex, sex.n)
        for bank in range(6):
            em.note_write(f"PS{bank}", smm, smm.n)
            em.note_read(f"PS{bank}", sex, sex.n)
        em.note_read("QT", smm, smm.n)
        em.note_read("KT", smm, smm.n)
        em.note_read("VV", spv, spv.n)
        em.note_read("SMcb", sex, sex.n)

    def proj_and_residual(self, lay):
        c, b, v, d, em = self.c, self.b, self.v, self.d, self.em
        do_lora = lay == "o"

        def fidx(ve):
            for j in range(c.DCH):
                ve.tensor_scalar(b["SM"][:, 208 + j:209 + j], b["IOTS"][:, :],
                                 1.0, float(128 * j), op0=ALU.mult, op1=ALU.add)
            ve.drain()
            for j in range(c.DCH):
                ve.tensor_tensor(b["SM"][:, 208 + j:209 + j],
                                 b["SM"][:, 208 + j:209 + j], b["MBS"][:, :],
                                 op=ALU.add)
            ve.drain()
            return ve.tensor_copy(b["EIDX"][:, 16:16 + c.DCH],
                                  b["SM"][:, 208:208 + c.DCH])
        em.op("vector", fidx, rd=("consts",), wr=("SM", "EIDX"))

        def fgat(g):
            last = None
            for j in range(c.DCH):
                last = g.indirect_dma_start(
                    out=v["PIN"][:, j * c.TOK:(j + 1) * c.TOK], out_offset=None,
                    in_=d["ao_all"],
                    in_offset=IndirectOffsetOnAxis(ap=b["EIDX"][:, 16 + j:17 + j],
                                                   axis=0))
            return last
        em.op("gpsimd", fgat, rd=("ao_all", "aoall0", "aoall1", "aoall2", "aoall3", "EIDX"), wr=("PIN",), dma=True)

        def fwp(g, lay=lay):
            last = g.dma_start(out=v["WP"][:, 0:c.DCH * c.D], in_=d[f"{lay}_wp"])
            if lay == "o":
                g.dma_start(out=v["LB"][:, 1024:1024 + c.DCH * 8], in_=d["o_Ap"])
                last = g.dma_start(out=v["LB"][0:8, 0:c.D], in_=d["o_Bp"])
            return last
        em.op("gpsimd", fwp, wr=("WP", "LB"), dma=True)

        if do_lora:
            for ch in range(c.TOK // 512):
                def fha(t, ch=ch):
                    for cch in range(c.DCH):
                        t.matmul(b["PS6"][0:8, 2560:3072],
                                 v["LB"][:, 1024 + cch * 8:1024 + (cch + 1) * 8],
                                 v["PIN"][:, cch * c.TOK + ch * 512:cch * c.TOK + (ch + 1) * 512],
                                 start=(cch == 0), stop=(cch == c.DCH - 1))
                    return t.drain()
                em.op("tensor", fha, rd=("PIN", "LB"), wr=("PS5",))

                def fhe(s, ch=ch):
                    return s.activation(
                        b["LHAP"][0:8, ch * 512:(ch + 1) * 512],
                        b["PS6"][0:8, 2560:3072], AF.Copy, scale=2.0)
                em.op("scalar", fhe, rd=("PS5",), wr=("LHAP", "PS5"))

        for co in range(c.DCH):
            for ch in range(c.TOK // 512):
                bank = (co * (c.TOK // 512) + ch) % 2

                def fpj(t, co=co, ch=ch, bank=bank):
                    for cch in range(c.DCH):
                        t.matmul(
                            b["PS6"][:, bank * 512:bank * 512 + 512],
                            v["WP"][:, (cch * c.DCH + co) * 128:(cch * c.DCH + co + 1) * 128],
                            v["PIN"][:, cch * c.TOK + ch * 512:cch * c.TOK + (ch + 1) * 512],
                            start=(cch == 0),
                            stop=(not do_lora and cch == c.DCH - 1))
                    if do_lora:
                        t.matmul(b["PS6"][:, bank * 512:bank * 512 + 512],
                                 v["LB"][0:8, co * 128:(co + 1) * 128],
                                 b["LHAP"][0:8, ch * 512:(ch + 1) * 512],
                                 start=False, stop=True)
                    return t.drain()
                rdl = ("PIN", "WP", "LB", "LHAP") if do_lora else ("PIN", "WP")
                em.op("tensor", fpj, rd=rdl, wr=(f"PS{bank}",))

                def fze(s, co=co, bank=bank, lay=lay):
                    if lay == "o":
                        return s.activation(
                            v["TPS"][:, 0:512],
                            b["PS6"][:, bank * 512:bank * 512 + 512],
                            AF.Identity, bias=b["BPS"][:, co:co + 1])
                    return s.copy(v["TPS"][:, 0:512],
                                  b["PS6"][:, bank * 512:bank * 512 + 512])
                em.op("scalar", fze, rd=(f"PS{bank}", "consts"),
                      wr=("TPS", f"PS{bank}"))

                def fzt(t):
                    pb = b["RSB"].ap().bitcast(BF16)
                    for k in range(4):
                        t.transpose(pb[:, 128 * k:128 * (k + 1)],
                                    v["TPS"][:, 128 * k:128 * (k + 1)],
                                    b["IDB"][:, :])
                    return t.drain()
                em.op("tensor", fzt, rd=("TPS", "consts"), wr=("RSB",))

                def fzv(ve, co=co, ch=ch, lay=lay):
                    pb = b["RSB"].ap().bitcast(BF16)
                    last = None
                    for k in range(4):
                        ti = ch * 4 + k
                        dst = v["H2"][:, ti * c.D + co * 128:ti * c.D + (co + 1) * 128]
                        if lay == "o":
                            last = ve.tensor_copy(dst, pb[:, 128 * k:128 * (k + 1)])
                        else:
                            last = ve.tensor_tensor(
                                dst, dst, pb[:, 128 * k:128 * (k + 1)],
                                op=ALU.add)
                        ve.drain()
                    return last
                em.op("vector", fzv, rd=("RSB", "H2"), wr=("H2", "RSB"))

        if lay == "o":
            for i in range(c.TT):
                self.ln_stats(v["H2"][:, i * c.D:(i + 1) * c.D], ("H2",))

                def fadd(ve, i=i):
                    sm = b["SM"]
                    ve.tensor_scalar(b["ROUT"][:, 0:c.D],
                                     v["H2"][:, i * c.D:(i + 1) * c.D],
                                     sm[:, 48:49], sm[:, 51:52],
                                     op0=ALU.subtract, op1=ALU.mult)
                    ve.drain()
                    return ve.tensor_tensor(b["R"][:, i * c.D:(i + 1) * c.D],
                                            b["R"][:, i * c.D:(i + 1) * c.D],
                                            b["ROUT"][:, 0:c.D], op=ALU.add)
                em.op("vector", fadd, rd=("H2", "SM"), wr=("R", "ROUT"))

    def h2_u_stage(self):
        c, b, v, em = self.c, self.b, self.v, self.em

        def fgb(g):
            g.dma_start(out=v["LB"][0:1, 0:c.D],
                        in_=self.d["i_g1"].rearrange("(a x) -> a x", a=1))
            return g.dma_start(out=v["LB"][0:1, c.D:2 * c.D],
                               in_=self.d["i_b1"].rearrange("(a x) -> a x", a=1))
        em.op("gpsimd", fgb, wr=("LB",), dma=True)
        for j in range(2 * c.DCH):
            half = j // c.DCH
            col = (j % c.DCH) * 128

            def fb2(t, half=half, col=col):
                t.matmul(b["PS6"][:, 0:128], b["ONEBR"][0:1, :],
                         v["LB"][0:1, half * c.D + col:half * c.D + col + 128],
                         start=True, stop=True)
                return t.drain()
            em.op("tensor", fb2, rd=("LB", "CSL0"), wr=("PS0",))

            def fe2(ve, j=j):
                return ve.tensor_copy(v["GBB"][:, j * 128:(j + 1) * 128],
                                      b["PS6"][:, 0:128])
            em.op("vector", fe2, rd=("PS0",), wr=("GBB", "PS0"))
        for i in range(c.TT):
            self.ln_stats(b["R"][:, i * c.D:(i + 1) * c.D], ("R",))

            def fa(ve, i=i):
                sm = b["SM"]
                ve.tensor_scalar(v["H2"][:, i * c.D:(i + 1) * c.D],
                                 b["R"][:, i * c.D:(i + 1) * c.D],
                                 sm[:, 48:49], sm[:, 51:52],
                                 op0=ALU.subtract, op1=ALU.mult)
                ve.drain()
                ve.tensor_tensor(v["TPS"][:, 0:c.D],
                                 v["H2"][:, i * c.D:(i + 1) * c.D],
                                 v["GBB"][:, 0:c.D], op=ALU.mult)
                ve.drain()
                return ve.tensor_tensor(v["TPS"][:, 0:c.D], v["TPS"][:, 0:c.D],
                                        v["GBB"][:, c.D:2 * c.D],
                                        op=ALU.add)
            em.op("vector", fa, rd=("R", "SM", "GBB"), wr=("H2", "TPS"))
            self.transpose_stage_tile(i, "ag_in")

    def spill_and_flat(self):
        c, b, v, d, em = self.c, self.b, self.v, self.d, self.em

        def fsp(sy):
            xr = d["sp_x1"].rearrange("(i p) dd -> i p dd", p=128)
            yr = d["sp_y"].rearrange("(i p) dd -> i p dd", p=128)
            last = None
            for i in range(c.TT):
                sy.dma_start(out=xr[i, :, :], in_=b["R"][:, i * c.D:(i + 1) * c.D])
                last = sy.dma_start(out=yr[i, :, :],
                                    in_=v["H2"][:, i * c.D:(i + 1) * c.D])
            return last
        em.op("sync", fsp, rd=("R", "H2"), wr=("sp_x1", "sp_y"), dma=True)

        for i in range(c.TT):
            self.ln_stats(v["H2"][:, i * c.D:(i + 1) * c.D], ("H2",))

            def fa(ve, i=i):
                sm = b["SM"]
                ve.tensor_scalar(b["R"][:, i * c.D:(i + 1) * c.D],
                                 v["H2"][:, i * c.D:(i + 1) * c.D],
                                 sm[:, 48:49], sm[:, 51:52],
                                 op0=ALU.subtract, op1=ALU.mult)
                ve.drain()
                ve.tensor_tensor(b["R"][:, i * c.D:(i + 1) * c.D],
                                 b["R"][:, i * c.D:(i + 1) * c.D],
                                 b["GB2"][:, 0:c.D], op=ALU.mult)
                ve.drain()
                ve.tensor_tensor(b["R"][:, i * c.D:(i + 1) * c.D],
                                 b["R"][:, i * c.D:(i + 1) * c.D],
                                 b["GB2"][:, c.D:2 * c.D], op=ALU.add)
                ve.drain()
                return ve.tensor_copy(v["TPS"][:, 0:c.D],
                                      b["R"][:, i * c.D:(i + 1) * c.D])
            em.op("vector", fa, rd=("H2", "SM", "GB2", "sp_x1"),
                  wr=("R", "TPS"))

            def fdm(sy, i=i):
                return sy.dma_start(
                    out=d["flat_st"].rearrange("(a p) dd -> a p dd", p=128)[i, :, :],
                    in_=v["TPS"][:, 0:c.D])
            em.op("sync", fdm, rd=("TPS",), wr=("flat_st",), dma=True)

    def routing(self):
        c, b, v, d, em = self.c, self.b, self.v, self.d, self.em

        def fz(ve):
            return ve.memset(b["ROUT"][:, 0:c.E * c.CAP // 128], 0.0)
        em.op("vector", fz, wr=("ROUT",))

        def fzd(sy):
            return sy.dma_start(
                out=d["tmap_in"].rearrange("(p f) o -> p (f o)", p=128),
                in_=b["ROUT"][:, 0:c.E * c.CAP // 128])
        em.op("sync", fzd, rd=("ROUT",), wr=("tmap_in",), dma=True)

        for i in range(c.TT):
            for g0 in range(0, c.DCH, 4):
                def ftt(t, i=i, g0=g0):
                    for k in range(4):
                        cch = g0 + k
                        t.transpose(
                            b["RSB"][:, 128 * k:128 * (k + 1)],
                            b["R"][:, i * c.D + cch * 128:i * c.D + (cch + 1) * 128],
                            b["IDF"][:, :])
                    return t.drain()
                em.op("tensor", ftt, rd=("R", "consts"), wr=("RSB",))

                def fte(ve, g0=g0):
                    return ve.tensor_copy(b["ROUT"][:, g0 * 128:(g0 + 4) * 128],
                                          b["RSB"][:, :])
                em.op("vector", fte, rd=("RSB",), wr=("ROUT", "RSB"))

            def flg(t):
                for cch in range(c.DCH):
                    t.matmul(b["PS6"][0:8, 2560:2688],
                             b["WG"][:, cch * 8:(cch + 1) * 8],
                             b["ROUT"][:, cch * 128:(cch + 1) * 128],
                             start=(cch == 0), stop=(cch == c.DCH - 1))
                return t.drain()
            em.op("tensor", flg, rd=("ROUT", "consts"), wr=("PS5",))

            def fle(ve):
                return ve.tensor_copy(b["ROUT"][0:8, 2048:2176],
                                      b["PS6"][0:8, 2560:2688])
            em.op("vector", fle, rd=("PS5",), wr=("ROUT", "PS5"))

            def flt(t):
                t.transpose(b["RSB"][:, 0:8], b["ROUT"][0:8, 2048:2176],
                            b["IDF"][0:8, 0:8])
                return t.drain()
            em.op("tensor", flt, rd=("ROUT", "consts"), wr=("RSB",))

            def fsm(ve):
                L = b["ROUT"][:, 2304:2312]
                ve.tensor_copy(L, b["RSB"][:, 0:8])
                ve.drain()
                ve.tensor_reduce(b["ROUT"][:, 2312:2313], L, axis=AXX, op=ALU.max)
                ve.drain()
                return ve.tensor_scalar(b["ROUT"][:, 2313:2314],
                                        b["ROUT"][:, 2312:2313], -1.0, None,
                                        op0=ALU.mult)
            em.op("vector", fsm, rd=("RSB",), wr=("ROUT", "RSB"))

            def fex(s):
                return s.activation(b["ROUT"][:, 2320:2328],
                                    b["ROUT"][:, 2304:2312], AF.Exp,
                                    bias=b["ROUT"][:, 2313:2314])
            em.op("scalar", fex, rd=("ROUT",), wr=("ROUT",))

            def ftop(ve, i=i):
                E8 = b["ROUT"][:, 2320:2328]
                ve.tensor_reduce(b["ROUT"][:, 2328:2329], E8, axis=AXX,
                                 op=ALU.add)
                ve.drain()
                ve.reciprocal(b["ROUT"][:, 2329:2330], b["ROUT"][:, 2328:2329])
                ve.drain()
                G = b["GTS"][:, 14 * i:14 * i + 8]
                ve.tensor_scalar(G, E8, b["ROUT"][:, 2329:2330], None,
                                 op0=ALU.mult)
                ve.drain()
                ve.max(b["ROUT"][:, 2330:2338], G)
                ve.drain()
                ve.max_index(b["EIDX"][:, 8:16], b["ROUT"][:, 2330:2338], G)
                ve.drain()
                ve.tensor_copy(b["ROUT"][:, 2338:2346], b["EIDX"][:, 8:16])
                ve.drain()
                ve.tensor_copy(b["GTS"][:, 14 * i + 8:14 * i + 10],
                               b["ROUT"][:, 2338:2340])
                ve.tensor_copy(b["GTS"][:, 14 * i + 10:14 * i + 12],
                               b["ROUT"][:, 2330:2332])
                ve.drain()
                ve.tensor_scalar(b["ROUT"][:, 2346:2354], b["IO8"][:, :],
                                 b["GTS"][:, 14 * i + 8:14 * i + 9], None,
                                 op0=ALU.is_equal)
                ve.drain()
                ve.tensor_scalar(b["ROUT"][:, 2354:2362], b["IO8"][:, :],
                                 b["GTS"][:, 14 * i + 9:14 * i + 10], None,
                                 op0=ALU.is_equal)
                ve.drain()
                ve.tensor_tensor(b["ROUT"][:, 2362:2370],
                                 b["ROUT"][:, 2346:2354],
                                 b["ROUT"][:, 2354:2362], op=ALU.add)
                ve.drain()
                return ve.tensor_copy(v["TPS"][:, 0:8], b["ROUT"][:, 2362:2370])
            em.op("vector", ftop, rd=("ROUT", "consts"),
                  wr=("ROUT", "GTS", "TPS", "EIDX"))

            def fmt(t):
                pb = b["RSB"].ap().bitcast(BF16)
                t.transpose(pb[0:8, 0:128], v["TPS"][:, 0:8], b["IDB"][:, :])
                return t.drain()
            em.op("tensor", fmt, rd=("TPS", "consts"), wr=("RSB",))

            def fmte(ve, i=i):
                pb = b["RSB"].ap().bitcast(BF16)
                return ve.tensor_copy(b["MKT"][0:8, i * 128:(i + 1) * 128],
                                      pb[0:8, 0:128])
            em.op("vector", fmte, rd=("RSB",), wr=("MKT", "RSB"))

        def fscan(ve):
            ve.tensor_tensor_scan(b["CUM"][0:8, :], b["MKT"][0:8, :],
                                  b["MKT"][0:8, :], 0.0, op0=ALU.add,
                                  op1=ALU.bypass)
            ve.drain()
            return ve.tensor_copy(b["SM"][0:8, 120:121],
                                  b["CUM"][0:8, c.TOK - 1:c.TOK])
        em.op("vector", fscan, rd=("MKT",), wr=("CUM", "SM"))

        def fcdm(sy):
            return sy.dma_start(out=d["cnt_in"].rearrange("o e -> e o"),
                                in_=b["SM"][0:8, 120:121],
                                allow_slow_non_contiguous=True)
        em.op("sync", fcdm, rd=("SM",), wr=("cnt_in",), dma=True)
        self.allgather_global("cnt_in", "cnt_all")

        def fcld(sy):
            return sy.dma_start(out=b["CNT"][:, :], in_=d["cnt_all"])
        em.op("sync", fcld, rd=("cnt_all",), wr=("CNT",), dma=True)

        def fbase(t):
            t.matmul(b["RSB"][0:1, 0:8], b["PFXS"][0:8, 0:1], b["CNT"][0:8, :],
                     start=True, stop=True)
            return t.drain()
        em.op("tensor", fbase, rd=("CNT",), wr=("RSB",))

        def fbse(ve):
            return ve.tensor_copy(b["SM"][0:1, 128:136], b["RSB"][0:1, 0:8])
        em.op("vector", fbse, rd=("RSB",), wr=("SM", "RSB"))

        def fbbc(t):
            t.matmul(b["RSB"][:, 0:8], b["ONESF"][0:1, :],
                     b["SM"][0:1, 128:136], start=True, stop=True)
            return t.drain()
        em.op("tensor", fbbc, rd=("SM", "consts2"), wr=("RSB",))

        def fbm1(ve):
            return ve.tensor_scalar(b["SM"][:, 136:144], b["RSB"][:, 0:8],
                                    -1.0, None, op0=ALU.add)
        em.op("vector", fbm1, rd=("RSB",), wr=("SM", "RSB"))

        for i in range(c.TT):
            def fcb(t, i=i):
                t.transpose(b["RSB"][:, 0:8],
                            b["CUM"][0:8, i * 128:(i + 1) * 128],
                            b["IDF"][0:8, 0:8])
                return t.drain()
            em.op("tensor", fcb, rd=("CUM", "consts"), wr=("RSB",))

            def fpos(ve, i=i):
                P = b["ROUT"][:, 2304:2312]
                ve.tensor_tensor(P, b["RSB"][:, 0:8], b["SM"][:, 136:144],
                                 op=ALU.add)
                ve.drain()
                last = None
                for rr in range(2):
                    oh = b["ROUT"][:, 2346 + 8 * rr:2354 + 8 * rr]
                    ve.tensor_scalar(oh, b["IO8"][:, :],
                                     b["GTS"][:, 14 * i + 8 + rr:14 * i + 9 + rr],
                                     None, op0=ALU.is_equal)
                    ve.drain()
                    ve.tensor_tensor(b["ROUT"][:, 2376:2384], P, oh, op=ALU.mult)
                    ve.drain()
                    ve.tensor_reduce(b["ROUT"][:, 2384 + rr:2385 + rr],
                                     b["ROUT"][:, 2376:2384], axis=AXX,
                                     op=ALU.add)
                    ve.drain()
                for rr in range(2):
                    pr = b["ROUT"][:, 2384 + rr:2385 + rr]
                    kp = b["ROUT"][:, 2386 + rr:2387 + rr]
                    ve.tensor_scalar(kp, pr, float(c.CAP - 1), None,
                                     op0=ALU.is_le)
                    ve.drain()
                    ve.tensor_tensor(
                        b["GTS"][:, 14 * i + 12 + rr:14 * i + 13 + rr],
                        b["GTS"][:, 14 * i + 10 + rr:14 * i + 11 + rr],
                        kp, op=ALU.mult)
                    ve.drain()
                    dv = b["ROUT"][:, 2388 + rr:2389 + rr]
                    ve.tensor_scalar(dv,
                                     b["GTS"][:, 14 * i + 8 + rr:14 * i + 9 + rr],
                                     float(c.CAP), None, op0=ALU.mult)
                    ve.drain()
                    ve.tensor_tensor(dv, dv, pr, op=ALU.add)
                    ve.drain()
                    ve.tensor_scalar(b["ROUT"][:, 2390:2391], kp, -3.0e7, 3.0e7,
                                     op0=ALU.mult, op1=ALU.add)
                    ve.drain()
                    ve.tensor_tensor(dv, dv, b["ROUT"][:, 2390:2391],
                                     op=ALU.add)
                    ve.drain()
                    ve.tensor_copy(b["EIDX"][:, 20 + rr:21 + rr], dv)
                    ve.drain()
                    mn = b["ROUT"][:, 2391:2392]
                    ve.tensor_scalar(mn, pr, float(c.CAP - 1), None,
                                     op0=ALU.min)
                    ve.drain()
                    ve.tensor_scalar(b["ROUT"][:, 2392:2393],
                                     b["GTS"][:, 14 * i + 8 + rr:14 * i + 9 + rr],
                                     float(c.CAP), None, op0=ALU.mult)
                    ve.drain()
                    ve.tensor_tensor(mn, mn, b["ROUT"][:, 2392:2393],
                                     op=ALU.add)
                    ve.drain()
                    last = ve.tensor_copy(b["CIX"][:, 2 * i + rr:2 * i + rr + 1],
                                          mn)
                    ve.drain()
                return last
            em.op("vector", fpos, rd=("RSB", "ROUT", "GTS", "consts"),
                  wr=("ROUT", "GTS", "EIDX", "CIX", "RSB"))

            def fsc(g, i=i):
                last = None
                for rr in range(2):
                    last = g.indirect_dma_start(
                        out=d["tmap_in"],
                        out_offset=IndirectOffsetOnAxis(
                            ap=b["EIDX"][:, 20 + rr:21 + rr], axis=0),
                        in_=b["TVS"][:, i:i + 1], in_offset=None,
                        bounds_check=c.E * c.CAP - 1, oob_is_err=False)
                return last
            em.op("gpsimd", fsc, rd=("EIDX", "consts", "tmap_in"),
                  wr=("tmap_in",), dma=True)

        self.allgather_global("tmap_in", "tmap_all", op="AllReduce",
                              alu=ALU.add)
        self.dbg_dram("tmap", "tmap_all")

    def expert_ffn(self):
        c, b, v, d, em = self.c, self.b, self.v, self.d, self.em

        def fti(ve):
            ve.tensor_scalar(b["SM"][:, 216:217], b["IOTS"][:, :],
                             float(c.CAPT - 1), None, op0=ALU.min)
            ve.drain()
            ve.tensor_tensor(b["SM"][:, 216:217], b["SM"][:, 216:217],
                             b["EBD"][:, :], op=ALU.add)
            ve.drain()
            return ve.tensor_copy(b["EIDX"][:, 0:1], b["SM"][:, 216:217])
        em.op("vector", fti, rd=("consts",), wr=("SM", "EIDX"))

        def ftg(g):
            return g.indirect_dma_start(
                out=b["ROUT"][:, 2048:2176], out_offset=None,
                in_=d["tmap_all"].rearrange("(a p) o -> a (p o)", p=128),
                in_offset=IndirectOffsetOnAxis(ap=b["EIDX"][:, 0:1], axis=0))
        em.op("gpsimd", ftg, rd=("tmap_all", "EIDX"), wr=("ROUT",), dma=True)

        def ftt(t):
            t.transpose(b["RSB"][:, 0:c.CAPT], b["ROUT"][0:c.CAPT, 2048:2176],
                        b["IDF"][0:c.CAPT, 0:c.CAPT])
            return t.drain()
        em.op("tensor", ftt, rd=("ROUT", "consts"), wr=("RSB",))

        def fte(ve):
            ve.tensor_copy(b["ROUT"][:, 0:c.CAPT], b["RSB"][:, 0:c.CAPT])
            ve.drain()
            return ve.tensor_copy(b["EIDX"][:, 0:c.CAPT],
                                  b["ROUT"][:, 0:c.CAPT])
        em.op("vector", fte, rd=("RSB",), wr=("ROUT", "EIDX", "RSB"))

        for j in range(c.CAPT):
            def fgi(g, j=j):
                return g.indirect_dma_start(
                    out=v["EIG"][:, 0:c.D], out_offset=None, in_=d["flat_all"],
                    in_offset=IndirectOffsetOnAxis(ap=b["EIDX"][:, j:j + 1],
                                                   axis=0))
            em.op("gpsimd", fgi, rd=("flat_all", "EIDX"), wr=("EIG",),
                  dma=True)

            for g0 in range(0, c.DCH, 4):
                def ftr(t, g0=g0):
                    pb = b["RSB"].ap().bitcast(BF16)
                    for k in range(4):
                        cch = g0 + k
                        t.transpose(pb[:, 128 * k:128 * (k + 1)],
                                    v["EIG"][:, cch * 128:(cch + 1) * 128],
                                    b["IDB"][:, :])
                    return t.drain()
                em.op("tensor", ftr, rd=("EIG", "consts"), wr=("RSB",))

                def fev(ve, j=j, g0=g0):
                    pb = b["RSB"].ap().bitcast(BF16)
                    last = None
                    for k in range(4):
                        cch = g0 + k
                        last = ve.tensor_copy(
                            v["EIT"][:, cch * c.CAP + j * 128:cch * c.CAP + (j + 1) * 128],
                            pb[:, 128 * k:128 * (k + 1)])
                    return last
                em.op("vector", fev, rd=("RSB",), wr=("EIT", "RSB"))

        for s_i in range(c.NSW):
            r0 = s_i * 512
            for ft in range(c.FT):
                par = ft % 2

                def fw1(g, ft=ft, par=par):
                    w = c.DCH * 128
                    return g.dma_start(
                        out=v["W1B"][:, par * w:(par + 1) * w],
                        in_=d["w1e"][ft, :, :])
                em.op("gpsimd", fw1, wr=(f"W1B{par}",), dma=True)

                def fmm(t, par=par, r0=r0):
                    w = c.DCH * 128
                    for cch in range(c.DCH):
                        t.matmul(
                            b["PS6"][:, par * 512:par * 512 + 512],
                            v["W1B"][:, par * w + cch * 128:par * w + (cch + 1) * 128],
                            v["EIT"][:, cch * c.CAP + r0:cch * c.CAP + r0 + 512],
                            start=(cch == 0), stop=(cch == c.DCH - 1))
                    return t.drain()
                em.op("tensor", fmm, rd=(f"W1B{par}", "EIT"), wr=(f"PS{par}",))

                def fgl(s, ft=ft, par=par):
                    return s.activation(v["MID"][:, ft * 512:(ft + 1) * 512],
                                        b["PS6"][:, par * 512:par * 512 + 512],
                                        AF.Gelu, bias=b["B1S"][:, ft:ft + 1])
                em.op("scalar", fgl, rd=(f"PS{par}", "consts"),
                      wr=("MID", f"PS{par}"))

            for dt in range(c.DCH):
                par = dt % 2

                def fw2(g, dt=dt):
                    return g.dma_start(out=v["W2B"][:, 0:c.FT * 128],
                                       in_=d["w2e"][dt, :, :])
                em.op("gpsimd", fw2, wr=("W2B",), dma=True)

                def fm2(t, par=par):
                    for fc in range(c.FT):
                        t.matmul(b["PS6"][:, (2 + par) * 512:(3 + par) * 512],
                                 v["W2B"][:, fc * 128:(fc + 1) * 128],
                                 v["MID"][:, fc * 512:(fc + 1) * 512],
                                 start=(fc == 0), stop=(fc == c.FT - 1))
                    return t.drain()
                em.op("tensor", fm2, rd=("W2B", "MID"), wr=(f"PS{2 + par}",))

                def fo(s, dt=dt, par=par):
                    return s.activation(
                        v["TPS"][:, 0:512],
                        b["PS6"][:, (2 + par) * 512:(3 + par) * 512],
                        AF.Identity, bias=b["B2S"][:, dt:dt + 1])
                em.op("scalar", fo, rd=(f"PS{2 + par}", "consts"),
                      wr=("TPS", f"PS{2 + par}"))

                def ftr2(t):
                    pb = b["RSB"].ap().bitcast(BF16)
                    for k in range(4):
                        t.transpose(pb[:, 128 * k:128 * (k + 1)],
                                    v["TPS"][:, 128 * k:128 * (k + 1)],
                                    b["IDB"][:, :])
                    return t.drain()
                em.op("tensor", ftr2, rd=("TPS", "consts"), wr=("RSB",))

                def fcv2(ve):
                    pb = b["RSB"].ap().bitcast(BF16)
                    return ve.tensor_copy(v["TPS"][:, 1024:1536], pb[:, 0:512])
                em.op("vector", fcv2, rd=("RSB",), wr=("TPS2", "RSB"))

                def fst2(sy, dt=dt, r0=r0):
                    last = None
                    for k in range(4):
                        last = sy.dma_start(
                            out=d["eo_st"][r0 + k * 128:r0 + (k + 1) * 128,
                                           dt * 128:(dt + 1) * 128],
                            in_=v["TPS"][:, 1024 + 128 * k:1024 + 128 * (k + 1)])
                    return last
                em.op("sync", fst2, rd=("TPS2",), wr=("eo_st",), dma=True)

        self.allgather_global("eo_st", "eo_all")
        self.dbg_dram("eoall", "eo_all")

    def combine_final(self):
        c, b, v, d, em = self.c, self.b, self.v, self.d, self.em
        for i in range(c.TT):
            def fr(sy, i=i):
                xr = d["sp_x1"].rearrange("(a p) dd -> a p dd", p=128)
                yr = d["sp_y"].rearrange("(a p) dd -> a p dd", p=128)
                sy.dma_start(out=b["R"][:, i * c.D:(i + 1) * c.D],
                             in_=xr[i, :, :])
                return sy.dma_start(out=v["MID"][:, 0:c.D], in_=yr[i, :, :])
            em.op("sync", fr, rd=("sp_x1", "sp_y"), wr=("R", "MID"), dma=True)

            def fg0(g, i=i):
                g.indirect_dma_start(
                    out=v["EIG"][:, 0:c.D], out_offset=None, in_=d["eo_all"],
                    in_offset=IndirectOffsetOnAxis(
                        ap=b["CIX"][:, 2 * i:2 * i + 1], axis=0))
                return g.indirect_dma_start(
                    out=v["MO1"][:, 0:c.D], out_offset=None, in_=d["eo_all"],
                    in_offset=IndirectOffsetOnAxis(
                        ap=b["CIX"][:, 2 * i + 1:2 * i + 2], axis=0))
            em.op("gpsimd", fg0, rd=("eo_all", "CIX"), wr=("EIG", "MO1"),
                  dma=True)

            def fmo(ve, i=i):
                ve.tensor_scalar(b["ROUT"][:, 0:c.D], v["EIG"][:, 0:c.D],
                                 b["GTS"][:, 14 * i + 12:14 * i + 13], None,
                                 op0=ALU.mult)
                ve.drain()
                ve.scalar_tensor_tensor(b["ROUT"][:, 1024:1024 + c.D],
                                        v["MO1"][:, 0:c.D],
                                        b["GTS"][:, 14 * i + 13:14 * i + 14],
                                        b["ROUT"][:, 0:c.D],
                                        op0=ALU.mult, op1=ALU.add)
                ve.drain()
                return ve.tensor_tensor(b["ROUT"][:, 0:c.D],
                                        b["ROUT"][:, 1024:1024 + c.D],
                                        v["MID"][:, 0:c.D], op=ALU.add)
            em.op("vector", fmo, rd=("EIG", "MO1", "MID", "GTS"), wr=("ROUT",))

            def fv(ve):
                sm = b["SM"]
                for j in range(c.D // 512):
                    ve.bn_stats(sm[:, 6 * j:6 * j + 6],
                                b["ROUT"][:, 512 * j:512 * (j + 1)])
                ve.drain()
                ve.bn_aggr(sm[:, 48:50], sm[:, 0:6 * (c.D // 512)])
                return ve.drain()
            em.op("vector", fv, rd=("ROUT",), wr=("SM",))

            def fs(s):
                sm = b["SM"]
                s.activation(sm[:, 50:51], sm[:, 49:50], AF.Ln, bias=b["EPSC"][:, 0:1])
                s.drain()
                return s.activation(sm[:, 51:52], sm[:, 50:51], AF.Exp,
                                    scale=-0.5)
            em.op("scalar", fs, rd=("SM",), wr=("SM",))

            def fo(ve, i=i):
                sm = b["SM"]
                ve.tensor_scalar(b["ROUT"][:, 1024:1024 + c.D],
                                 b["ROUT"][:, 0:c.D], sm[:, 48:49],
                                 sm[:, 51:52], op0=ALU.subtract, op1=ALU.mult)
                ve.drain()
                return ve.tensor_tensor(b["ROUT"][:, 1024:1024 + c.D],
                                        b["ROUT"][:, 1024:1024 + c.D],
                                        b["R"][:, i * c.D:(i + 1) * c.D],
                                        op=ALU.add)
            em.op("vector", fo, rd=("ROUT", "SM", "R"), wr=("ROUT",))

            def fdo(sy, i=i):
                return sy.dma_start(
                    out=d["xout"].rearrange("(a p) dd -> a p dd", p=128)[i, :, :],
                    in_=b["ROUT"][:, 1024:1024 + c.D])
            em.op("sync", fdo, rd=("ROUT",), wr=("xout",), dma=True)

        pairs = self._fence_pairs()

        def fin(sy):
            for s, n in pairs:
                sy.wait_ge(s.h, n)
        self.blk.sync(fin)


def _lhsT_tiles(w, colsel):
    Din = w.shape[0]
    dch = Din // 128
    out = np.empty((128, dch * 128), np.float32)
    sub = w[:, colsel]
    for cc in range(dch):
        out[:, cc * 128:(cc + 1) * 128] = sub[cc * 128:(cc + 1) * 128, :]
    return out


def make_inputs(c, ins, r):
    D, TB = c.D, c.TB
    m = {}
    x = np.asarray(ins["L0_attn_x"], np.float32).reshape(c.N, D)
    m["xin"] = np.ascontiguousarray(x[r * c.TOK:(r + 1) * c.TOK])
    Wq = np.asarray(ins["L0_attn_W_qkv"], np.float32)
    bq = np.asarray(ins["L0_attn_b_qkv"], np.float32)
    Aq = np.asarray(ins["L0_attn_A_qkv"], np.float32)
    Bq = np.asarray(ins["L0_attn_B_qkv"], np.float32)
    Wpo = np.asarray(ins["L0_attn_W_proj"], np.float32)
    bpo = np.asarray(ins["L0_attn_b_proj"], np.float32)
    Ap = np.asarray(ins["L0_attn_A_proj"], np.float32)
    Bp = np.asarray(ins["L0_attn_B_proj"], np.float32)
    Wqkv_i = np.asarray(ins["L0_ffn_Wqkv"], np.float32)
    Wp_i = np.asarray(ins["L0_ffn_Wproj"], np.float32)

    for p in range(c.PAIRS):
        h1 = c.UN * (r % c.GS) + 2 * p
        h2 = h1 + 1
        for ki, knd in enumerate("qkv"):
            base = ki * D
            csel = np.r_[base + h1 * 64:base + h1 * 64 + 64,
                         base + h2 * 64:base + h2 * 64 + 64]
            m[f"o_w{knd}{p}"] = _lhsT_tiles(Wq, csel)
            m[f"o_b{knd}{p}"] = np.ascontiguousarray(bq[csel])
            m[f"o_B{knd}{p}"] = np.ascontiguousarray(Bq[:, csel])
            csel_i = np.r_[h1 * 192 + ki * 64:h1 * 192 + ki * 64 + 64,
                           h2 * 192 + ki * 64:h2 * 192 + ki * 64 + 64]
            m[f"i_w{knd}{p}"] = _lhsT_tiles(Wqkv_i, csel_i)
    Ao = np.empty((128, c.DCH * 8), np.float32)
    App = np.empty((128, c.DCH * 8), np.float32)
    for cc in range(c.DCH):
        Ao[:, cc * 8:(cc + 1) * 8] = Aq[cc * 128:(cc + 1) * 128, :]
        App[:, cc * 8:(cc + 1) * 8] = Ap[cc * 128:(cc + 1) * 128, :]
    m["o_A"] = Ao
    m["o_Ap"] = App
    m["o_Bp"] = np.ascontiguousarray(Bp)
    for lay, W in (("o", Wpo), ("i", Wp_i)):
        out = np.empty((128, c.DCH * D), np.float32)
        for cc in range(c.DCH):
            for co in range(c.DCH):
                out[:, (cc * c.DCH + co) * 128:(cc * c.DCH + co + 1) * 128] = \
                    W[cc * 128:(cc + 1) * 128, co * 128:(co + 1) * 128]
        m[f"{lay}_wp"] = out
    m["o_bp"] = bpo
    m["i_g1"] = np.asarray(ins["L0_ffn_ln1_g"], np.float32)
    m["i_b1"] = np.asarray(ins["L0_ffn_ln1_b"], np.float32)
    m["i_g2"] = np.asarray(ins["L0_ffn_ln2_g"], np.float32)
    m["i_b2"] = np.asarray(ins["L0_ffn_ln2_b"], np.float32)
    Wg = np.asarray(ins["L0_ffn_Wgate"], np.float32)
    wg = np.empty((128, c.DCH * 8), np.float32)
    for cc in range(c.DCH):
        wg[:, cc * 8:(cc + 1) * 8] = Wg[cc * 128:(cc + 1) * 128, :]
    m["wgate"] = wg
    W1 = np.asarray(ins["L0_ffn_W1"], np.float32)[r]
    W2 = np.asarray(ins["L0_ffn_W2"], np.float32)[r]
    w1r = np.empty((c.FT, 128, c.DCH * 128), np.float32)
    for ftl in range(c.FT):
        for cc in range(c.DCH):
            w1r[ftl, :, cc * 128:(cc + 1) * 128] = \
                W1[cc * 128:(cc + 1) * 128, ftl * 128:(ftl + 1) * 128]
    m["w1e"] = w1r
    w2r = np.empty((c.DCH, 128, c.FT * 128), np.float32)
    for dtl in range(c.DCH):
        for fc in range(c.FT):
            w2r[dtl, :, fc * 128:(fc + 1) * 128] = \
                W2[fc * 128:(fc + 1) * 128, dtl * 128:(dtl + 1) * 128]
    m["w2e"] = w2r
    m["b1e"] = np.asarray(ins["L0_ffn_b1"], np.float32)[r]
    m["b2e"] = np.asarray(ins["L0_ffn_b2"], np.float32)[r]
    m["ident_f"] = np.eye(128, dtype=np.float32)
    inv = 1.0 / 10000.0 ** (np.arange(0, HD, 2, dtype=np.float64) / HD)
    ang = np.arange(TB, dtype=np.float64)[:, None] * inv[None, :]
    cos = np.cos(ang).T.astype(np.float32)
    sin = np.sin(ang).T.astype(np.float32)
    m["cc_tab"] = np.ascontiguousarray(np.concatenate([cos] * 4, 0))
    m["ss_tab"] = np.ascontiguousarray(np.concatenate([-sin, sin, -sin, sin], 0))
    cm = np.zeros((128, 4 * QCH), np.float32)
    for j in range(4):
        kpos = j * 128 + np.arange(128)[:, None]
        qpos = np.arange(QCH)[None, :]
        cm[:, j * QCH:(j + 1) * QCH] = (kpos <= qpos).astype(np.float32)
    m["cmask"] = cm
    tv = np.empty((128, c.TT), np.float32)
    for i in range(c.TT):
        tv[:, i] = r * c.TOK + i * 128 + np.arange(128)
    m["tvals"] = tv
    m["iot"] = np.arange(128, dtype=np.float32).reshape(128, 1)
    m["io8"] = np.tile(np.arange(8, dtype=np.float32), (128, 1))
    pfx = np.zeros(8, np.float32)
    pfx[:r] = 1.0
    m["pfx"] = pfx
    m["ebd"] = np.full((128, 1), (c.CAP * r) // 128, np.float32)
    m["mb"] = np.full((128, 1), (r % c.GS) * c.GS * c.SR, np.float32)
    return m


_PROG_CACHE = {}


def kernel(**inputs):
    c = FULL
    if "full" not in _PROG_CACHE:
        _PROG_CACHE["full"] = K(c).build()
    nc = _PROG_CACHE["full"]
    in_maps = [make_inputs(c, inputs, r) for r in range(NCORE)]
    res = run_bass_kernel_spmd(nc, in_maps, core_ids=list(range(NCORE)))
    out = np.concatenate([np.asarray(res.results[r]["xout"], np.float32)
                          for r in range(NCORE)], 0)
    return out.reshape(c.B, c.TB, c.D)


# version 02
# speedup vs baseline: 1.0329x; 1.0329x over previous
import math
from contextlib import ExitStack

import numpy as np
import ml_dtypes

import concourse.bass as bass
import concourse.mybir as mybir
from concourse.bass import IndirectOffsetOnAxis
from concourse.bass_utils import run_bass_kernel_spmd

F32 = mybir.dt.float32
BF16 = mybir.dt.bfloat16
U32 = mybir.dt.uint32
AF = mybir.ActivationFunctionType
ALU = mybir.AluOpType
AXX = mybir.AxisListType.X

NCORE = 8
PTD = 4
HD = 64
QCH = 512
EPS = 1e-5


class Cfg:
    def __init__(self, D=1024, TB=8192, H=16, E=8, FFM=4):
        self.B = 2
        self.D, self.TB, self.H, self.E = D, TB, H, E
        self.FF = FFM * D
        assert D == H * HD and E == NCORE
        self.N = self.B * TB
        self.TOK = self.N // NCORE
        self.GS = NCORE // self.B
        self.UN = self.B * H // NCORE
        self.PAIRS = self.UN // 2
        self.DCH = D // 128
        self.TT = self.TOK // 128
        self.NQC = TB // QCH
        self.NKT = TB // 128
        self.CAP = int(math.ceil(2.0 * self.N / E))
        self.FT = self.FF // 128
        self.NSW = self.CAP // 512
        self.CAPT = self.CAP // 128
        self.SR = self.PAIRS * 128
        assert self.TOK % QCH == 0 and self.CAP % 512 == 0


FULL = Cfg()
MINI = Cfg(D=512, TB=2048, H=8)


class Sem:
    def __init__(self, h):
        self.h = h
        self.n = 0

    def inc(self, k=1):
        self.n += k
        return self.n


_EPIR = {"EPIs", "EPIr", "EPIrr", "EPIb1", "EPIb2", "EPIo", "ROUT"}
ALIAS = {
    "QT": {"PIN", "EIT"}, "KT": {"PIN", "EIT"}, "VV": {"WP", "EIT"},
    "XS0": {"SQ", "EIT"}, "XS1": {"SQ", "EIT"},
    "PIN": {"QT", "KT", "EIT"}, "WP": {"VV", "EIT"},
    "SQ": {"XS0", "XS1", "EIT"},
    "EIT": {"QT", "KT", "VV", "XS0", "XS1", "PIN", "WP", "SQ"},
    "H2": {"MID", "LHA"}, "MID": {"H2", "LHA"}, "LHA": {"H2", "MID"},
    "WQKV": {"W1B"}, "W1B": {"WQKV"},
    "LB": {"EIG", "MO1"}, "EIG": {"LB"}, "MO1": {"LB"},
    "GB2": {"CUM"}, "CUM": {"GB2"},
    "SM": {"SMln0", "SMln1", "SMcb"}, "SMln0": {"SM"}, "SMln1": {"SM"},
    "SMcb": {"SM"},
    "GBB": {"PT", "LHAP", "W2B"},
    "LHAP": {"PT", "GBB", "W2B"}, "PT": {"LHAP", "GBB", "W2B", "CSL0", "CSL1", "MKT"},
    "MKT": {"CSL0", "CSL1", "W2B", "PT"}, "CSL0": {"MKT", "W2B", "PT"},
    "CSL1": {"MKT", "W2B", "PT"},
    "W2B": {"PT", "CSL0", "CSL1", "MKT", "LHAP", "GBB"},
}
for _r in _EPIR:
    ALIAS[_r] = (ALIAS.get(_r, set()) | _EPIR) - {_r}


class EM:
    ESEM = {"tensor": "eT", "scalar": "eS", "vector": "eV", "gpsimd": "eG",
            "sync": "eY"}

    def __init__(self, blk, sems, pools):
        self.blk = blk
        self.s = sems
        self.pools = pools
        self.pidx = {q: 0 for q in pools}
        self.w = {}
        self.r = {}

    def _pick(self, q):
        pool = self.pools[q]
        i = self.pidx[q]
        self.pidx[q] = (i + 1) % len(pool)
        return pool[i]

    def _expand(self, regs):
        out = set(regs)
        for reg in regs:
            out |= ALIAS.get(reg, set())
        return out

    def _waits(self, rd, wr):
        acc = {}

        def add(pair):
            if pair is None:
                return
            s, n = pair
            if acc.get(s, 0) < n:
                acc[s] = n
        for reg in rd:
            add(self.w.get(reg))
        for reg in self._expand(wr):
            add(self.w.get(reg))
            for s, n in self.r.get(reg, {}).items():
                add((s, n))
        return acc

    def op(self, eng, fn, rd=(), wr=(), dma=False, cc=False):
        if dma:
            sem = self._pick({"sync": "qsy", "scalar": "qsc",
                              "gpsimd": "qg"}[eng])
            step = 16
        elif cc:
            sem = self._pick("qcc")
            step = 1
        else:
            sem = self.s[self.ESEM[eng]]
            step = 1
        waits = dict(self._waits(rd, wr))
        if (dma or cc) and sem.n > 0:
            if waits.get(sem, 0) < sem.n:
                waits[sem] = sem.n
        cnt = [0]

        class _DmaProxy:
            def __init__(self, e):
                self._e = e

            def dma_start(self, *a, **k):
                inst = self._e.dma_start(*a, **k)
                inst.then_inc(sem.h, step)
                cnt[0] += step
                return inst

            def indirect_dma_start(self, *a, **k):
                inst = self._e.indirect_dma_start(*a, **k)
                inst.then_inc(sem.h, step)
                cnt[0] += step
                return inst

            def collective_compute(self, *a, **k):
                inst = self._e.collective_compute(*a, **k)
                inst.then_inc(sem.h, step)
                cnt[0] += step
                return inst

            def __getattr__(self, name):
                return getattr(self._e, name)

        def body(e):
            for s, n in waits.items():
                e.wait_ge(s.h, n)
            if dma or cc:
                fn(_DmaProxy(e))
            else:
                inst = fn(e)
                assert inst is not None, "op fn must return its last instruction"
                inst.then_inc(sem.h, 1)
                cnt[0] = 1
        getattr(self.blk, eng)(body)
        n = sem.inc(cnt[0])
        for reg in self._expand(wr):
            self.w[reg] = (sem, n)
            self.r[reg] = {}
        for reg in rd:
            self.r.setdefault(reg, {})[sem] = n
        return sem, n

    def note_write(self, reg, sem, n):
        for rg in self._expand((reg,)):
            self.w[rg] = (sem, n)
            self.r[rg] = {}

    def note_read(self, reg, sem, n):
        self.r.setdefault(reg, {})[sem] = n

    def need(self, *regs):
        acc = {}
        for reg in regs:
            p = self.w.get(reg)
            if p and acc.get(p[0], 0) < p[1]:
                acc[p[0]] = p[1]
        return list(acc.items())

    def need_w(self, *regs):
        acc = {}
        for reg in self._expand(regs):
            pairs = []
            if self.w.get(reg):
                pairs.append(self.w[reg])
            pairs += list(self.r.get(reg, {}).items())
            for s, n in pairs:
                if acc.get(s, 0) < n:
                    acc[s] = n
        return list(acc.items())


class K:
    def __init__(self, cfg=FULL, dbg=()):
        self.c = cfg
        self.dbg = set(dbg)
        self.nc = bass.Bass("TRN2", target_bir_lowering=False)
        self.d = {}
        self.b = {}
        self.s = {}
        self.dbg_keys = []

    def declare(self):
        nc, c, d = self.nc, self.c, self.d

        def inp(name, shape, dt=F32):
            d[name] = nc.dram_tensor(name, shape, dt, kind="ExternalInput").ap()

        inp("xin", [c.TOK, c.D])
        for p in range(c.PAIRS):
            for lay in ("o", "i"):
                for knd in "qkv":
                    inp(f"{lay}_w{knd}{p}", [128, c.DCH * 128])
            for knd in "qkv":
                inp(f"o_b{knd}{p}", [128])
                inp(f"o_B{knd}{p}", [8, 128])
        inp("o_A", [128, c.DCH * 8])
        for nm in ("i_g1", "i_b1", "i_g2", "i_b2", "o_bp", "b2e"):
            inp(nm, [c.D])
        inp("b1e", [c.FF])
        for lay in ("o", "i"):
            inp(f"{lay}_wp", [128, c.DCH * c.D])
        inp("o_Ap", [128, c.DCH * 8])
        inp("o_Bp", [8, c.D])
        inp("wgate", [128, c.DCH * 8])
        inp("w1e", [c.FT, 128, c.DCH * 128])
        inp("w2e", [c.DCH, 128, c.FT * 128])
        inp("ident_f", [128, 128])
        inp("cc_tab", [128, c.TB])
        inp("ss_tab", [128, c.TB])
        inp("cmask", [128, 4 * QCH])
        inp("tvals", [128, c.TT])
        inp("iot", [128, 1])
        inp("io8", [128, 8])
        inp("pfx", [8])
        inp("ebd", [128, 1])
        inp("mb", [128, 1])

        d["xout"] = nc.dram_tensor("xout", [c.TOK, c.D], F32,
                                   kind="ExternalOutput").ap()

        def internal(name, shape, dt):
            d[name] = nc.dram_tensor(name, shape, dt, kind="Internal").ap()

        internal("ag_in", [c.D, c.TOK], BF16)
        internal("ag_out", [c.GS * c.D, c.TOK], BF16)
        for m in range(c.GS):
            internal(f"st{m}", [c.SR, c.TOK], BF16)
        internal("ao_all", [c.GS * c.GS * c.SR, c.TOK], BF16)
        internal("flat_st", [c.TOK, c.D], BF16)
        internal("flat_all", [c.N, c.D], BF16)
        internal("cnt_in", [1, 8], F32)
        internal("cnt_all", [8, 8], F32)
        internal("tmap_in", [c.E * c.CAP, 1], F32)
        internal("tmap_all", [c.E * c.CAP, 1], F32)
        internal("eo_st", [c.CAP, c.D], BF16)
        internal("eo_all", [c.E * c.CAP, c.D], BF16)
        internal("sp_x1", [c.TOK, c.D], F32)
        internal("sp_y", [c.TOK, c.D], BF16)

    def alloc(self, st):
        nc, c, b = self.nc, self.c, self.b

        def sb(name, shape, dt):
            b[name] = st.enter_context(nc.sbuf_tensor(name, shape, dt))

        sb("R", [128, c.TT * c.D], F32)
        sb("GB2", [128, 2 * c.D], F32)
        sb("SM", [128, 256], F32)
        sb("ROUT", [128, 3072], F32)
        sb("AR1", [128, 32768], BF16)
        sb("AR2", [128, 8192], BF16)
        sb("AR3", [128, 20480], BF16)
        sb("ONES", [128, 1], BF16)
        sb("ONESF", [128, 128], F32)
        sb("IDB", [128, 128], BF16)
        sb("IDF", [128, 128], F32)
        sb("WG", [128, c.DCH * 8], F32)
        sb("EIDX", [128, max(c.CAPT, 24)], U32)
        sb("CIX", [128, 2 * c.TT], U32)
        sb("GTS", [128, 14 * c.TT], F32)
        sb("TVS", [128, c.TT], F32)
        sb("CNT", [8, 8], F32)
        sb("PFXS", [8, 1], F32)
        b["PS6"] = st.enter_context(nc.psum_tensor("PS6", [128, 3072], F32))
        b["PVB"] = st.enter_context(nc.psum_tensor("PVB", [128, 512], F32))
        b["RSB"] = st.enter_context(nc.psum_tensor("RSB", [128, 512], F32))

        A1, A2, A3 = b["AR1"], b["AR2"], b["AR3"]
        v = self.v = {}
        v["QT"] = A1[:, 0:8192]
        v["KT"] = A1[:, 8192:16384]
        v["VV"] = A1[:, 16384:24576]
        v["XS"] = A1[:, 24576:32768]
        v["PIN"] = A1[:, 0:16384]
        v["WP"] = A1[:, 16384:24576]
        v["SQ"] = A1[:, 24576:32768]
        v["EIT"] = A1[:, 0:c.DCH * c.CAP]
        v["PT"] = A2[:, 0:4096]
        v["CS"] = A2[:, 2048:4096]
        v["CM"] = A2[:, 4096:6144]
        v["GBB"] = A2[:, 0:2048]
        v["TPS"] = A2[:, 6144:8192]
        v["H2"] = A3[:, 0:16384]
        v["WQKV"] = A3[:, 16384:18432]
        v["LB"] = A3[:, 18432:20480]
        v["MID"] = A3[:, 0:16384]
        v["W1B"] = A3[:, 16384:18432]
        v["W2B"] = A2[:, 0:c.FT * 128]
        v["EIG"] = A3[:, 18432:19456]
        v["MO1"] = A3[:, 19456:20480]
        b["EPI"] = b["ROUT"][:, 0:3072]
        b["CUM"] = b["GB2"][:, 0:c.TOK]
        b["MKT"] = A2[:, 2048:2048 + c.TOK]
        b["LHA"] = A3[:, 0:c.TB]
        b["LHAP"] = A2[:, 0:c.TOK]
        b["ONEBR"] = A2[:, 2048:2176]
        b["IO8"] = b["SM"][:, 144:152]
        b["B2S"] = b["SM"][:, 152:160]
        b["BPS"] = b["SM"][:, 160:168]
        b["BQB"] = b["SM"][:, 168:174]
        b["B1S"] = b["SM"][:, 176:176 + c.FT]
        b["IOTS"] = b["SM"][:, 224:225]
        b["EBD"] = b["SM"][:, 225:226]
        b["MBS"] = b["SM"][:, 226:227]
        b["EPSC"] = b["SM"][:, 227:228]

    def build(self):
        self.declare()
        with ExitStack() as st:
            self.alloc(st)
            for nm in ("smm", "sex", "spv", "smsk",
                       "eT", "eS", "eV", "eG", "eY"):
                self.s[nm] = Sem(st.enter_context(self.nc.semaphore(nm)))
            pools = {}
            for q, npool in (("qsy", 24), ("qsc", 12), ("qg", 24), ("qcc", 8)):
                pools[q] = [Sem(st.enter_context(self.nc.semaphore(f"{q}{i}")))
                            for i in range(npool)]
            with self.nc.Block() as blk:
                self.blk = blk
                self.em = EM(blk, self.s, pools)
                self.emit()
        return self.nc

    def _fence_pairs(self):
        pairs = [(s, s.n) for s in self.s.values() if s.n]
        for pool in self.em.pools.values():
            pairs += [(s, s.n) for s in pool if s.n]
        return pairs

    def dbg_sbuf(self, name, view, regs=()):
        if name not in self.dbg:
            return
        h = self.nc.dram_tensor("dbg_" + name, [128, view.shape[-1]],
                                view.tensor.dtype, kind="ExternalOutput").ap()
        self.dbg_keys.append("dbg_" + name)
        pairs = self._fence_pairs()

        def f(sy):
            for s, n in pairs:
                sy.wait_ge(s.h, n)
            return sy.dma_start(out=h, in_=view)
        self.em.op("sync", f, rd=tuple(regs), dma=True)

    def dbg_dram(self, name, key):
        if name not in self.dbg:
            return
        src = self.d[key]
        h = self.nc.dram_tensor("dbg_" + name, list(src.tensor.shape),
                                src.tensor.dtype, kind="ExternalOutput").ap()
        self.dbg_keys.append("dbg_" + name)
        pairs = self._fence_pairs()

        def f(sy):
            for s, n in pairs:
                sy.wait_ge(s.h, n)
            return sy.dma_start(out=h, in_=src)
        self.em.op("sync", f, rd=(key,), dma=True)

    def emit(self):
        c = self.c
        self.p0_load()
        self.ln_stage_xln()
        self.allgather_group("ag_in", "ag_out")
        self.dbg_dram("xlnT", "ag_out")
        for p in range(c.PAIRS):
            self.qkv_pair("o", p)
            self.dbg_sbuf(f"o_qt{p}", self.v["QT"][:, 0:c.TB], regs=("QT",))
            self.dbg_sbuf(f"o_kt{p}", self.v["KT"][:, 0:c.TB], regs=("KT",))
            self.dbg_sbuf(f"o_vv{p}", self.v["VV"][:, 0:c.TB], regs=("VV",))
            self.bounds_pair("o", p)
            self.attention_pair("o", p, causal=False)
        self.chunk_ags()
        self.proj_and_residual("o")
        self.dbg_sbuf("x1", self.b["R"][:, 0:c.TT * c.D], regs=("R",))
        self.h2_u_stage()
        self.allgather_group("ag_in", "ag_out")
        self.dbg_dram("uT", "ag_out")
        for p in range(c.PAIRS):
            self.qkv_pair("i", p)
            self.dbg_sbuf(f"i_qt{p}", self.v["QT"][:, 0:c.TB], regs=("QT",))
            self.bounds_pair("i", p)
            self.attention_pair("i", p, causal=True)
        self.chunk_ags()
        self.proj_and_residual("i")
        self.dbg_sbuf("ybuf", self.v["H2"][:, 0:c.TT * c.D], regs=("H2",))
        self.spill_and_flat()
        self.allgather_global("flat_st", "flat_all")
        self.routing()
        self.expert_ffn()
        self.combine_final()

    def p0_load(self):
        c, b, v, d, em = self.c, self.b, self.v, self.d, self.em

        def fm(g):
            g.memset(b["SM"][:, :], 0.0)
            g.memset(b["ROUT"][:, :], 0.0)
            g.drain()
            g.memset(b["EPSC"][:, 0:1], float(EPS))
            g.memset(b["ONES"][:, :], 1.0)
            g.memset(b["ONEBR"][0:1, :], 1.0)
            return g.memset(b["ONESF"][:, :], 1.0)
        em.op("gpsimd", fm, wr=("consts2", "SM", "CSL0", "ROUT"))

        def f1(sy):
            xr = d["xin"].rearrange("(i p) dd -> i p dd", p=128)
            for i in range(c.TT):
                sy.dma_start(out=b["R"][:, i * c.D:(i + 1) * c.D], in_=xr[i, :, :])
            sy.dma_start(out=b["IDF"][:, :], in_=d["ident_f"])
            sy.dma_start(out=b["PFXS"][:, :],
                         in_=d["pfx"].rearrange("(a o) -> a o", o=1))
            sy.dma_start(out=b["EBD"][:, 0:1], in_=d["ebd"])
            sy.dma_start(out=b["MBS"][:, 0:1], in_=d["mb"])
            sy.dma_start(out=b["IOTS"][:, 0:1], in_=d["iot"])
            sy.dma_start(out=b["SM"][:, 144:152], in_=d["io8"])
            sy.dma_start(out=b["TVS"][:, :], in_=d["tvals"])
            sy.dma_start(out=b["B1S"][:, 0:c.FT],
                         in_=d["b1e"].rearrange("(f p) -> p f", p=128), allow_slow_non_contiguous=True)
            sy.dma_start(out=b["B2S"][:, 0:c.DCH],
                         in_=d["b2e"].rearrange("(t p) -> p t", p=128), allow_slow_non_contiguous=True)
            sy.dma_start(out=b["BPS"][:, 0:c.DCH],
                         in_=d["o_bp"].rearrange("(t p) -> p t", p=128), allow_slow_non_contiguous=True)
            sy.dma_start(out=b["WG"][:, :], in_=d["wgate"])
            for p in range(c.PAIRS):
                for ki, knd in enumerate("qkv"):
                    sy.dma_start(out=b["BQB"][:, 3 * p + ki:3 * p + ki + 1],
                                 in_=d[f"o_b{knd}{p}"].rearrange("(p o) -> p o", o=1), allow_slow_non_contiguous=True)
            sy.dma_start(out=b["ROUT"][0:1, 0:c.D],
                         in_=d["i_g2"].rearrange("(a x) -> a x", a=1))
            return sy.dma_start(out=b["ROUT"][0:1, c.D:2 * c.D],
                                in_=d["i_b2"].rearrange("(a x) -> a x", a=1))
        em.op("sync", f1, wr=("R", "consts", "ROUT", "SM"), dma=True)

        def fg(g):
            g.dma_start(out=b["IDB"][:, :], in_=d["ident_f"])
            return g.dma_start(out=v["CM"], in_=d["cmask"])
        em.op("gpsimd", fg, wr=("CM", "consts"), dma=True)

        for j in range(2 * c.DCH):
            half = j // c.DCH
            col = (j % c.DCH) * 128

            def fb(t, half=half, col=col):
                src = b["ROUT"][0:1, half * c.D + col:half * c.D + col + 128]
                t.matmul(b["PS6"][:, 0:128], b["ONESF"][0:1, :], src,
                         start=True, stop=True)
                return t.drain()
            em.op("tensor", fb, rd=("ROUT", "consts2"), wr=("PS0",))

            def fe(ve, j=j):
                return ve.tensor_copy(b["GB2"][:, j * 128:(j + 1) * 128],
                                      b["PS6"][:, 0:128])
            em.op("vector", fe, rd=("PS0",), wr=("GB2", "PS0"))

    def ln_stats(self, src_view, rd_regs, slot=0):
        c, b, em = self.c, self.b, self.em
        base = 24 * slot
        mb_ = 48 + 8 * slot
        reg = f"SMln{slot}"

        def fv(ve):
            sm = b["SM"]
            for j in range(c.D // 512):
                ve.bn_stats(sm[:, base + 6 * j:base + 6 * j + 6],
                            src_view[:, 512 * j:512 * (j + 1)])
            ve.drain()
            ve.bn_aggr(sm[:, mb_:mb_ + 2], sm[:, base:base + 6 * (c.D // 512)])
            return ve.drain()
        em.op("vector", fv, rd=rd_regs, wr=(reg,))

        def fs(s):
            sm = b["SM"]
            s.activation(sm[:, mb_ + 2:mb_ + 3], sm[:, mb_ + 1:mb_ + 2], AF.Ln,
                         bias=b["EPSC"][:, 0:1])
            s.drain()
            s.activation(sm[:, mb_ + 3:mb_ + 4], sm[:, mb_ + 2:mb_ + 3], AF.Exp,
                         scale=-0.5)
            return s.drain()
        em.op("scalar", fs, rd=(reg,), wr=(reg,))
        return mb_, reg

    def ln_stage_xln(self):
        c, b, v, em = self.c, self.b, self.v, self.em
        for i in range(c.TT):
            mb_, reg = self.ln_stats(b["R"][:, i * c.D:(i + 1) * c.D], ("R",),
                                     slot=i % 2)

            def fa(ve, i=i, mb_=mb_):
                sm = b["SM"]
                return ve.tensor_scalar(v["TPS"][:, 0:c.D],
                                        b["R"][:, i * c.D:(i + 1) * c.D],
                                        sm[:, mb_:mb_ + 1], sm[:, mb_ + 3:mb_ + 4],
                                        op0=ALU.subtract, op1=ALU.mult)
            em.op("vector", fa, rd=("R", reg), wr=("TPS",))
            self.transpose_stage_tile(i, "ag_in")

    def transpose_stage_tile(self, i, dram):
        c, b, v, em = self.c, self.b, self.v, self.em
        for g0 in range(0, c.DCH, 4):
            def ft(t, g0=g0):
                pb = b["PVB"].ap().bitcast(BF16)
                for k in range(4):
                    cch = g0 + k
                    t.transpose(pb[:, 128 * k:128 * (k + 1)],
                                v["TPS"][:, cch * 128:(cch + 1) * 128],
                                b["IDB"][:, :])
                return t.drain()
            em.op("tensor", ft, rd=("TPS", "consts"), wr=("PVB",))

            def fe(ve):
                pb = b["PVB"].ap().bitcast(BF16)
                return ve.tensor_copy(v["TPS"][:, 1024:1536], pb[:, 0:512])
            em.op("vector", fe, rd=("PVB",), wr=("TPS2", "PVB"))

            def fd(sy, i=i, g0=g0):
                last = None
                for k in range(4):
                    cch = g0 + k
                    last = sy.dma_start(
                        out=self.d[dram][cch * 128:(cch + 1) * 128,
                                         i * 128:(i + 1) * 128],
                        in_=v["TPS"][:, 1024 + 128 * k:1024 + 128 * (k + 1)])
                return last
            em.op("sync", fd, rd=("TPS2",), wr=(dram,), dma=True)

    def allgather_group(self, src, dst):
        c, em = self.c, self.em

        def fg(g):
            rg = [list(range(g0, g0 + c.GS)) for g0 in range(0, NCORE, c.GS)]
            g.collective_compute("AllGather", ALU.bypass, replica_groups=rg,
                                 ins=[self.d[src]], outs=[self.d[dst]])
        em.op("gpsimd", fg, rd=(src,), wr=(dst,), cc=True)

    def allgather_global(self, src, dst, op="AllGather", alu=ALU.bypass):
        em = self.em

        def fg(g):
            g.collective_compute(op, alu, replica_groups=[list(range(NCORE))],
                                 ins=[self.d[src]], outs=[self.d[dst]])
        em.op("gpsimd", fg, rd=(src,), wr=(dst,), cc=True)

    def chunk_ags(self):
        c, em = self.c, self.em
        for m in range(c.GS):
            def fg(g, m=m):
                rg = [list(range(g0, g0 + c.GS))
                      for g0 in range(0, NCORE, c.GS)]
                g.collective_compute(
                    "AllGather", ALU.bypass, replica_groups=rg,
                    ins=[self.d[f"st{m}"]],
                    outs=[self.d["ao_all"][m * c.GS * c.SR:(m + 1) * c.GS * c.SR, :]])
            em.op("gpsimd", fg, rd=(f"st{m}",), wr=(f"aoall{m}",), cc=True)

    def qkv_pair(self, lay, p):
        c, b, v, d, em = self.c, self.b, self.v, self.d, self.em
        do_lora = lay == "o"
        do_hA = do_lora and p == 0

        if do_lora:
            def fwl(g, p=p):
                g.dma_start(out=v["LB"][0:8, 0:128], in_=d[f"o_Bq{p}"])
                g.dma_start(out=v["LB"][0:8, 128:256], in_=d[f"o_Bk{p}"])
                g.dma_start(out=v["LB"][0:8, 256:384], in_=d[f"o_Bv{p}"])
                return g.dma_start(out=v["LB"][:, 1024:1024 + c.DCH * 8],
                                   in_=d["o_A"])
            em.op("gpsimd", fwl, wr=("LB",), dma=True)

        if do_hA:
            for tc in range(c.NQC):
                par = tc % 2
                self._qkv_load_chunk(tc, par)

                def fha(t, tc=tc, par=par):
                    for cch in range(c.DCH):
                        t.matmul(b["PS6"][0:8, 2560:3072],
                                 v["LB"][:, 1024 + cch * 8:1024 + (cch + 1) * 8],
                                 v["XS"][:, 4096 * par + cch * 512:4096 * par + (cch + 1) * 512],
                                 start=(cch == 0), stop=(cch == c.DCH - 1))
                    return t.drain()
                em.op("tensor", fha, rd=(f"XS{par}", "LB"), wr=("PS5",))

                def fhe(s, tc=tc):
                    return s.activation(b["LHA"][0:8, tc * QCH:(tc + 1) * QCH],
                                        b["PS6"][0:8, 2560:3072], AF.Copy,
                                        scale=2.0)
                em.op("scalar", fhe, rd=("PS5",), wr=("LHA", "PS5"))

        for ki, knd in enumerate("qkv"):
            def fw(g, ki=ki, knd=knd, lay=lay, p=p):
                return g.dma_start(out=v["WQKV"][:, 0:c.DCH * 128],
                                   in_=d[f"{lay}_w{knd}{p}"])
            em.op("gpsimd", fw, wr=("WQKV",), dma=True)
            dstv = (v["QT"], v["KT"], v["VV"])[ki]
            dreg = ("QT", "KT", "VV")[ki]
            for tc in range(c.NQC):
                par = tc % 2
                self._qkv_load_chunk(tc, par)
                if lay == "i" and ki == 0:
                    bt = (tc * QCH) % c.TB

                    def ftab(g, par=par, bt=bt):
                        g.dma_start(out=v["CS"][:, par * 1024:par * 1024 + 512],
                                    in_=d["cc_tab"][:, bt:bt + QCH])
                        return g.dma_start(
                            out=v["CS"][:, par * 1024 + 512:par * 1024 + 1024],
                            in_=d["ss_tab"][:, bt:bt + QCH])
                    em.op("gpsimd", ftab, wr=(f"CSL{par}",), dma=True)
                bank = tc % 2

                def fmm(t, tc=tc, par=par, ki=ki, bank=bank):
                    for cch in range(c.DCH):
                        t.matmul(b["PS6"][:, bank * 512:bank * 512 + 512],
                                 v["WQKV"][:, cch * 128:(cch + 1) * 128],
                                 v["XS"][:, 4096 * par + cch * 512:4096 * par + (cch + 1) * 512],
                                 start=(cch == 0),
                                 stop=(not do_lora and cch == c.DCH - 1))
                    if do_lora:
                        t.matmul(b["PS6"][:, bank * 512:bank * 512 + 512],
                                 v["LB"][0:8, ki * 128:(ki + 1) * 128],
                                 b["LHA"][0:8, tc * QCH:(tc + 1) * QCH],
                                 start=False, stop=True)
                    return t.drain()
                rdl = (f"XS{par}", "WQKV", "LB", "LHA") if do_lora else \
                      (f"XS{par}", "WQKV")
                em.op("tensor", fmm, rd=rdl, wr=(f"PS{bank}",))

                def fev(s, tc=tc, ki=ki, bank=bank, p=p, lay=lay, dstv=dstv):
                    if lay == "o":
                        return s.activation(
                            dstv[:, tc * QCH:(tc + 1) * QCH],
                            b["PS6"][:, bank * 512:bank * 512 + 512],
                            AF.Identity,
                            bias=b["BQB"][:, 3 * p + ki:3 * p + ki + 1])
                    return s.copy(dstv[:, tc * QCH:(tc + 1) * QCH],
                                  b["PS6"][:, bank * 512:bank * 512 + 512])
                em.op("scalar", fev, rd=(f"PS{bank}", "consts"),
                      wr=(dreg, f"PS{bank}"))

                if lay == "i" and ki < 2:
                    def frope(ve, tc=tc, par=par, dstv=dstv, dreg=dreg):
                        cs = v["CS"][:, par * 1024:par * 1024 + 512]
                        ss = v["CS"][:, par * 1024 + 512:par * 1024 + 1024]
                        q = dstv[:, tc * QCH:(tc + 1) * QCH]
                        sw = v["PT"][:, 0:512]
                        for hh in (0, 64):
                            ve.tensor_copy(sw[hh:hh + 32, :], q[hh + 32:hh + 64, :])
                            ve.tensor_copy(sw[hh + 32:hh + 64, :], q[hh:hh + 32, :])
                        ve.drain()
                        ve.tensor_tensor(q, q, cs, op=ALU.mult)
                        ve.tensor_tensor(sw[:, :], sw[:, :], ss, op=ALU.mult)
                        ve.drain()
                        ve.tensor_tensor(q, q, sw[:, :], op=ALU.add)
                        return ve.drain()
                    em.op("vector", frope, rd=(dreg, f"CSL{par}"),
                          wr=(dreg, "PT"))

        NB = c.TB // 128
        for j0 in range(0, NB, 4):
            def ftv(t, j0=j0):
                pb = b["RSB"].ap().bitcast(BF16)
                for k in range(4):
                    j = j0 + k
                    t.transpose(pb[:, 128 * k:128 * (k + 1)],
                                v["VV"][:, j * 128:(j + 1) * 128], b["IDB"][:, :])
                return t.drain()
            em.op("tensor", ftv, rd=("VV", "consts"), wr=("RSB",))

            def fev2(ve, j0=j0):
                pb = b["RSB"].ap().bitcast(BF16)
                last = None
                for k in range(4):
                    j = j0 + k
                    last = ve.tensor_copy(v["VV"][:, j * 128:(j + 1) * 128],
                                          pb[:, 128 * k:128 * (k + 1)])
                return last
            em.op("vector", fev2, rd=("RSB",), wr=("VV", "RSB"))

    def _qkv_load_chunk(self, tc, par):
        c, b, v, d, em = self.c, self.b, self.v, self.d, self.em
        m = (tc * QCH) // c.TOK
        lt = (tc * QCH) % c.TOK

        def fld(s, par=par, m=m, lt=lt):
            last = None
            for cch in range(c.DCH):
                last = s.dma_start(
                    out=v["XS"][:, 4096 * par + cch * 512:4096 * par + (cch + 1) * 512],
                    in_=d["ag_out"][m * c.D + cch * 128:m * c.D + (cch + 1) * 128,
                                    lt:lt + QCH])
            return last
        em.op("scalar", fld, rd=("ag_out",), wr=(f"XS{par}",), dma=True)

    def bounds_pair(self, lay, p):
        c, b, v, em = self.c, self.b, self.v, self.em
        SM = b["SM"]
        for t_i, (srcv, sreg) in enumerate(((v["QT"], "QT"), (v["KT"], "KT"))):
            def fsq(ve, srcv=srcv):
                return ve.tensor_tensor(v["SQ"][:, 0:c.TB], srcv[:, 0:c.TB],
                                        srcv[:, 0:c.TB], op=ALU.mult)
            em.op("vector", fsq, rd=(sreg,), wr=("SQ",))
            for ch in range(c.TB // 512):
                def fm(t, ch=ch):
                    t.matmul(b["RSB"][0:1, 0:512], b["ONES"][0:64, 0:1],
                             v["SQ"][0:64, ch * 512:(ch + 1) * 512],
                             start=True, stop=True, tile_position=(0, 0))
                    t.matmul(b["PVB"][0:1, 0:512], b["ONES"][64:128, 0:1],
                             v["SQ"][64:128, ch * 512:(ch + 1) * 512],
                             start=True, stop=True, tile_position=(64, 0))
                    return t.drain()
                em.op("tensor", fm, rd=("SQ", "consts2"), wr=("RSB", "PVB"))

                def facc(ve, ch=ch, t_i=t_i):
                    col = 96 + 2 * t_i
                    last = None
                    for q, ps in ((0, b["RSB"]), (1, b["PVB"])):
                        if ch == 0:
                            last = ve.tensor_scalar(
                                b["EPI"][0:1, 0:512], ps[0:1, 0:512], 1.0, None,
                                op0=ALU.mult, op1=ALU.max,
                                accum_out=SM[0:1, col + q:col + q + 1])
                        else:
                            last = ve.tensor_scalar(
                                b["EPI"][0:1, 0:512], ps[0:1, 0:512], 1.0,
                                SM[0:1, 110 + 2 * t_i + q:111 + 2 * t_i + q],
                                op0=ALU.mult, op1=ALU.max,
                                accum_out=SM[0:1, col + q:col + q + 1])
                        ve.drain()
                    return last
                em.op("vector", facc, rd=("RSB", "PVB", "SM"),
                      wr=("SM", "EPIs", "RSB", "PVB"))

                if ch < c.TB // 512 - 1:
                    def fcp(ve, t_i=t_i):
                        col = 96 + 2 * t_i
                        ve.tensor_copy(SM[0:1, 110 + 2 * t_i:111 + 2 * t_i],
                                       SM[0:1, col:col + 1])
                        ve.drain()
                        return ve.tensor_copy(
                            SM[0:1, 111 + 2 * t_i:112 + 2 * t_i],
                            SM[0:1, col + 1:col + 2])
                    em.op("vector", fcp, rd=("SM",), wr=("SM",))

        def fc(ve):
            ve.tensor_tensor(SM[0:1, 104:105], SM[0:1, 96:97], SM[0:1, 97:98],
                             op=ALU.max)
            ve.drain()
            ve.tensor_tensor(SM[0:1, 105:106], SM[0:1, 98:99], SM[0:1, 99:100],
                             op=ALU.max)
            ve.drain()
            return ve.tensor_tensor(SM[0:1, 106:107], SM[0:1, 104:105],
                                    SM[0:1, 105:106], op=ALU.mult)
        em.op("vector", fc, rd=("SM",), wr=("SM",))

        def fsr(s):
            return s.activation(SM[0:1, 107:108], SM[0:1, 106:107], AF.Sqrt)
        em.op("scalar", fsr, rd=("SM",), wr=("SM",))

        def fneg(ve):
            return ve.tensor_scalar(SM[0:1, 108:109], SM[0:1, 107:108],
                                    -0.125, None, op0=ALU.mult)
        em.op("vector", fneg, rd=("SM",), wr=("SM",))

        def fbc(t):
            t.matmul(b["RSB"][:, 0:1], b["ONESF"][0:1, :], SM[0:1, 108:109],
                     start=True, stop=True)
            return t.drain()
        em.op("tensor", fbc, rd=("SM", "consts2"), wr=("RSB",))

        def fcb(ve):
            return ve.tensor_copy(SM[:, 80:81], b["RSB"][:, 0:1])
        em.op("vector", fcb, rd=("RSB",), wr=("SMcb", "RSB"))

    def attention_pair(self, lay, p, causal):
        c, b, v, em = self.c, self.b, self.v, self.em
        smm, sex, spv, smsk = (self.s[x] for x in ("smm", "sex", "spv", "smsk"))
        eV = self.s["eV"]
        cb = b["SM"][:, 80:81]
        QT, KTv, VV, PT = v["QT"], v["KT"], v["VV"], v["PT"]
        PS6, PVB, RSB = b["PS6"], b["PVB"], b["RSB"]
        w_pe = em.need("QT", "KT", "VV", "SMcb") + \
            em.need_w("PVB", "RSB", "PS0", "PS1", "PS2", "PS3", "PS4", "PS5")
        w_act = em.need("SMcb") + em.need_w("PT")
        w_dve = em.need("CM")
        a0 = sex.n
        first = True
        for qc in range(c.NQC):
            kmax = c.NKT if not causal else 4 * (qc + 1)
            diag0 = kmax - 4 if causal else kmax + 1
            smm0, spv0, sex0, smsk0 = smm.n, spv.n, sex.n, smsk.n
            nep = eV.n

            def fpe(t, qc=qc, kmax=kmax, diag0=diag0, a0=a0, spv0=spv0,
                    sex0=sex0, smsk0=smsk0, nep=nep, first=first, causal=causal):
                if first:
                    for s, n in w_pe:
                        t.wait_ge(s.h, n)
                for kt in range(kmax + 1):
                    if kt < kmax:
                        a = a0 + kt
                        j3 = a % 3
                        if kt >= 3:
                            t.wait_ge(sex.h, sex0 + kt - 2)
                        elif sex0 > 0:
                            t.wait_ge(sex.h, sex0)
                        t.matmul(PS6[:, j3 * 512:(j3 + 1) * 512],
                                 KTv[0:64, kt * 128:(kt + 1) * 128],
                                 QT[0:64, qc * 512:(qc + 1) * 512],
                                 start=True, stop=True, tile_position=(0, 0))
                        t.matmul(PS6[:, (3 + j3) * 512:(4 + j3) * 512],
                                 KTv[64:128, kt * 128:(kt + 1) * 128],
                                 QT[64:128, qc * 512:(qc + 1) * 512],
                                 start=True, stop=True, tile_position=(64, 0)
                                 ).then_inc(smm.h, 1)
                    if kt >= 1:
                        kp = kt - 1
                        ap = a0 + kp
                        t.wait_ge(sex.h, sex0 + kp + 1)
                        if kp == 0:
                            t.wait_ge(eV.h, nep)
                        if causal and kp >= diag0:
                            t.wait_ge(smsk.h, smsk0 + (kp - diag0) + 1)
                        par = ap % PTD
                        stt = kp == 0
                        t.matmul(PVB[0:64, :], VV[:, kp * 128:kp * 128 + 64],
                                 PT[:, par * 1024:par * 1024 + 512],
                                 start=stt, stop=False, tile_position=(0, 0),
                                 skip_group_check=True)
                        t.matmul(PVB[64:128, :],
                                 VV[:, kp * 128 + 64:kp * 128 + 128],
                                 PT[:, par * 1024 + 512:par * 1024 + 1024],
                                 start=stt, stop=False, tile_position=(0, 64),
                                 skip_group_check=True)
                        t.matmul(RSB[0:1, :], b["ONES"][:, 0:1],
                                 PT[:, par * 1024:par * 1024 + 512],
                                 start=stt, stop=False, tile_position=(0, 0),
                                 skip_group_check=True)
                        t.matmul(RSB[64:65, :], b["ONES"][:, 0:1],
                                 PT[:, par * 1024 + 512:par * 1024 + 1024],
                                 start=stt, stop=False, tile_position=(0, 64),
                                 skip_group_check=True).then_inc(spv.h, 1)
            self.blk.tensor(fpe)
            smm.inc(kmax)
            spv.inc(kmax)

            def fex(s, kmax=kmax, a0=a0, smm0=smm0, spv0=spv0, first=first):
                if first:
                    for ss, n in w_act:
                        s.wait_ge(ss.h, n)
                for kt in range(kmax):
                    a = a0 + kt
                    j3, par = a % 3, a % PTD
                    s.wait_ge(smm.h, smm0 + kt + 1)
                    if kt >= 4:
                        s.wait_ge(spv.h, spv0 + kt - 3)
                    elif spv0 > 0:
                        s.wait_ge(spv.h, spv0)
                    src = PS6.ap()[:, j3 * 512:j3 * 512 + 2048].rearrange(
                        "q (a bb) -> q a bb", bb=512)[:, 0:4:3, :]
                    dst = PT[:, par * 1024:par * 1024 + 1024].rearrange(
                        "q (a bb) -> q a bb", bb=512)
                    s.activation(dst, src, AF.Exp, bias=cb, scale=0.125
                                 ).then_inc(sex.h, 1)
            self.blk.scalar(fex)
            sex.inc(kmax)

            if causal:
                def fmk(ve, kmax=kmax, diag0=diag0, a0=a0, sex0=sex0,
                        first=first):
                    if first:
                        for ss, n in w_dve:
                            ve.wait_ge(ss.h, n)
                    for j in range(4):
                        kt = diag0 + j
                        par = (a0 + kt) % PTD
                        ve.wait_ge(sex.h, sex0 + kt + 1)
                        ve.tensor_tensor(PT[:, par * 1024:par * 1024 + 512],
                                         PT[:, par * 1024:par * 1024 + 512],
                                         v["CM"][:, j * 512:(j + 1) * 512],
                                         op=ALU.mult)
                        ve.drain()
                        ve.tensor_tensor(
                            PT[:, par * 1024 + 512:par * 1024 + 1024],
                            PT[:, par * 1024 + 512:par * 1024 + 1024],
                            v["CM"][:, j * 512:(j + 1) * 512],
                            op=ALU.mult)
                        ve.drain().then_inc(smsk.h, 1)
                self.blk.vector(fmk)
                smsk.inc(4)

            npv = spv.n

            def ep1(ve, npv=npv):
                ve.wait_ge(spv.h, npv)
                ve.memset(b["EPI"][:, 0:512], 1.0)
                ve.drain()
                ve.tensor_copy(b["EPI"][0:1, 0:512], RSB[0:1, :])
                ve.tensor_copy(b["EPI"][64:65, 0:512], RSB[64:65, :])
                ve.drain()
                return ve.tensor_copy(b["EPI"][:, 2560:3072], PVB[:, :])
            em.op("vector", ep1, rd=(), wr=("EPIs", "EPIo", "RSB", "PVB"))

            def ep2(t):
                for k in range(4):
                    t.transpose(RSB[:, 128 * k:128 * (k + 1)],
                                b["EPI"][:, 128 * k:128 * (k + 1)],
                                b["IDF"][:, :])
                return t.drain()
            em.op("tensor", ep2, rd=("EPIs", "consts"), wr=("RSB",))

            def ep3(ve):
                return ve.reciprocal(b["EPI"][:, 512:1024], RSB[:, :])
            em.op("vector", ep3, rd=("RSB",), wr=("EPIr", "RSB"))

            def ep4(t):
                for k in range(4):
                    t.transpose(RSB[:, 128 * k:128 * (k + 1)],
                                b["EPI"][:, 512 + 128 * k:512 + 128 * (k + 1)],
                                b["IDF"][:, :])
                return t.drain()
            em.op("tensor", ep4, rd=("EPIr", "consts"), wr=("RSB",))

            def ep5(ve):
                return ve.tensor_copy(b["EPI"][:, 1024:1536], RSB[:, :])
            em.op("vector", ep5, rd=("RSB",), wr=("EPIrr", "RSB"))

            def ep6(t):
                t.matmul(RSB[:, :], b["ONESF"][0:1, :], b["EPI"][0:1, 1024:1536],
                         start=True, stop=True)
                return t.drain()
            em.op("tensor", ep6, rd=("EPIrr", "consts2"), wr=("RSB",))

            def ep7(ve):
                return ve.tensor_copy(b["EPI"][:, 1536:2048], RSB[:, :])
            em.op("vector", ep7, rd=("RSB",), wr=("EPIb1", "RSB"))

            def ep8(t):
                t.matmul(RSB[:, :], b["ONESF"][64:65, :],
                         b["EPI"][64:65, 1024:1536], start=True, stop=True)
                return t.drain()
            em.op("tensor", ep8, rd=("EPIrr", "consts2"), wr=("RSB",))

            def ep9(ve):
                ve.tensor_copy(b["EPI"][:, 2048:2560], RSB[:, :])
                ve.drain()
                ve.tensor_tensor(v["TPS"][0:64, 0:512],
                                 b["EPI"][0:64, 2560:3072],
                                 b["EPI"][0:64, 1536:2048], op=ALU.mult)
                ve.drain()
                return ve.tensor_tensor(v["TPS"][64:128, 0:512],
                                        b["EPI"][64:128, 2560:3072],
                                        b["EPI"][64:128, 2048:2560],
                                        op=ALU.mult)
            em.op("vector", ep9, rd=("EPIb1", "EPIb2", "EPIo"),
                  wr=("TPS", "RSB", "EPIb2"))

            m = (qc * QCH) // c.TOK
            lt = (qc * QCH) % c.TOK

            def fst(sy, m=m, lt=lt, p=p):
                return sy.dma_start(
                    out=self.d[f"st{m}"][p * 128:(p + 1) * 128, lt:lt + QCH],
                    in_=v["TPS"][:, 0:512])
            em.op("sync", fst, rd=("TPS",), wr=(f"st{m}",), dma=True)
            a0 += kmax
            first = False
        em.note_write("PT", sex, sex.n)
        for bank in range(6):
            em.note_write(f"PS{bank}", smm, smm.n)
            em.note_read(f"PS{bank}", sex, sex.n)
        em.note_read("QT", smm, smm.n)
        em.note_read("KT", smm, smm.n)
        em.note_read("VV", spv, spv.n)
        em.note_read("SMcb", sex, sex.n)

    def proj_and_residual(self, lay):
        c, b, v, d, em = self.c, self.b, self.v, self.d, self.em
        do_lora = lay == "o"

        def fidx(ve):
            for j in range(c.DCH):
                ve.tensor_scalar(b["SM"][:, 208 + j:209 + j], b["IOTS"][:, :],
                                 1.0, float(128 * j), op0=ALU.mult, op1=ALU.add)
            ve.drain()
            for j in range(c.DCH):
                ve.tensor_tensor(b["SM"][:, 208 + j:209 + j],
                                 b["SM"][:, 208 + j:209 + j], b["MBS"][:, :],
                                 op=ALU.add)
            ve.drain()
            return ve.tensor_copy(b["EIDX"][:, 16:16 + c.DCH],
                                  b["SM"][:, 208:208 + c.DCH])
        em.op("vector", fidx, rd=("consts",), wr=("SM", "EIDX"))

        def fgat(g):
            last = None
            for j in range(c.DCH):
                last = g.indirect_dma_start(
                    out=v["PIN"][:, j * c.TOK:(j + 1) * c.TOK], out_offset=None,
                    in_=d["ao_all"],
                    in_offset=IndirectOffsetOnAxis(ap=b["EIDX"][:, 16 + j:17 + j],
                                                   axis=0))
            return last
        em.op("gpsimd", fgat, rd=("ao_all", "aoall0", "aoall1", "aoall2", "aoall3", "EIDX"), wr=("PIN",), dma=True)

        def fwp(g, lay=lay):
            last = g.dma_start(out=v["WP"][:, 0:c.DCH * c.D], in_=d[f"{lay}_wp"])
            if lay == "o":
                g.dma_start(out=v["LB"][:, 1024:1024 + c.DCH * 8], in_=d["o_Ap"])
                last = g.dma_start(out=v["LB"][0:8, 0:c.D], in_=d["o_Bp"])
            return last
        em.op("gpsimd", fwp, wr=("WP", "LB"), dma=True)

        if do_lora:
            for ch in range(c.TOK // 512):
                def fha(t, ch=ch):
                    for cch in range(c.DCH):
                        t.matmul(b["PS6"][0:8, 2560:3072],
                                 v["LB"][:, 1024 + cch * 8:1024 + (cch + 1) * 8],
                                 v["PIN"][:, cch * c.TOK + ch * 512:cch * c.TOK + (ch + 1) * 512],
                                 start=(cch == 0), stop=(cch == c.DCH - 1))
                    return t.drain()
                em.op("tensor", fha, rd=("PIN", "LB"), wr=("PS5",))

                def fhe(s, ch=ch):
                    return s.activation(
                        b["LHAP"][0:8, ch * 512:(ch + 1) * 512],
                        b["PS6"][0:8, 2560:3072], AF.Copy, scale=2.0)
                em.op("scalar", fhe, rd=("PS5",), wr=("LHAP", "PS5"))

        for co in range(c.DCH):
            for ch in range(c.TOK // 512):
                bank = (co * (c.TOK // 512) + ch) % 2

                def fpj(t, co=co, ch=ch, bank=bank):
                    for cch in range(c.DCH):
                        t.matmul(
                            b["PS6"][:, bank * 512:bank * 512 + 512],
                            v["WP"][:, (cch * c.DCH + co) * 128:(cch * c.DCH + co + 1) * 128],
                            v["PIN"][:, cch * c.TOK + ch * 512:cch * c.TOK + (ch + 1) * 512],
                            start=(cch == 0),
                            stop=(not do_lora and cch == c.DCH - 1))
                    if do_lora:
                        t.matmul(b["PS6"][:, bank * 512:bank * 512 + 512],
                                 v["LB"][0:8, co * 128:(co + 1) * 128],
                                 b["LHAP"][0:8, ch * 512:(ch + 1) * 512],
                                 start=False, stop=True)
                    return t.drain()
                rdl = ("PIN", "WP", "LB", "LHAP") if do_lora else ("PIN", "WP")
                em.op("tensor", fpj, rd=rdl, wr=(f"PS{bank}",))

                def fze(s, co=co, bank=bank, lay=lay):
                    if lay == "o":
                        return s.activation(
                            v["TPS"][:, 0:512],
                            b["PS6"][:, bank * 512:bank * 512 + 512],
                            AF.Identity, bias=b["BPS"][:, co:co + 1])
                    return s.copy(v["TPS"][:, 0:512],
                                  b["PS6"][:, bank * 512:bank * 512 + 512])
                em.op("scalar", fze, rd=(f"PS{bank}", "consts"),
                      wr=("TPS", f"PS{bank}"))

                def fzt(t):
                    pb = b["RSB"].ap().bitcast(BF16)
                    for k in range(4):
                        t.transpose(pb[:, 128 * k:128 * (k + 1)],
                                    v["TPS"][:, 128 * k:128 * (k + 1)],
                                    b["IDB"][:, :])
                    return t.drain()
                em.op("tensor", fzt, rd=("TPS", "consts"), wr=("RSB",))

                def fzv(ve, co=co, ch=ch, lay=lay):
                    pb = b["RSB"].ap().bitcast(BF16)
                    last = None
                    for k in range(4):
                        ti = ch * 4 + k
                        dst = v["H2"][:, ti * c.D + co * 128:ti * c.D + (co + 1) * 128]
                        if lay == "o":
                            last = ve.tensor_copy(dst, pb[:, 128 * k:128 * (k + 1)])
                        else:
                            last = ve.tensor_tensor(
                                dst, dst, pb[:, 128 * k:128 * (k + 1)],
                                op=ALU.add)
                        ve.drain()
                    return last
                em.op("vector", fzv, rd=("RSB", "H2"), wr=("H2", "RSB"))

        if lay == "o":
            for i in range(c.TT):
                mb_, reg = self.ln_stats(v["H2"][:, i * c.D:(i + 1) * c.D],
                                         ("H2",), slot=i % 2)

                def fadd(ve, i=i, mb_=mb_):
                    sm = b["SM"]
                    ve.tensor_scalar(b["ROUT"][:, 0:c.D],
                                     v["H2"][:, i * c.D:(i + 1) * c.D],
                                     sm[:, mb_:mb_ + 1], sm[:, mb_ + 3:mb_ + 4],
                                     op0=ALU.subtract, op1=ALU.mult)
                    ve.drain()
                    return ve.tensor_tensor(b["R"][:, i * c.D:(i + 1) * c.D],
                                            b["R"][:, i * c.D:(i + 1) * c.D],
                                            b["ROUT"][:, 0:c.D], op=ALU.add)
                em.op("vector", fadd, rd=("H2", reg), wr=("R", "ROUT"))

    def h2_u_stage(self):
        c, b, v, em = self.c, self.b, self.v, self.em

        def fgb(g):
            g.dma_start(out=v["LB"][0:1, 0:c.D],
                        in_=self.d["i_g1"].rearrange("(a x) -> a x", a=1))
            return g.dma_start(out=v["LB"][0:1, c.D:2 * c.D],
                               in_=self.d["i_b1"].rearrange("(a x) -> a x", a=1))
        em.op("gpsimd", fgb, wr=("LB",), dma=True)

        def fob(g):
            return g.memset(b["ONEBR"][0:1, :], 1.0)
        em.op("gpsimd", fob, wr=("CSL0",))
        for j in range(2 * c.DCH):
            half = j // c.DCH
            col = (j % c.DCH) * 128

            def fb2(t, half=half, col=col):
                t.matmul(b["PS6"][:, 0:128], b["ONEBR"][0:1, :],
                         v["LB"][0:1, half * c.D + col:half * c.D + col + 128],
                         start=True, stop=True)
                return t.drain()
            em.op("tensor", fb2, rd=("LB", "CSL0"), wr=("PS0",))

            def fe2(ve, j=j):
                return ve.tensor_copy(v["GBB"][:, j * 128:(j + 1) * 128],
                                      b["PS6"][:, 0:128])
            em.op("vector", fe2, rd=("PS0",), wr=("GBB", "PS0"))
        for i in range(c.TT):
            mb_, reg = self.ln_stats(b["R"][:, i * c.D:(i + 1) * c.D], ("R",),
                                     slot=i % 2)

            def fa(ve, i=i, mb_=mb_, reg=reg):
                sm = b["SM"]
                ve.tensor_scalar(v["H2"][:, i * c.D:(i + 1) * c.D],
                                 b["R"][:, i * c.D:(i + 1) * c.D],
                                 sm[:, mb_:mb_ + 1], sm[:, mb_ + 3:mb_ + 4],
                                 op0=ALU.subtract, op1=ALU.mult)
                ve.drain()
                ve.tensor_tensor(v["TPS"][:, 0:c.D],
                                 v["H2"][:, i * c.D:(i + 1) * c.D],
                                 v["GBB"][:, 0:c.D], op=ALU.mult)
                ve.drain()
                return ve.tensor_tensor(v["TPS"][:, 0:c.D], v["TPS"][:, 0:c.D],
                                        v["GBB"][:, c.D:2 * c.D],
                                        op=ALU.add)
            em.op("vector", fa, rd=("R", reg, "GBB"), wr=("H2", "TPS"))
            self.transpose_stage_tile(i, "ag_in")

    def spill_and_flat(self):
        c, b, v, d, em = self.c, self.b, self.v, self.d, self.em

        def fsp(sy):
            xr = d["sp_x1"].rearrange("(i p) dd -> i p dd", p=128)
            yr = d["sp_y"].rearrange("(i p) dd -> i p dd", p=128)
            last = None
            for i in range(c.TT):
                sy.dma_start(out=xr[i, :, :], in_=b["R"][:, i * c.D:(i + 1) * c.D])
                last = sy.dma_start(out=yr[i, :, :],
                                    in_=v["H2"][:, i * c.D:(i + 1) * c.D])
            return last
        em.op("sync", fsp, rd=("R", "H2"), wr=("sp_x1", "sp_y"), dma=True)

        for i in range(c.TT):
            mb_, reg = self.ln_stats(v["H2"][:, i * c.D:(i + 1) * c.D], ("H2",),
                                     slot=i % 2)

            def fa(ve, i=i, mb_=mb_):
                sm = b["SM"]
                ve.tensor_scalar(b["R"][:, i * c.D:(i + 1) * c.D],
                                 v["H2"][:, i * c.D:(i + 1) * c.D],
                                 sm[:, mb_:mb_ + 1], sm[:, mb_ + 3:mb_ + 4],
                                 op0=ALU.subtract, op1=ALU.mult)
                ve.drain()
                ve.tensor_tensor(b["R"][:, i * c.D:(i + 1) * c.D],
                                 b["R"][:, i * c.D:(i + 1) * c.D],
                                 b["GB2"][:, 0:c.D], op=ALU.mult)
                ve.drain()
                ve.tensor_tensor(b["R"][:, i * c.D:(i + 1) * c.D],
                                 b["R"][:, i * c.D:(i + 1) * c.D],
                                 b["GB2"][:, c.D:2 * c.D], op=ALU.add)
                ve.drain()
                return ve.tensor_copy(v["TPS"][:, 0:c.D],
                                      b["R"][:, i * c.D:(i + 1) * c.D])
            em.op("vector", fa, rd=("H2", reg, "GB2", "sp_x1"),
                  wr=("R", "TPS"))

            def fdm(sy, i=i):
                return sy.dma_start(
                    out=d["flat_st"].rearrange("(a p) dd -> a p dd", p=128)[i, :, :],
                    in_=v["TPS"][:, 0:c.D])
            em.op("sync", fdm, rd=("TPS",), wr=("flat_st",), dma=True)

    def routing(self):
        c, b, v, d, em = self.c, self.b, self.v, self.d, self.em

        def fz(ve):
            return ve.memset(b["ROUT"][:, 0:c.E * c.CAP // 128], 0.0)
        em.op("vector", fz, wr=("ROUT",))

        def fzd(sy):
            return sy.dma_start(
                out=d["tmap_in"].rearrange("(p f) o -> p (f o)", p=128),
                in_=b["ROUT"][:, 0:c.E * c.CAP // 128])
        em.op("sync", fzd, rd=("ROUT",), wr=("tmap_in",), dma=True)

        for i in range(c.TT):
            for g0 in range(0, c.DCH, 4):
                def ftt(t, i=i, g0=g0):
                    for k in range(4):
                        cch = g0 + k
                        t.transpose(
                            b["RSB"][:, 128 * k:128 * (k + 1)],
                            b["R"][:, i * c.D + cch * 128:i * c.D + (cch + 1) * 128],
                            b["IDF"][:, :])
                    return t.drain()
                em.op("tensor", ftt, rd=("R", "consts"), wr=("RSB",))

                def fte(ve, g0=g0):
                    return ve.tensor_copy(b["ROUT"][:, g0 * 128:(g0 + 4) * 128],
                                          b["RSB"][:, :])
                em.op("vector", fte, rd=("RSB",), wr=("ROUT", "RSB"))

            def flg(t):
                for cch in range(c.DCH):
                    t.matmul(b["PS6"][0:8, 2560:2688],
                             b["WG"][:, cch * 8:(cch + 1) * 8],
                             b["ROUT"][:, cch * 128:(cch + 1) * 128],
                             start=(cch == 0), stop=(cch == c.DCH - 1))
                return t.drain()
            em.op("tensor", flg, rd=("ROUT", "consts"), wr=("PS5",))

            def fle(ve):
                return ve.tensor_copy(b["ROUT"][0:8, 2048:2176],
                                      b["PS6"][0:8, 2560:2688])
            em.op("vector", fle, rd=("PS5",), wr=("ROUT", "PS5"))

            def flt(t):
                t.transpose(b["RSB"][:, 0:8], b["ROUT"][0:8, 2048:2176],
                            b["IDF"][0:8, 0:8])
                return t.drain()
            em.op("tensor", flt, rd=("ROUT", "consts"), wr=("RSB",))

            def fsm(ve):
                L = b["ROUT"][:, 2304:2312]
                ve.tensor_copy(L, b["RSB"][:, 0:8])
                ve.drain()
                ve.tensor_reduce(b["ROUT"][:, 2312:2313], L, axis=AXX, op=ALU.max)
                ve.drain()
                return ve.tensor_scalar(b["ROUT"][:, 2313:2314],
                                        b["ROUT"][:, 2312:2313], -1.0, None,
                                        op0=ALU.mult)
            em.op("vector", fsm, rd=("RSB",), wr=("ROUT", "RSB"))

            def fex(s):
                return s.activation(b["ROUT"][:, 2320:2328],
                                    b["ROUT"][:, 2304:2312], AF.Exp,
                                    bias=b["ROUT"][:, 2313:2314])
            em.op("scalar", fex, rd=("ROUT",), wr=("ROUT",))

            def ftop(ve, i=i):
                E8 = b["ROUT"][:, 2320:2328]
                ve.tensor_reduce(b["ROUT"][:, 2328:2329], E8, axis=AXX,
                                 op=ALU.add)
                ve.drain()
                ve.reciprocal(b["ROUT"][:, 2329:2330], b["ROUT"][:, 2328:2329])
                ve.drain()
                G = b["GTS"][:, 14 * i:14 * i + 8]
                ve.tensor_scalar(G, E8, b["ROUT"][:, 2329:2330], None,
                                 op0=ALU.mult)
                ve.drain()
                ve.max(b["ROUT"][:, 2330:2338], G)
                ve.drain()
                ve.max_index(b["EIDX"][:, 8:16], b["ROUT"][:, 2330:2338], G)
                ve.drain()
                ve.tensor_copy(b["ROUT"][:, 2338:2346], b["EIDX"][:, 8:16])
                ve.drain()
                ve.tensor_copy(b["GTS"][:, 14 * i + 8:14 * i + 10],
                               b["ROUT"][:, 2338:2340])
                ve.tensor_copy(b["GTS"][:, 14 * i + 10:14 * i + 12],
                               b["ROUT"][:, 2330:2332])
                ve.drain()
                ve.tensor_scalar(b["ROUT"][:, 2346:2354], b["IO8"][:, :],
                                 b["GTS"][:, 14 * i + 8:14 * i + 9], None,
                                 op0=ALU.is_equal)
                ve.drain()
                ve.tensor_scalar(b["ROUT"][:, 2354:2362], b["IO8"][:, :],
                                 b["GTS"][:, 14 * i + 9:14 * i + 10], None,
                                 op0=ALU.is_equal)
                ve.drain()
                ve.tensor_tensor(b["ROUT"][:, 2362:2370],
                                 b["ROUT"][:, 2346:2354],
                                 b["ROUT"][:, 2354:2362], op=ALU.add)
                ve.drain()
                return ve.tensor_copy(v["TPS"][:, 0:8], b["ROUT"][:, 2362:2370])
            em.op("vector", ftop, rd=("ROUT", "consts"),
                  wr=("ROUT", "GTS", "TPS", "EIDX"))

            def fmt(t):
                pb = b["RSB"].ap().bitcast(BF16)
                t.transpose(pb[0:8, 0:128], v["TPS"][:, 0:8], b["IDB"][:, :])
                return t.drain()
            em.op("tensor", fmt, rd=("TPS", "consts"), wr=("RSB",))

            def fmte(ve, i=i):
                pb = b["RSB"].ap().bitcast(BF16)
                return ve.tensor_copy(b["MKT"][0:8, i * 128:(i + 1) * 128],
                                      pb[0:8, 0:128])
            em.op("vector", fmte, rd=("RSB",), wr=("MKT", "RSB"))

        def fscan(ve):
            ve.tensor_tensor_scan(b["CUM"][0:8, :], b["MKT"][0:8, :],
                                  b["MKT"][0:8, :], 0.0, op0=ALU.add,
                                  op1=ALU.bypass)
            ve.drain()
            return ve.tensor_copy(b["SM"][0:8, 120:121],
                                  b["CUM"][0:8, c.TOK - 1:c.TOK])
        em.op("vector", fscan, rd=("MKT",), wr=("CUM", "SM"))

        def fcdm(sy):
            return sy.dma_start(out=d["cnt_in"].rearrange("o e -> e o"),
                                in_=b["SM"][0:8, 120:121],
                                allow_slow_non_contiguous=True)
        em.op("sync", fcdm, rd=("SM",), wr=("cnt_in",), dma=True)
        self.allgather_global("cnt_in", "cnt_all")

        def fcld(sy):
            return sy.dma_start(out=b["CNT"][:, :], in_=d["cnt_all"])
        em.op("sync", fcld, rd=("cnt_all",), wr=("CNT",), dma=True)

        def fbase(t):
            t.matmul(b["RSB"][0:1, 0:8], b["PFXS"][0:8, 0:1], b["CNT"][0:8, :],
                     start=True, stop=True)
            return t.drain()
        em.op("tensor", fbase, rd=("CNT",), wr=("RSB",))

        def fbse(ve):
            return ve.tensor_copy(b["SM"][0:1, 128:136], b["RSB"][0:1, 0:8])
        em.op("vector", fbse, rd=("RSB",), wr=("SM", "RSB"))

        def fbbc(t):
            t.matmul(b["RSB"][:, 0:8], b["ONESF"][0:1, :],
                     b["SM"][0:1, 128:136], start=True, stop=True)
            return t.drain()
        em.op("tensor", fbbc, rd=("SM", "consts2"), wr=("RSB",))

        def fbm1(ve):
            return ve.tensor_scalar(b["SM"][:, 136:144], b["RSB"][:, 0:8],
                                    -1.0, None, op0=ALU.add)
        em.op("vector", fbm1, rd=("RSB",), wr=("SM", "RSB"))

        for i in range(c.TT):
            def fcb(t, i=i):
                t.transpose(b["RSB"][:, 0:8],
                            b["CUM"][0:8, i * 128:(i + 1) * 128],
                            b["IDF"][0:8, 0:8])
                return t.drain()
            em.op("tensor", fcb, rd=("CUM", "consts"), wr=("RSB",))

            def fpos(ve, i=i):
                P = b["ROUT"][:, 2304:2312]
                ve.tensor_tensor(P, b["RSB"][:, 0:8], b["SM"][:, 136:144],
                                 op=ALU.add)
                ve.drain()
                last = None
                for rr in range(2):
                    oh = b["ROUT"][:, 2346 + 8 * rr:2354 + 8 * rr]
                    ve.tensor_scalar(oh, b["IO8"][:, :],
                                     b["GTS"][:, 14 * i + 8 + rr:14 * i + 9 + rr],
                                     None, op0=ALU.is_equal)
                    ve.drain()
                    ve.tensor_tensor(b["ROUT"][:, 2376:2384], P, oh, op=ALU.mult)
                    ve.drain()
                    ve.tensor_reduce(b["ROUT"][:, 2384 + rr:2385 + rr],
                                     b["ROUT"][:, 2376:2384], axis=AXX,
                                     op=ALU.add)
                    ve.drain()
                for rr in range(2):
                    pr = b["ROUT"][:, 2384 + rr:2385 + rr]
                    kp = b["ROUT"][:, 2386 + rr:2387 + rr]
                    ve.tensor_scalar(kp, pr, float(c.CAP - 1), None,
                                     op0=ALU.is_le)
                    ve.drain()
                    ve.tensor_tensor(
                        b["GTS"][:, 14 * i + 12 + rr:14 * i + 13 + rr],
                        b["GTS"][:, 14 * i + 10 + rr:14 * i + 11 + rr],
                        kp, op=ALU.mult)
                    ve.drain()
                    dv = b["ROUT"][:, 2388 + rr:2389 + rr]
                    ve.tensor_scalar(dv,
                                     b["GTS"][:, 14 * i + 8 + rr:14 * i + 9 + rr],
                                     float(c.CAP), None, op0=ALU.mult)
                    ve.drain()
                    ve.tensor_tensor(dv, dv, pr, op=ALU.add)
                    ve.drain()
                    ve.tensor_scalar(b["ROUT"][:, 2390:2391], kp, -3.0e7, 3.0e7,
                                     op0=ALU.mult, op1=ALU.add)
                    ve.drain()
                    ve.tensor_tensor(dv, dv, b["ROUT"][:, 2390:2391],
                                     op=ALU.add)
                    ve.drain()
                    ve.tensor_copy(b["EIDX"][:, 20 + rr:21 + rr], dv)
                    ve.drain()
                    mn = b["ROUT"][:, 2391:2392]
                    ve.tensor_scalar(mn, pr, float(c.CAP - 1), None,
                                     op0=ALU.min)
                    ve.drain()
                    ve.tensor_scalar(b["ROUT"][:, 2392:2393],
                                     b["GTS"][:, 14 * i + 8 + rr:14 * i + 9 + rr],
                                     float(c.CAP), None, op0=ALU.mult)
                    ve.drain()
                    ve.tensor_tensor(mn, mn, b["ROUT"][:, 2392:2393],
                                     op=ALU.add)
                    ve.drain()
                    last = ve.tensor_copy(b["CIX"][:, 2 * i + rr:2 * i + rr + 1],
                                          mn)
                    ve.drain()
                return last
            em.op("vector", fpos, rd=("RSB", "ROUT", "GTS", "consts"),
                  wr=("ROUT", "GTS", "EIDX", "CIX", "RSB"))

            def fsc(g, i=i):
                last = None
                for rr in range(2):
                    last = g.indirect_dma_start(
                        out=d["tmap_in"],
                        out_offset=IndirectOffsetOnAxis(
                            ap=b["EIDX"][:, 20 + rr:21 + rr], axis=0),
                        in_=b["TVS"][:, i:i + 1], in_offset=None,
                        bounds_check=c.E * c.CAP - 1, oob_is_err=False)
                return last
            em.op("gpsimd", fsc, rd=("EIDX", "consts", "tmap_in"),
                  wr=("tmap_in",), dma=True)

        self.allgather_global("tmap_in", "tmap_all", op="AllReduce",
                              alu=ALU.add)
        self.dbg_dram("tmap", "tmap_all")

    def expert_ffn(self):
        c, b, v, d, em = self.c, self.b, self.v, self.d, self.em

        def fti(ve):
            ve.tensor_scalar(b["SM"][:, 216:217], b["IOTS"][:, :],
                             float(c.CAPT - 1), None, op0=ALU.min)
            ve.drain()
            ve.tensor_tensor(b["SM"][:, 216:217], b["SM"][:, 216:217],
                             b["EBD"][:, :], op=ALU.add)
            ve.drain()
            return ve.tensor_copy(b["EIDX"][:, 0:1], b["SM"][:, 216:217])
        em.op("vector", fti, rd=("consts",), wr=("SM", "EIDX"))

        def ftg(g):
            return g.indirect_dma_start(
                out=b["ROUT"][:, 2048:2176], out_offset=None,
                in_=d["tmap_all"].rearrange("(a p) o -> a (p o)", p=128),
                in_offset=IndirectOffsetOnAxis(ap=b["EIDX"][:, 0:1], axis=0))
        em.op("gpsimd", ftg, rd=("tmap_all", "EIDX"), wr=("ROUT",), dma=True)

        def ftt(t):
            t.transpose(b["RSB"][:, 0:c.CAPT], b["ROUT"][0:c.CAPT, 2048:2176],
                        b["IDF"][0:c.CAPT, 0:c.CAPT])
            return t.drain()
        em.op("tensor", ftt, rd=("ROUT", "consts"), wr=("RSB",))

        def fte(ve):
            ve.tensor_copy(b["ROUT"][:, 0:c.CAPT], b["RSB"][:, 0:c.CAPT])
            ve.drain()
            return ve.tensor_copy(b["EIDX"][:, 0:c.CAPT],
                                  b["ROUT"][:, 0:c.CAPT])
        em.op("vector", fte, rd=("RSB",), wr=("ROUT", "EIDX", "RSB"))

        for j in range(c.CAPT):
            gb = (v["EIG"], "EIG") if j % 2 == 0 else (v["MO1"], "MO1")

            def fgi(g, j=j, gb=gb):
                return g.indirect_dma_start(
                    out=gb[0][:, 0:c.D], out_offset=None, in_=d["flat_all"],
                    in_offset=IndirectOffsetOnAxis(ap=b["EIDX"][:, j:j + 1],
                                                   axis=0))
            em.op("gpsimd", fgi, rd=("flat_all", "EIDX"), wr=(gb[1],),
                  dma=True)

            for g0 in range(0, c.DCH, 4):
                pbank = (b["RSB"], "RSB") if (j + g0 // 4) % 2 == 0 else                     (b["PVB"], "PVB")

                def ftr(t, g0=g0, gb=gb, pbank=pbank):
                    pb = pbank[0].ap().bitcast(BF16)
                    for k in range(4):
                        cch = g0 + k
                        t.transpose(pb[:, 128 * k:128 * (k + 1)],
                                    gb[0][:, cch * 128:(cch + 1) * 128],
                                    b["IDB"][:, :])
                    return t.drain()
                em.op("tensor", ftr, rd=(gb[1], "consts"), wr=(pbank[1],))

                def fev(ve, j=j, g0=g0, pbank=pbank):
                    pb = pbank[0].ap().bitcast(BF16)
                    last = None
                    for k in range(4):
                        cch = g0 + k
                        last = ve.tensor_copy(
                            v["EIT"][:, cch * c.CAP + j * 128:cch * c.CAP + (j + 1) * 128],
                            pb[:, 128 * k:128 * (k + 1)])
                    return last
                em.op("vector", fev, rd=(pbank[1],), wr=("EIT", pbank[1]))

        for s_i in range(c.NSW):
            r0 = s_i * 512
            for ft in range(c.FT):
                par = ft % 2

                def fw1(g, ft=ft, par=par):
                    w = c.DCH * 128
                    return g.dma_start(
                        out=v["W1B"][:, par * w:(par + 1) * w],
                        in_=d["w1e"][ft, :, :])
                em.op("gpsimd", fw1, wr=(f"W1B{par}",), dma=True)

                def fmm(t, par=par, r0=r0):
                    w = c.DCH * 128
                    for cch in range(c.DCH):
                        t.matmul(
                            b["PS6"][:, par * 512:par * 512 + 512],
                            v["W1B"][:, par * w + cch * 128:par * w + (cch + 1) * 128],
                            v["EIT"][:, cch * c.CAP + r0:cch * c.CAP + r0 + 512],
                            start=(cch == 0), stop=(cch == c.DCH - 1))
                    return t.drain()
                em.op("tensor", fmm, rd=(f"W1B{par}", "EIT"), wr=(f"PS{par}",))

                def fgl(s, ft=ft, par=par):
                    return s.activation(v["MID"][:, ft * 512:(ft + 1) * 512],
                                        b["PS6"][:, par * 512:par * 512 + 512],
                                        AF.Gelu, bias=b["B1S"][:, ft:ft + 1])
                em.op("scalar", fgl, rd=(f"PS{par}", "consts"),
                      wr=("MID", f"PS{par}"))

            for dt in range(c.DCH):
                par = dt % 2

                def fw2(g, dt=dt):
                    return g.dma_start(out=v["W2B"][:, 0:c.FT * 128],
                                       in_=d["w2e"][dt, :, :])
                em.op("gpsimd", fw2, wr=("W2B",), dma=True)

                def fm2(t, par=par):
                    for fc in range(c.FT):
                        t.matmul(b["PS6"][:, (2 + par) * 512:(3 + par) * 512],
                                 v["W2B"][:, fc * 128:(fc + 1) * 128],
                                 v["MID"][:, fc * 512:(fc + 1) * 512],
                                 start=(fc == 0), stop=(fc == c.FT - 1))
                    return t.drain()
                em.op("tensor", fm2, rd=("W2B", "MID"), wr=(f"PS{2 + par}",))

                def fo(s, dt=dt, par=par):
                    return s.activation(
                        v["TPS"][:, 0:512],
                        b["PS6"][:, (2 + par) * 512:(3 + par) * 512],
                        AF.Identity, bias=b["B2S"][:, dt:dt + 1])
                em.op("scalar", fo, rd=(f"PS{2 + par}", "consts"),
                      wr=("TPS", f"PS{2 + par}"))

                def ftr2(t):
                    pb = b["RSB"].ap().bitcast(BF16)
                    for k in range(4):
                        t.transpose(pb[:, 128 * k:128 * (k + 1)],
                                    v["TPS"][:, 128 * k:128 * (k + 1)],
                                    b["IDB"][:, :])
                    return t.drain()
                em.op("tensor", ftr2, rd=("TPS", "consts"), wr=("RSB",))

                def fcv2(ve):
                    pb = b["RSB"].ap().bitcast(BF16)
                    return ve.tensor_copy(v["TPS"][:, 1024:1536], pb[:, 0:512])
                em.op("vector", fcv2, rd=("RSB",), wr=("TPS2", "RSB"))

                def fst2(sy, dt=dt, r0=r0):
                    last = None
                    for k in range(4):
                        last = sy.dma_start(
                            out=d["eo_st"][r0 + k * 128:r0 + (k + 1) * 128,
                                           dt * 128:(dt + 1) * 128],
                            in_=v["TPS"][:, 1024 + 128 * k:1024 + 128 * (k + 1)])
                    return last
                em.op("sync", fst2, rd=("TPS2",), wr=("eo_st",), dma=True)

        self.allgather_global("eo_st", "eo_all")
        self.dbg_dram("eoall", "eo_all")

    def combine_final(self):
        c, b, v, d, em = self.c, self.b, self.v, self.d, self.em
        for i in range(c.TT):
            def fr(sy, i=i):
                xr = d["sp_x1"].rearrange("(a p) dd -> a p dd", p=128)
                yr = d["sp_y"].rearrange("(a p) dd -> a p dd", p=128)
                sy.dma_start(out=b["R"][:, i * c.D:(i + 1) * c.D],
                             in_=xr[i, :, :])
                return sy.dma_start(out=v["MID"][:, 0:c.D], in_=yr[i, :, :])
            em.op("sync", fr, rd=("sp_x1", "sp_y"), wr=("R", "MID"), dma=True)

            def fg0(g, i=i):
                g.indirect_dma_start(
                    out=v["EIG"][:, 0:c.D], out_offset=None, in_=d["eo_all"],
                    in_offset=IndirectOffsetOnAxis(
                        ap=b["CIX"][:, 2 * i:2 * i + 1], axis=0))
                return g.indirect_dma_start(
                    out=v["MO1"][:, 0:c.D], out_offset=None, in_=d["eo_all"],
                    in_offset=IndirectOffsetOnAxis(
                        ap=b["CIX"][:, 2 * i + 1:2 * i + 2], axis=0))
            em.op("gpsimd", fg0, rd=("eo_all", "CIX"), wr=("EIG", "MO1"),
                  dma=True)

            def fmo(ve, i=i):
                ve.tensor_scalar(b["ROUT"][:, 0:c.D], v["EIG"][:, 0:c.D],
                                 b["GTS"][:, 14 * i + 12:14 * i + 13], None,
                                 op0=ALU.mult)
                ve.drain()
                ve.scalar_tensor_tensor(b["ROUT"][:, 1024:1024 + c.D],
                                        v["MO1"][:, 0:c.D],
                                        b["GTS"][:, 14 * i + 13:14 * i + 14],
                                        b["ROUT"][:, 0:c.D],
                                        op0=ALU.mult, op1=ALU.add)
                ve.drain()
                return ve.tensor_tensor(b["ROUT"][:, 0:c.D],
                                        b["ROUT"][:, 1024:1024 + c.D],
                                        v["MID"][:, 0:c.D], op=ALU.add)
            em.op("vector", fmo, rd=("EIG", "MO1", "MID", "GTS"), wr=("ROUT",))

            mb_, reg = self.ln_stats(b["ROUT"][:, 0:c.D], ("ROUT",),
                                     slot=i % 2)

            def fo(ve, i=i, mb_=mb_, reg=reg):
                sm = b["SM"]
                ve.tensor_scalar(b["ROUT"][:, 1024:1024 + c.D],
                                 b["ROUT"][:, 0:c.D], sm[:, mb_:mb_ + 1],
                                 sm[:, mb_ + 3:mb_ + 4],
                                 op0=ALU.subtract, op1=ALU.mult)
                ve.drain()
                return ve.tensor_tensor(b["ROUT"][:, 1024:1024 + c.D],
                                        b["ROUT"][:, 1024:1024 + c.D],
                                        b["R"][:, i * c.D:(i + 1) * c.D],
                                        op=ALU.add)
            em.op("vector", fo, rd=("ROUT", reg, "R"), wr=("ROUT",))

            def fdo(sy, i=i):
                return sy.dma_start(
                    out=d["xout"].rearrange("(a p) dd -> a p dd", p=128)[i, :, :],
                    in_=b["ROUT"][:, 1024:1024 + c.D])
            em.op("sync", fdo, rd=("ROUT",), wr=("xout",), dma=True)

        pairs = self._fence_pairs()

        def fin(sy):
            for s, n in pairs:
                sy.wait_ge(s.h, n)
        self.blk.sync(fin)


def _lhsT_tiles(w, colsel):
    Din = w.shape[0]
    dch = Din // 128
    out = np.empty((128, dch * 128), np.float32)
    sub = w[:, colsel]
    for cc in range(dch):
        out[:, cc * 128:(cc + 1) * 128] = sub[cc * 128:(cc + 1) * 128, :]
    return out


def make_inputs(c, ins, r):
    D, TB = c.D, c.TB
    m = {}
    x = np.asarray(ins["L0_attn_x"], np.float32).reshape(c.N, D)
    m["xin"] = np.ascontiguousarray(x[r * c.TOK:(r + 1) * c.TOK])
    Wq = np.asarray(ins["L0_attn_W_qkv"], np.float32)
    bq = np.asarray(ins["L0_attn_b_qkv"], np.float32)
    Aq = np.asarray(ins["L0_attn_A_qkv"], np.float32)
    Bq = np.asarray(ins["L0_attn_B_qkv"], np.float32)
    Wpo = np.asarray(ins["L0_attn_W_proj"], np.float32)
    bpo = np.asarray(ins["L0_attn_b_proj"], np.float32)
    Ap = np.asarray(ins["L0_attn_A_proj"], np.float32)
    Bp = np.asarray(ins["L0_attn_B_proj"], np.float32)
    Wqkv_i = np.asarray(ins["L0_ffn_Wqkv"], np.float32)
    Wp_i = np.asarray(ins["L0_ffn_Wproj"], np.float32)

    for p in range(c.PAIRS):
        h1 = c.UN * (r % c.GS) + 2 * p
        h2 = h1 + 1
        for ki, knd in enumerate("qkv"):
            base = ki * D
            csel = np.r_[base + h1 * 64:base + h1 * 64 + 64,
                         base + h2 * 64:base + h2 * 64 + 64]
            m[f"o_w{knd}{p}"] = _lhsT_tiles(Wq, csel)
            m[f"o_b{knd}{p}"] = np.ascontiguousarray(bq[csel])
            m[f"o_B{knd}{p}"] = np.ascontiguousarray(Bq[:, csel])
            csel_i = np.r_[h1 * 192 + ki * 64:h1 * 192 + ki * 64 + 64,
                           h2 * 192 + ki * 64:h2 * 192 + ki * 64 + 64]
            m[f"i_w{knd}{p}"] = _lhsT_tiles(Wqkv_i, csel_i)
    Ao = np.empty((128, c.DCH * 8), np.float32)
    App = np.empty((128, c.DCH * 8), np.float32)
    for cc in range(c.DCH):
        Ao[:, cc * 8:(cc + 1) * 8] = Aq[cc * 128:(cc + 1) * 128, :]
        App[:, cc * 8:(cc + 1) * 8] = Ap[cc * 128:(cc + 1) * 128, :]
    m["o_A"] = Ao
    m["o_Ap"] = App
    m["o_Bp"] = np.ascontiguousarray(Bp)
    for lay, W in (("o", Wpo), ("i", Wp_i)):
        out = np.empty((128, c.DCH * D), np.float32)
        for cc in range(c.DCH):
            for co in range(c.DCH):
                out[:, (cc * c.DCH + co) * 128:(cc * c.DCH + co + 1) * 128] = \
                    W[cc * 128:(cc + 1) * 128, co * 128:(co + 1) * 128]
        m[f"{lay}_wp"] = out
    m["o_bp"] = bpo
    m["i_g1"] = np.asarray(ins["L0_ffn_ln1_g"], np.float32)
    m["i_b1"] = np.asarray(ins["L0_ffn_ln1_b"], np.float32)
    m["i_g2"] = np.asarray(ins["L0_ffn_ln2_g"], np.float32)
    m["i_b2"] = np.asarray(ins["L0_ffn_ln2_b"], np.float32)
    Wg = np.asarray(ins["L0_ffn_Wgate"], np.float32)
    wg = np.empty((128, c.DCH * 8), np.float32)
    for cc in range(c.DCH):
        wg[:, cc * 8:(cc + 1) * 8] = Wg[cc * 128:(cc + 1) * 128, :]
    m["wgate"] = wg
    W1 = np.asarray(ins["L0_ffn_W1"], np.float32)[r]
    W2 = np.asarray(ins["L0_ffn_W2"], np.float32)[r]
    w1r = np.empty((c.FT, 128, c.DCH * 128), np.float32)
    for ftl in range(c.FT):
        for cc in range(c.DCH):
            w1r[ftl, :, cc * 128:(cc + 1) * 128] = \
                W1[cc * 128:(cc + 1) * 128, ftl * 128:(ftl + 1) * 128]
    m["w1e"] = w1r
    w2r = np.empty((c.DCH, 128, c.FT * 128), np.float32)
    for dtl in range(c.DCH):
        for fc in range(c.FT):
            w2r[dtl, :, fc * 128:(fc + 1) * 128] = \
                W2[fc * 128:(fc + 1) * 128, dtl * 128:(dtl + 1) * 128]
    m["w2e"] = w2r
    m["b1e"] = np.asarray(ins["L0_ffn_b1"], np.float32)[r]
    m["b2e"] = np.asarray(ins["L0_ffn_b2"], np.float32)[r]
    m["ident_f"] = np.eye(128, dtype=np.float32)
    inv = 1.0 / 10000.0 ** (np.arange(0, HD, 2, dtype=np.float64) / HD)
    ang = np.arange(TB, dtype=np.float64)[:, None] * inv[None, :]
    cos = np.cos(ang).T.astype(np.float32)
    sin = np.sin(ang).T.astype(np.float32)
    m["cc_tab"] = np.ascontiguousarray(np.concatenate([cos] * 4, 0))
    m["ss_tab"] = np.ascontiguousarray(np.concatenate([-sin, sin, -sin, sin], 0))
    cm = np.zeros((128, 4 * QCH), np.float32)
    for j in range(4):
        kpos = j * 128 + np.arange(128)[:, None]
        qpos = np.arange(QCH)[None, :]
        cm[:, j * QCH:(j + 1) * QCH] = (kpos <= qpos).astype(np.float32)
    m["cmask"] = cm
    tv = np.empty((128, c.TT), np.float32)
    for i in range(c.TT):
        tv[:, i] = r * c.TOK + i * 128 + np.arange(128)
    m["tvals"] = tv
    m["iot"] = np.arange(128, dtype=np.float32).reshape(128, 1)
    m["io8"] = np.tile(np.arange(8, dtype=np.float32), (128, 1))
    pfx = np.zeros(8, np.float32)
    pfx[:r] = 1.0
    m["pfx"] = pfx
    m["ebd"] = np.full((128, 1), (c.CAP * r) // 128, np.float32)
    m["mb"] = np.full((128, 1), (r % c.GS) * c.GS * c.SR, np.float32)
    return m


_PROG_CACHE = {}


def kernel(**inputs):
    c = FULL
    if "full" not in _PROG_CACHE:
        _PROG_CACHE["full"] = K(c).build()
    nc = _PROG_CACHE["full"]
    in_maps = [make_inputs(c, inputs, r) for r in range(NCORE)]
    res = run_bass_kernel_spmd(nc, in_maps, core_ids=list(range(NCORE)))
    out = np.concatenate([np.asarray(res.results[r]["xout"], np.float32)
                          for r in range(NCORE)], 0)
    return out.reshape(c.B, c.TB, c.D)


# version 03
# speedup vs baseline: 1.0856x; 1.0510x over previous
import math
from contextlib import ExitStack

import numpy as np
import ml_dtypes

import concourse.bass as bass
import concourse.mybir as mybir
from concourse.bass import IndirectOffsetOnAxis
from concourse.bass_utils import run_bass_kernel_spmd

F32 = mybir.dt.float32
BF16 = mybir.dt.bfloat16
U32 = mybir.dt.uint32
AF = mybir.ActivationFunctionType
ALU = mybir.AluOpType
AXX = mybir.AxisListType.X

NCORE = 8
PTD = 4
HD = 64
ABLATE = set()
QCH = 512
EPS = 1e-5


class Cfg:
    def __init__(self, D=1024, TB=8192, H=16, E=8, FFM=4):
        self.B = 2
        self.D, self.TB, self.H, self.E = D, TB, H, E
        self.FF = FFM * D
        assert D == H * HD and E == NCORE
        self.N = self.B * TB
        self.TOK = self.N // NCORE
        self.GS = NCORE // self.B
        self.UN = self.B * H // NCORE
        self.PAIRS = self.UN // 2
        self.DCH = D // 128
        self.TT = self.TOK // 128
        self.NQC = TB // QCH
        self.NKT = TB // 128
        self.CAP = int(math.ceil(2.0 * self.N / E))
        self.FT = self.FF // 128
        self.NSW = self.CAP // 512
        self.CAPT = self.CAP // 128
        self.SR = self.PAIRS * 128
        assert self.TOK % QCH == 0 and self.CAP % 512 == 0


FULL = Cfg()
MINI = Cfg(D=512, TB=2048, H=8)


class Sem:
    def __init__(self, h):
        self.h = h
        self.n = 0

    def inc(self, k=1):
        self.n += k
        return self.n


_EPIR = {"EPIs", "EPIr", "EPIrr", "EPIb1", "EPIb2", "EPIo", "ROUT"}
ALIAS = {
    "QT": {"PIN", "EIT"}, "KT": {"PIN", "EIT"}, "VV": {"WP", "EIT"},
    "XS0": {"SQ", "EIT"}, "XS1": {"SQ", "EIT"},
    "PIN": {"QT", "KT", "EIT"}, "WP": {"VV", "EIT"},
    "SQ": {"XS0", "XS1", "EIT"},
    "EIT": {"QT", "KT", "VV", "XS0", "XS1", "PIN", "WP", "SQ"},
    "H2": {"MID", "LHA"}, "MID": {"H2", "LHA"}, "LHA": {"H2", "MID"},
    "WQKV": {"W1B"}, "W1B": {"WQKV"},
    "LB": {"EIG", "MO1"}, "EIG": {"LB"}, "MO1": {"LB"},
    "GB2": {"CUM"}, "CUM": {"GB2"},
    "SM": {"SMln0", "SMln1", "SMcb"}, "SMln0": {"SM"}, "SMln1": {"SM"},
    "SMcb": {"SM"},
    "GBB": {"PT", "LHAP", "W2B"},
    "LHAP": {"PT", "GBB", "W2B"}, "PT": {"LHAP", "GBB", "W2B", "CSL0", "CSL1", "MKT"},
    "MKT": {"CSL0", "CSL1", "W2B", "PT"}, "CSL0": {"MKT", "W2B", "PT"},
    "CSL1": {"MKT", "W2B", "PT"},
        "W2B": {"PT", "CSL0", "CSL1", "MKT", "LHAP", "GBB"},
    "W2B2": {"CM", "TPS", "TPS2"},
    "CM": {"W2B2"}, "TPS": {"W2B2"}, "TPS2": {"W2B2"},
}
for _r in _EPIR:
    ALIAS[_r] = (ALIAS.get(_r, set()) | _EPIR) - {_r}


class EM:
    ESEM = {"tensor": "eT", "scalar": "eS", "vector": "eV", "gpsimd": "eG",
            "sync": "eY"}

    def __init__(self, blk, sems, pools):
        self.blk = blk
        self.s = sems
        self.pools = pools
        self.pidx = {q: 0 for q in pools}
        self.w = {}
        self.r = {}

    def _pick(self, q):
        pool = self.pools[q]
        i = self.pidx[q]
        self.pidx[q] = (i + 1) % len(pool)
        return pool[i]

    def _expand(self, regs):
        out = set(regs)
        for reg in regs:
            out |= ALIAS.get(reg, set())
        return out

    def _waits(self, rd, wr):
        acc = {}

        def add(pair):
            if pair is None:
                return
            s, n = pair
            if acc.get(s, 0) < n:
                acc[s] = n
        for reg in rd:
            add(self.w.get(reg))
        for reg in self._expand(wr):
            add(self.w.get(reg))
            for s, n in self.r.get(reg, {}).items():
                add((s, n))
        return acc

    def op(self, eng, fn, rd=(), wr=(), dma=False, cc=False):
        if dma:
            sem = self._pick({"sync": "qsy", "scalar": "qsc",
                              "gpsimd": "qg"}[eng])
            step = 16
        elif cc:
            sem = self._pick("qcc")
            step = 1
        else:
            sem = self.s[self.ESEM[eng]]
            step = 1
        waits = dict(self._waits(rd, wr))
        if (dma or cc) and sem.n > 0:
            if waits.get(sem, 0) < sem.n:
                waits[sem] = sem.n
        cnt = [0]

        class _DmaProxy:
            def __init__(self, e):
                self._e = e

            def dma_start(self, *a, **k):
                inst = self._e.dma_start(*a, **k)
                inst.then_inc(sem.h, step)
                cnt[0] += step
                return inst

            def indirect_dma_start(self, *a, **k):
                inst = self._e.indirect_dma_start(*a, **k)
                inst.then_inc(sem.h, step)
                cnt[0] += step
                return inst

            def collective_compute(self, *a, **k):
                inst = self._e.collective_compute(*a, **k)
                inst.then_inc(sem.h, step)
                cnt[0] += step
                return inst

            def __getattr__(self, name):
                return getattr(self._e, name)

        def body(e):
            for s, n in waits.items():
                e.wait_ge(s.h, n)
            if dma or cc:
                fn(_DmaProxy(e))
            else:
                inst = fn(e)
                assert inst is not None, "op fn must return its last instruction"
                inst.then_inc(sem.h, 1)
                cnt[0] = 1
        getattr(self.blk, eng)(body)
        n = sem.inc(cnt[0])
        for reg in self._expand(wr):
            self.w[reg] = (sem, n)
            self.r[reg] = {}
        for reg in rd:
            self.r.setdefault(reg, {})[sem] = n
        return sem, n

    def note_write(self, reg, sem, n):
        for rg in self._expand((reg,)):
            self.w[rg] = (sem, n)
            self.r[rg] = {}

    def note_read(self, reg, sem, n):
        self.r.setdefault(reg, {})[sem] = n

    def need(self, *regs):
        acc = {}
        for reg in regs:
            p = self.w.get(reg)
            if p and acc.get(p[0], 0) < p[1]:
                acc[p[0]] = p[1]
        return list(acc.items())

    def need_w(self, *regs):
        acc = {}
        for reg in self._expand(regs):
            pairs = []
            if self.w.get(reg):
                pairs.append(self.w[reg])
            pairs += list(self.r.get(reg, {}).items())
            for s, n in pairs:
                if acc.get(s, 0) < n:
                    acc[s] = n
        return list(acc.items())


class K:
    def __init__(self, cfg=FULL, dbg=()):
        self.c = cfg
        self.dbg = set(dbg)
        self.nc = bass.Bass("TRN2", target_bir_lowering=False)
        self.d = {}
        self.b = {}
        self.s = {}
        self.dbg_keys = []

    def declare(self):
        nc, c, d = self.nc, self.c, self.d

        def inp(name, shape, dt=F32):
            d[name] = nc.dram_tensor(name, shape, dt, kind="ExternalInput").ap()

        inp("xin", [c.TOK, c.D])
        for p in range(c.PAIRS):
            for lay in ("o", "i"):
                for knd in "qkv":
                    inp(f"{lay}_w{knd}{p}", [128, c.DCH * 128])
            for knd in "qkv":
                inp(f"o_b{knd}{p}", [128])
                inp(f"o_B{knd}{p}", [8, 128])
        inp("o_A", [128, c.DCH * 8])
        for nm in ("i_g1", "i_b1", "i_g2", "i_b2", "o_bp", "b2e"):
            inp(nm, [c.D])
        inp("b1e", [c.FF])
        for lay in ("o", "i"):
            inp(f"{lay}_wp", [128, c.DCH * c.D])
        inp("o_Ap", [128, c.DCH * 8])
        inp("o_Bp", [8, c.D])
        inp("wgate", [128, c.DCH * 8])
        inp("w1e", [c.FT, 128, c.DCH * 128])
        inp("w2e", [c.DCH, 128, c.FT * 128])
        inp("ident_f", [128, 128])
        inp("cc_tab", [128, c.TB])
        inp("ss_tab", [128, c.TB])
        inp("cmask", [128, 4 * QCH])
        inp("tvals", [128, c.TT])
        inp("iot", [128, 1])
        inp("io8", [128, 8])
        inp("pfx", [8])
        inp("ebd", [128, 1])
        inp("mb", [128, 1])

        d["xout"] = nc.dram_tensor("xout", [c.TOK, c.D], F32,
                                   kind="ExternalOutput").ap()

        def internal(name, shape, dt):
            d[name] = nc.dram_tensor(name, shape, dt, kind="Internal").ap()

        internal("ag_in", [c.D, c.TOK], BF16)
        internal("ag_out", [c.GS * c.D, c.TOK], BF16)
        for m in range(c.GS):
            internal(f"st{m}", [c.SR, c.TOK], BF16)
        internal("ao_all", [c.GS * c.GS * c.SR, c.TOK], BF16)
        internal("flat_st", [c.TOK, c.D], BF16)
        internal("flat_all", [c.N, c.D], BF16)
        internal("cnt_in", [1, 8], F32)
        internal("cnt_all", [8, 8], F32)
        internal("tmap_in", [c.E * c.CAP, 1], F32)
        internal("tmap_all", [c.E * c.CAP, 1], F32)
        internal("ei_tok", [c.CAP, c.D], BF16)
        internal("eo_st", [c.CAP, c.D], BF16)
        internal("eo_all", [c.E * c.CAP, c.D], BF16)
        internal("sp_x1", [c.TOK, c.D], F32)
        internal("sp_y", [c.TOK, c.D], BF16)

    def alloc(self, st):
        nc, c, b = self.nc, self.c, self.b

        def sb(name, shape, dt):
            b[name] = st.enter_context(nc.sbuf_tensor(name, shape, dt))

        sb("R", [128, c.TT * c.D], F32)
        sb("GB2", [128, 2 * c.D], F32)
        sb("SM", [128, 256], F32)
        sb("ROUT", [128, 3072], F32)
        sb("AR1", [128, 32768], BF16)
        sb("AR2", [128, 8192], BF16)
        sb("AR3", [128, 20480], BF16)
        sb("ONES", [128, 1], BF16)
        sb("ONESF", [128, 128], F32)
        sb("IDB", [128, 128], BF16)
        sb("IDF", [128, 128], F32)
        sb("WG", [128, c.DCH * 8], F32)
        sb("EIDX", [128, max(c.CAPT, 24)], U32)
        sb("CIX", [128, 2 * c.TT], U32)
        sb("GTS", [128, 14 * c.TT], F32)
        sb("TVS", [128, c.TT], F32)
        sb("CNT", [8, 8], F32)
        sb("PFXS", [8, 1], F32)
        b["PS6"] = st.enter_context(nc.psum_tensor("PS6", [128, 3072], F32))
        b["PVB"] = st.enter_context(nc.psum_tensor("PVB", [128, 512], F32))
        b["RSB"] = st.enter_context(nc.psum_tensor("RSB", [128, 512], F32))

        A1, A2, A3 = b["AR1"], b["AR2"], b["AR3"]
        v = self.v = {}
        v["QT"] = A1[:, 0:8192]
        v["KT"] = A1[:, 8192:16384]
        v["VV"] = A1[:, 16384:24576]
        v["XS"] = A1[:, 24576:32768]
        v["PIN"] = A1[:, 0:16384]
        v["WP"] = A1[:, 16384:24576]
        v["SQ"] = A1[:, 24576:32768]
        v["EIT"] = A1[:, 0:c.DCH * c.CAP]
        v["PT"] = A2[:, 0:4096]
        v["CS"] = A2[:, 2048:4096]
        v["CM"] = A2[:, 4096:6144]
        v["GBB"] = A2[:, 0:2048]
        v["TPS"] = A2[:, 6144:8192]
        v["H2"] = A3[:, 0:16384]
        v["WQKV"] = A3[:, 16384:18432]
        v["LB"] = A3[:, 18432:20480]
        v["MID"] = A3[:, 0:16384]
        v["W1B"] = A3[:, 16384:18432]
        v["W2B"] = A2[:, 0:c.FT * 128]
        v["W2B2"] = A2[:, 4096:4096 + c.FT * 128]
        v["EIG"] = A3[:, 18432:19456]
        v["MO1"] = A3[:, 19456:20480]
        b["EPI"] = b["ROUT"][:, 0:3072]
        b["CUM"] = b["GB2"][:, 0:c.TOK]
        b["MKT"] = A2[:, 2048:2048 + c.TOK]
        b["LHA"] = A3[:, 0:c.TB]
        b["LHAP"] = A2[:, 0:c.TOK]
        b["ONEBR"] = A2[:, 2048:2176]
        b["IO8"] = b["SM"][:, 144:152]
        b["B2S"] = b["SM"][:, 152:160]
        b["BPS"] = b["SM"][:, 160:168]
        b["BQB"] = b["SM"][:, 168:174]
        b["B1S"] = b["SM"][:, 176:176 + c.FT]
        b["IOTS"] = b["SM"][:, 224:225]
        b["EBD"] = b["SM"][:, 225:226]
        b["MBS"] = b["SM"][:, 226:227]
        b["EPSC"] = b["SM"][:, 227:228]

    def build(self):
        self.declare()
        with ExitStack() as st:
            self.alloc(st)
            for nm in ("smm", "sex", "spv", "smsk",
                       "eT", "eS", "eV", "eG", "eY"):
                self.s[nm] = Sem(st.enter_context(self.nc.semaphore(nm)))
            pools = {}
            for q, npool in (("qsy", 24), ("qsc", 12), ("qg", 24), ("qcc", 8)):
                pools[q] = [Sem(st.enter_context(self.nc.semaphore(f"{q}{i}")))
                            for i in range(npool)]
            with self.nc.Block() as blk:
                self.blk = blk
                self.em = EM(blk, self.s, pools)
                self.emit()
        return self.nc

    def _fence_pairs(self):
        pairs = [(s, s.n) for s in self.s.values() if s.n]
        for pool in self.em.pools.values():
            pairs += [(s, s.n) for s in pool if s.n]
        return pairs

    def dbg_sbuf(self, name, view, regs=()):
        if name not in self.dbg:
            return
        h = self.nc.dram_tensor("dbg_" + name, [128, view.shape[-1]],
                                view.tensor.dtype, kind="ExternalOutput").ap()
        self.dbg_keys.append("dbg_" + name)
        pairs = self._fence_pairs()

        def f(sy):
            for s, n in pairs:
                sy.wait_ge(s.h, n)
            return sy.dma_start(out=h, in_=view)
        self.em.op("sync", f, rd=tuple(regs), dma=True)

    def dbg_dram(self, name, key):
        if name not in self.dbg:
            return
        src = self.d[key]
        h = self.nc.dram_tensor("dbg_" + name, list(src.tensor.shape),
                                src.tensor.dtype, kind="ExternalOutput").ap()
        self.dbg_keys.append("dbg_" + name)
        pairs = self._fence_pairs()

        def f(sy):
            for s, n in pairs:
                sy.wait_ge(s.h, n)
            return sy.dma_start(out=h, in_=src)
        self.em.op("sync", f, rd=(key,), dma=True)

    def emit(self):
        c = self.c
        self.p0_load()
        self.ln_stage_xln()
        self.allgather_group("ag_in", "ag_out")
        self.dbg_dram("xlnT", "ag_out")
        for p in range(c.PAIRS):
            self.qkv_pair("o", p)
            self.dbg_sbuf(f"o_qt{p}", self.v["QT"][:, 0:c.TB], regs=("QT",))
            self.dbg_sbuf(f"o_kt{p}", self.v["KT"][:, 0:c.TB], regs=("KT",))
            self.dbg_sbuf(f"o_vv{p}", self.v["VV"][:, 0:c.TB], regs=("VV",))
            self.bounds_pair("o", p)
            self.attention_pair("o", p, causal=False)
        self.proj_and_residual("o")
        self.dbg_sbuf("x1", self.b["R"][:, 0:c.TT * c.D], regs=("R",))
        self.h2_u_stage()
        self.allgather_group("ag_in", "ag_out")
        self.dbg_dram("uT", "ag_out")
        for p in range(c.PAIRS):
            self.qkv_pair("i", p)
            self.dbg_sbuf(f"i_qt{p}", self.v["QT"][:, 0:c.TB], regs=("QT",))
            self.bounds_pair("i", p)
            self.attention_pair("i", p, causal=True)
        self.proj_and_residual("i")
        self.dbg_sbuf("ybuf", self.v["H2"][:, 0:c.TT * c.D], regs=("H2",))
        self.spill_and_flat()
        self.allgather_global("flat_st", "flat_all")
        self.routing()
        self.expert_ffn()
        self.combine_final()

    def p0_load(self):
        c, b, v, d, em = self.c, self.b, self.v, self.d, self.em

        def fm(g):
            g.memset(b["SM"][:, :], 0.0)
            g.memset(b["ROUT"][:, :], 0.0)
            g.drain()
            g.memset(b["EPSC"][:, 0:1], float(EPS))
            g.memset(b["ONES"][:, :], 1.0)
            g.memset(b["ONEBR"][0:1, :], 1.0)
            return g.memset(b["ONESF"][:, :], 1.0)
        em.op("gpsimd", fm, wr=("consts2", "SM", "CSL0", "ROUT"))

        def f1(sy):
            xr = d["xin"].rearrange("(i p) dd -> i p dd", p=128)
            for i in range(c.TT):
                sy.dma_start(out=b["R"][:, i * c.D:(i + 1) * c.D], in_=xr[i, :, :])
            sy.dma_start(out=b["IDF"][:, :], in_=d["ident_f"])
            sy.dma_start(out=b["PFXS"][:, :],
                         in_=d["pfx"].rearrange("(a o) -> a o", o=1))
            sy.dma_start(out=b["EBD"][:, 0:1], in_=d["ebd"])
            sy.dma_start(out=b["MBS"][:, 0:1], in_=d["mb"])
            sy.dma_start(out=b["IOTS"][:, 0:1], in_=d["iot"])
            sy.dma_start(out=b["SM"][:, 144:152], in_=d["io8"])
            sy.dma_start(out=b["TVS"][:, :], in_=d["tvals"])
            sy.dma_start(out=b["B1S"][:, 0:c.FT],
                         in_=d["b1e"].rearrange("(f p) -> p f", p=128), allow_slow_non_contiguous=True)
            sy.dma_start(out=b["B2S"][:, 0:c.DCH],
                         in_=d["b2e"].rearrange("(t p) -> p t", p=128), allow_slow_non_contiguous=True)
            sy.dma_start(out=b["BPS"][:, 0:c.DCH],
                         in_=d["o_bp"].rearrange("(t p) -> p t", p=128), allow_slow_non_contiguous=True)
            sy.dma_start(out=b["WG"][:, :], in_=d["wgate"])
            for p in range(c.PAIRS):
                for ki, knd in enumerate("qkv"):
                    sy.dma_start(out=b["BQB"][:, 3 * p + ki:3 * p + ki + 1],
                                 in_=d[f"o_b{knd}{p}"].rearrange("(p o) -> p o", o=1), allow_slow_non_contiguous=True)
            sy.dma_start(out=b["ROUT"][0:1, 0:c.D],
                         in_=d["i_g2"].rearrange("(a x) -> a x", a=1))
            return sy.dma_start(out=b["ROUT"][0:1, c.D:2 * c.D],
                                in_=d["i_b2"].rearrange("(a x) -> a x", a=1))
        em.op("sync", f1, wr=("R", "consts", "ROUT", "SM"), dma=True)

        def fg(g):
            g.dma_start(out=b["IDB"][:, :], in_=d["ident_f"])
            return g.dma_start(out=v["CM"], in_=d["cmask"])
        em.op("gpsimd", fg, wr=("CM", "consts"), dma=True)

        for j in range(2 * c.DCH):
            half = j // c.DCH
            col = (j % c.DCH) * 128

            def fb(t, half=half, col=col):
                src = b["ROUT"][0:1, half * c.D + col:half * c.D + col + 128]
                t.matmul(b["PS6"][:, 0:128], b["ONESF"][0:1, :], src,
                         start=True, stop=True)
                return t.drain()
            em.op("tensor", fb, rd=("ROUT", "consts2"), wr=("PS0",))

            def fe(ve, j=j):
                return ve.tensor_copy(b["GB2"][:, j * 128:(j + 1) * 128],
                                      b["PS6"][:, 0:128])
            em.op("vector", fe, rd=("PS0",), wr=("GB2", "PS0"))

    def ln_stats(self, src_view, rd_regs, slot=0):
        c, b, em = self.c, self.b, self.em
        base = 24 * slot
        mb_ = 48 + 8 * slot
        reg = f"SMln{slot}"

        def fv(ve):
            sm = b["SM"]
            for j in range(c.D // 512):
                ve.bn_stats(sm[:, base + 6 * j:base + 6 * j + 6],
                            src_view[:, 512 * j:512 * (j + 1)])
            ve.drain()
            ve.bn_aggr(sm[:, mb_:mb_ + 2], sm[:, base:base + 6 * (c.D // 512)])
            return ve.drain()
        em.op("vector", fv, rd=rd_regs, wr=(reg,))

        def fs(s):
            sm = b["SM"]
            s.activation(sm[:, mb_ + 2:mb_ + 3], sm[:, mb_ + 1:mb_ + 2], AF.Ln,
                         bias=b["EPSC"][:, 0:1])
            s.drain()
            s.activation(sm[:, mb_ + 3:mb_ + 4], sm[:, mb_ + 2:mb_ + 3], AF.Exp,
                         scale=-0.5)
            return s.drain()
        em.op("scalar", fs, rd=(reg,), wr=(reg,))
        return mb_, reg

    def ln_stage_xln(self):
        c, b, v, em = self.c, self.b, self.v, self.em
        for i in range(c.TT):
            mb_, reg = self.ln_stats(b["R"][:, i * c.D:(i + 1) * c.D], ("R",),
                                     slot=i % 2)

            def fa(ve, i=i, mb_=mb_):
                sm = b["SM"]
                return ve.tensor_scalar(v["TPS"][:, 0:c.D],
                                        b["R"][:, i * c.D:(i + 1) * c.D],
                                        sm[:, mb_:mb_ + 1], sm[:, mb_ + 3:mb_ + 4],
                                        op0=ALU.subtract, op1=ALU.mult)
            em.op("vector", fa, rd=("R", reg), wr=("TPS",))
            self.transpose_stage_tile(i, "ag_in")

    def transpose_stage_tile(self, i, dram):
        c, b, v, em = self.c, self.b, self.v, self.em
        for g0 in range(0, c.DCH, 4):
            def ft(t, g0=g0):
                pb = b["PVB"].ap().bitcast(BF16)
                for k in range(4):
                    cch = g0 + k
                    t.transpose(pb[:, 128 * k:128 * (k + 1)],
                                v["TPS"][:, cch * 128:(cch + 1) * 128],
                                b["IDB"][:, :])
                return t.drain()
            em.op("tensor", ft, rd=("TPS", "consts"), wr=("PVB",))

            def fe(ve):
                pb = b["PVB"].ap().bitcast(BF16)
                return ve.tensor_copy(v["TPS"][:, 1024:1536], pb[:, 0:512])
            em.op("vector", fe, rd=("PVB",), wr=("TPS2", "PVB"))

            def fd(sy, i=i, g0=g0):
                last = None
                for k in range(4):
                    cch = g0 + k
                    last = sy.dma_start(
                        out=self.d[dram][cch * 128:(cch + 1) * 128,
                                         i * 128:(i + 1) * 128],
                        in_=v["TPS"][:, 1024 + 128 * k:1024 + 128 * (k + 1)])
                return last
            em.op("sync", fd, rd=("TPS2",), wr=(dram,), dma=True)

    def allgather_group(self, src, dst):
        c, em = self.c, self.em

        def fg(g):
            rg = [list(range(g0, g0 + c.GS)) for g0 in range(0, NCORE, c.GS)]
            g.collective_compute("AllGather", ALU.bypass, replica_groups=rg,
                                 ins=[self.d[src]], outs=[self.d[dst]])
        em.op("gpsimd", fg, rd=(src,), wr=(dst,), cc=True)

    def allgather_global(self, src, dst, op="AllGather", alu=ALU.bypass):
        em = self.em

        def fg(g):
            g.collective_compute(op, alu, replica_groups=[list(range(NCORE))],
                                 ins=[self.d[src]], outs=[self.d[dst]])
        em.op("gpsimd", fg, rd=(src,), wr=(dst,), cc=True)

    def chunk_ags(self):
        c, em = self.c, self.em
        for m in range(c.GS):
            def fg(g, m=m):
                rg = [list(range(g0, g0 + c.GS))
                      for g0 in range(0, NCORE, c.GS)]
                g.collective_compute(
                    "AllGather", ALU.bypass, replica_groups=rg,
                    ins=[self.d[f"st{m}"]],
                    outs=[self.d["ao_all"][m * c.GS * c.SR:(m + 1) * c.GS * c.SR, :]])
            em.op("gpsimd", fg, rd=(f"st{m}",), wr=(f"aoall{m}",), cc=True)

    def qkv_pair(self, lay, p):
        c, b, v, d, em = self.c, self.b, self.v, self.d, self.em
        do_lora = lay == "o"
        do_hA = do_lora and p == 0

        if do_lora:
            def fwl(g, p=p):
                g.dma_start(out=v["LB"][0:8, 0:128], in_=d[f"o_Bq{p}"])
                g.dma_start(out=v["LB"][0:8, 128:256], in_=d[f"o_Bk{p}"])
                g.dma_start(out=v["LB"][0:8, 256:384], in_=d[f"o_Bv{p}"])
                return g.dma_start(out=v["LB"][:, 1024:1024 + c.DCH * 8],
                                   in_=d["o_A"])
            em.op("gpsimd", fwl, wr=("LB",), dma=True)

        if do_hA:
            for tc in range(1 if "qkv" in ABLATE else c.NQC):
                par = tc % 2
                self._qkv_load_chunk(tc, par)

                def fha(t, tc=tc, par=par):
                    for cch in range(c.DCH):
                        t.matmul(b["PS6"][0:8, 2560:3072],
                                 v["LB"][:, 1024 + cch * 8:1024 + (cch + 1) * 8],
                                 v["XS"][:, 4096 * par + cch * 512:4096 * par + (cch + 1) * 512],
                                 start=(cch == 0), stop=(cch == c.DCH - 1))
                    return t.drain()
                em.op("tensor", fha, rd=(f"XS{par}", "LB"), wr=("PS5",))

                def fhe(s, tc=tc):
                    return s.activation(b["LHA"][0:8, tc * QCH:(tc + 1) * QCH],
                                        b["PS6"][0:8, 2560:3072], AF.Copy,
                                        scale=2.0)
                em.op("scalar", fhe, rd=("PS5",), wr=("LHA", "PS5"))

        for ki, knd in enumerate("qkv"):
            def fw(g, ki=ki, knd=knd, lay=lay, p=p):
                return g.dma_start(out=v["WQKV"][:, 0:c.DCH * 128],
                                   in_=d[f"{lay}_w{knd}{p}"])
            em.op("gpsimd", fw, wr=("WQKV",), dma=True)
            dstv = (v["QT"], v["KT"], v["VV"])[ki]
            dreg = ("QT", "KT", "VV")[ki]
            for tc in range(1 if "qkv" in ABLATE else c.NQC):
                par = tc % 2
                self._qkv_load_chunk(tc, par)
                if lay == "i" and ki == 0:
                    bt = (tc * QCH) % c.TB

                    def ftab(g, par=par, bt=bt):
                        g.dma_start(out=v["CS"][:, par * 1024:par * 1024 + 512],
                                    in_=d["cc_tab"][:, bt:bt + QCH])
                        return g.dma_start(
                            out=v["CS"][:, par * 1024 + 512:par * 1024 + 1024],
                            in_=d["ss_tab"][:, bt:bt + QCH])
                    em.op("gpsimd", ftab, wr=(f"CSL{par}",), dma=True)
                bank = tc % 2

                def fmm(t, tc=tc, par=par, ki=ki, bank=bank):
                    for cch in range(c.DCH):
                        t.matmul(b["PS6"][:, bank * 512:bank * 512 + 512],
                                 v["WQKV"][:, cch * 128:(cch + 1) * 128],
                                 v["XS"][:, 4096 * par + cch * 512:4096 * par + (cch + 1) * 512],
                                 start=(cch == 0),
                                 stop=(not do_lora and cch == c.DCH - 1))
                    if do_lora:
                        t.matmul(b["PS6"][:, bank * 512:bank * 512 + 512],
                                 v["LB"][0:8, ki * 128:(ki + 1) * 128],
                                 b["LHA"][0:8, tc * QCH:(tc + 1) * QCH],
                                 start=False, stop=True)
                    return t.drain()
                rdl = (f"XS{par}", "WQKV", "LB", "LHA") if do_lora else \
                      (f"XS{par}", "WQKV")
                em.op("tensor", fmm, rd=rdl, wr=(f"PS{bank}",))

                def fev(s, tc=tc, ki=ki, bank=bank, p=p, lay=lay, dstv=dstv):
                    if lay == "o":
                        return s.activation(
                            dstv[:, tc * QCH:(tc + 1) * QCH],
                            b["PS6"][:, bank * 512:bank * 512 + 512],
                            AF.Identity,
                            bias=b["BQB"][:, 3 * p + ki:3 * p + ki + 1])
                    return s.copy(dstv[:, tc * QCH:(tc + 1) * QCH],
                                  b["PS6"][:, bank * 512:bank * 512 + 512])
                em.op("scalar", fev, rd=(f"PS{bank}", "consts"),
                      wr=(dreg, f"PS{bank}"))

                if lay == "i" and ki < 2:
                    def frope(ve, tc=tc, par=par, dstv=dstv, dreg=dreg):
                        cs = v["CS"][:, par * 1024:par * 1024 + 512]
                        ss = v["CS"][:, par * 1024 + 512:par * 1024 + 1024]
                        q = dstv[:, tc * QCH:(tc + 1) * QCH]
                        sw = v["PT"][:, 0:512]
                        for hh in (0, 64):
                            ve.tensor_copy(sw[hh:hh + 32, :], q[hh + 32:hh + 64, :])
                            ve.tensor_copy(sw[hh + 32:hh + 64, :], q[hh:hh + 32, :])
                        ve.drain()
                        ve.tensor_tensor(q, q, cs, op=ALU.mult)
                        ve.tensor_tensor(sw[:, :], sw[:, :], ss, op=ALU.mult)
                        ve.drain()
                        ve.tensor_tensor(q, q, sw[:, :], op=ALU.add)
                        return ve.drain()
                    em.op("vector", frope, rd=(dreg, f"CSL{par}"),
                          wr=(dreg, "PT"))

        NB = c.TB // 128
        for j0 in range(0, NB, 4):
            def ftv(t, j0=j0):
                pb = b["RSB"].ap().bitcast(BF16)
                for k in range(4):
                    j = j0 + k
                    t.transpose(pb[:, 128 * k:128 * (k + 1)],
                                v["VV"][:, j * 128:(j + 1) * 128], b["IDB"][:, :])
                return t.drain()
            em.op("tensor", ftv, rd=("VV", "consts"), wr=("RSB",))

            def fev2(ve, j0=j0):
                pb = b["RSB"].ap().bitcast(BF16)
                last = None
                for k in range(4):
                    j = j0 + k
                    last = ve.tensor_copy(v["VV"][:, j * 128:(j + 1) * 128],
                                          pb[:, 128 * k:128 * (k + 1)])
                return last
            em.op("vector", fev2, rd=("RSB",), wr=("VV", "RSB"))

    def _qkv_load_chunk(self, tc, par):
        c, b, v, d, em = self.c, self.b, self.v, self.d, self.em
        m = (tc * QCH) // c.TOK
        lt = (tc * QCH) % c.TOK

        def fld(s, par=par, m=m, lt=lt):
            src = d["ag_out"][m * c.D:(m + 1) * c.D, lt:lt + QCH]
            return s.dma_start(
                out=v["XS"][:, 4096 * par:4096 * par + c.DCH * 512]
                .rearrange("p (cc t) -> p cc t", t=QCH),
                in_=src.rearrange("(cc p) t -> p cc t", p=128))
        em.op("scalar", fld, rd=("ag_out",), wr=(f"XS{par}",), dma=True)

    def bounds_pair(self, lay, p):
        c, b, v, em = self.c, self.b, self.v, self.em
        SM = b["SM"]
        for t_i, (srcv, sreg) in enumerate(((v["QT"], "QT"), (v["KT"], "KT"))):
            def fsq(ve, srcv=srcv):
                return ve.tensor_tensor(v["SQ"][:, 0:c.TB], srcv[:, 0:c.TB],
                                        srcv[:, 0:c.TB], op=ALU.mult)
            em.op("vector", fsq, rd=(sreg,), wr=("SQ",))
            for ch in range(c.TB // 512):
                def fm(t, ch=ch):
                    t.matmul(b["RSB"][0:1, 0:512], b["ONES"][0:64, 0:1],
                             v["SQ"][0:64, ch * 512:(ch + 1) * 512],
                             start=True, stop=True, tile_position=(0, 0))
                    t.matmul(b["PVB"][0:1, 0:512], b["ONES"][64:128, 0:1],
                             v["SQ"][64:128, ch * 512:(ch + 1) * 512],
                             start=True, stop=True, tile_position=(64, 0))
                    return t.drain()
                em.op("tensor", fm, rd=("SQ", "consts2"), wr=("RSB", "PVB"))

                def facc(ve, ch=ch, t_i=t_i):
                    col = 96 + 2 * t_i
                    last = None
                    for q, ps in ((0, b["RSB"]), (1, b["PVB"])):
                        if ch == 0:
                            last = ve.tensor_scalar(
                                b["EPI"][0:1, 0:512], ps[0:1, 0:512], 1.0, None,
                                op0=ALU.mult, op1=ALU.max,
                                accum_out=SM[0:1, col + q:col + q + 1])
                        else:
                            last = ve.tensor_scalar(
                                b["EPI"][0:1, 0:512], ps[0:1, 0:512], 1.0,
                                SM[0:1, 110 + 2 * t_i + q:111 + 2 * t_i + q],
                                op0=ALU.mult, op1=ALU.max,
                                accum_out=SM[0:1, col + q:col + q + 1])
                        ve.drain()
                    return last
                em.op("vector", facc, rd=("RSB", "PVB", "SM"),
                      wr=("SM", "EPIs", "RSB", "PVB"))

                if ch < c.TB // 512 - 1:
                    def fcp(ve, t_i=t_i):
                        col = 96 + 2 * t_i
                        ve.tensor_copy(SM[0:1, 110 + 2 * t_i:111 + 2 * t_i],
                                       SM[0:1, col:col + 1])
                        ve.drain()
                        return ve.tensor_copy(
                            SM[0:1, 111 + 2 * t_i:112 + 2 * t_i],
                            SM[0:1, col + 1:col + 2])
                    em.op("vector", fcp, rd=("SM",), wr=("SM",))

        def fc(ve):
            ve.tensor_tensor(SM[0:1, 104:105], SM[0:1, 96:97], SM[0:1, 97:98],
                             op=ALU.max)
            ve.drain()
            ve.tensor_tensor(SM[0:1, 105:106], SM[0:1, 98:99], SM[0:1, 99:100],
                             op=ALU.max)
            ve.drain()
            return ve.tensor_tensor(SM[0:1, 106:107], SM[0:1, 104:105],
                                    SM[0:1, 105:106], op=ALU.mult)
        em.op("vector", fc, rd=("SM",), wr=("SM",))

        def fsr(s):
            return s.activation(SM[0:1, 107:108], SM[0:1, 106:107], AF.Sqrt)
        em.op("scalar", fsr, rd=("SM",), wr=("SM",))

        def fneg(ve):
            return ve.tensor_scalar(SM[0:1, 108:109], SM[0:1, 107:108],
                                    -0.125, None, op0=ALU.mult)
        em.op("vector", fneg, rd=("SM",), wr=("SM",))

        def fbc(t):
            t.matmul(b["RSB"][:, 0:1], b["ONESF"][0:1, :], SM[0:1, 108:109],
                     start=True, stop=True)
            return t.drain()
        em.op("tensor", fbc, rd=("SM", "consts2"), wr=("RSB",))

        def fcb(ve):
            return ve.tensor_copy(SM[:, 80:81], b["RSB"][:, 0:1])
        em.op("vector", fcb, rd=("RSB",), wr=("SMcb", "RSB"))

    def attention_pair(self, lay, p, causal):
        c, b, v, em = self.c, self.b, self.v, self.em
        smm, sex, spv, smsk = (self.s[x] for x in ("smm", "sex", "spv", "smsk"))
        eV = self.s["eV"]
        cb = b["SM"][:, 80:81]
        QT, KTv, VV, PT = v["QT"], v["KT"], v["VV"], v["PT"]
        PS6, PVB, RSB = b["PS6"], b["PVB"], b["RSB"]
        w_pe = em.need("QT", "KT", "VV", "SMcb") + \
            em.need_w("PVB", "RSB", "PS0", "PS1", "PS2", "PS3", "PS4", "PS5")
        w_act = em.need("SMcb") + em.need_w("PT")
        w_dve = em.need("CM")
        a0 = sex.n
        first = True
        for qc in range(c.NQC):
            kmax = c.NKT if not causal else 4 * (qc + 1)
            if "attn" in ABLATE:
                kmax = 1
            diag0 = kmax - 4 if causal else kmax + 1
            smm0, spv0, sex0, smsk0 = smm.n, spv.n, sex.n, smsk.n
            nep = eV.n

            def fpe(t, qc=qc, kmax=kmax, diag0=diag0, a0=a0, spv0=spv0,
                    sex0=sex0, smsk0=smsk0, nep=nep, first=first, causal=causal):
                if first:
                    for s, n in w_pe:
                        t.wait_ge(s.h, n)
                for kt in range(kmax + 1):
                    if kt < kmax:
                        a = a0 + kt
                        j3 = a % 3
                        if kt >= 3:
                            t.wait_ge(sex.h, sex0 + kt - 2)
                        elif sex0 > 0:
                            t.wait_ge(sex.h, sex0)
                        t.matmul(PS6[:, j3 * 512:(j3 + 1) * 512],
                                 KTv[0:64, kt * 128:(kt + 1) * 128],
                                 QT[0:64, qc * 512:(qc + 1) * 512],
                                 start=True, stop=True, tile_position=(0, 0))
                        t.matmul(PS6[:, (3 + j3) * 512:(4 + j3) * 512],
                                 KTv[64:128, kt * 128:(kt + 1) * 128],
                                 QT[64:128, qc * 512:(qc + 1) * 512],
                                 start=True, stop=True, tile_position=(64, 0)
                                 ).then_inc(smm.h, 1)
                    if kt >= 1:
                        kp = kt - 1
                        ap = a0 + kp
                        t.wait_ge(sex.h, sex0 + kp + 1)
                        if kp == 0:
                            t.wait_ge(eV.h, nep)
                        if causal and kp >= diag0:
                            t.wait_ge(smsk.h, smsk0 + (kp - diag0) + 1)
                        par = ap % PTD
                        stt = kp == 0
                        t.matmul(PVB[0:64, :], VV[:, kp * 128:kp * 128 + 64],
                                 PT[:, par * 1024:par * 1024 + 512],
                                 start=stt, stop=False, tile_position=(0, 0),
                                 skip_group_check=True)
                        t.matmul(PVB[64:128, :],
                                 VV[:, kp * 128 + 64:kp * 128 + 128],
                                 PT[:, par * 1024 + 512:par * 1024 + 1024],
                                 start=stt, stop=False, tile_position=(0, 64),
                                 skip_group_check=True)
                        t.matmul(RSB[0:1, :], b["ONES"][:, 0:1],
                                 PT[:, par * 1024:par * 1024 + 512],
                                 start=stt, stop=False, tile_position=(0, 0),
                                 skip_group_check=True)
                        t.matmul(RSB[64:65, :], b["ONES"][:, 0:1],
                                 PT[:, par * 1024 + 512:par * 1024 + 1024],
                                 start=stt, stop=False, tile_position=(0, 64),
                                 skip_group_check=True).then_inc(spv.h, 1)
            self.blk.tensor(fpe)
            smm.inc(kmax)
            spv.inc(kmax)

            def fex(s, kmax=kmax, a0=a0, smm0=smm0, spv0=spv0, first=first):
                if first:
                    for ss, n in w_act:
                        s.wait_ge(ss.h, n)
                for kt in range(kmax):
                    a = a0 + kt
                    j3, par = a % 3, a % PTD
                    s.wait_ge(smm.h, smm0 + kt + 1)
                    if kt >= 4:
                        s.wait_ge(spv.h, spv0 + kt - 3)
                    elif spv0 > 0:
                        s.wait_ge(spv.h, spv0)
                    src = PS6.ap()[:, j3 * 512:j3 * 512 + 2048].rearrange(
                        "q (a bb) -> q a bb", bb=512)[:, 0:4:3, :]
                    dst = PT[:, par * 1024:par * 1024 + 1024].rearrange(
                        "q (a bb) -> q a bb", bb=512)
                    s.activation(dst, src, AF.Exp, bias=cb, scale=0.125
                                 ).then_inc(sex.h, 1)
            self.blk.scalar(fex)
            sex.inc(kmax)

            if causal:
                def fmk(ve, kmax=kmax, diag0=diag0, a0=a0, sex0=sex0,
                        first=first):
                    if first:
                        for ss, n in w_dve:
                            ve.wait_ge(ss.h, n)
                    for j in range(4):
                        kt = diag0 + j
                        par = (a0 + kt) % PTD
                        ve.wait_ge(sex.h, sex0 + kt + 1)
                        ve.tensor_tensor(PT[:, par * 1024:par * 1024 + 512],
                                         PT[:, par * 1024:par * 1024 + 512],
                                         v["CM"][:, j * 512:(j + 1) * 512],
                                         op=ALU.mult)
                        ve.drain()
                        ve.tensor_tensor(
                            PT[:, par * 1024 + 512:par * 1024 + 1024],
                            PT[:, par * 1024 + 512:par * 1024 + 1024],
                            v["CM"][:, j * 512:(j + 1) * 512],
                            op=ALU.mult)
                        ve.drain().then_inc(smsk.h, 1)
                self.blk.vector(fmk)
                smsk.inc(4)

            npv = spv.n

            def ep1(ve, npv=npv):
                ve.wait_ge(spv.h, npv)
                ve.memset(b["EPI"][:, 0:512], 1.0)
                ve.drain()
                ve.tensor_copy(b["EPI"][0:1, 0:512], RSB[0:1, :])
                ve.tensor_copy(b["EPI"][64:65, 0:512], RSB[64:65, :])
                ve.drain()
                return ve.tensor_copy(b["EPI"][:, 2560:3072], PVB[:, :])
            em.op("vector", ep1, rd=(), wr=("EPIs", "EPIo", "RSB", "PVB"))

            def ep2(t):
                for k in range(4):
                    t.transpose(RSB[:, 128 * k:128 * (k + 1)],
                                b["EPI"][:, 128 * k:128 * (k + 1)],
                                b["IDF"][:, :])
                return t.drain()
            em.op("tensor", ep2, rd=("EPIs", "consts"), wr=("RSB",))

            def ep3(ve):
                return ve.reciprocal(b["EPI"][:, 512:1024], RSB[:, :])
            em.op("vector", ep3, rd=("RSB",), wr=("EPIr", "RSB"))

            def ep4(t):
                for k in range(4):
                    t.transpose(RSB[:, 128 * k:128 * (k + 1)],
                                b["EPI"][:, 512 + 128 * k:512 + 128 * (k + 1)],
                                b["IDF"][:, :])
                return t.drain()
            em.op("tensor", ep4, rd=("EPIr", "consts"), wr=("RSB",))

            def ep5(ve):
                return ve.tensor_copy(b["EPI"][:, 1024:1536], RSB[:, :])
            em.op("vector", ep5, rd=("RSB",), wr=("EPIrr", "RSB"))

            def ep6(t):
                t.matmul(RSB[:, :], b["ONESF"][0:1, :], b["EPI"][0:1, 1024:1536],
                         start=True, stop=True)
                return t.drain()
            em.op("tensor", ep6, rd=("EPIrr", "consts2"), wr=("RSB",))

            def ep7(ve):
                return ve.tensor_copy(b["EPI"][:, 1536:2048], RSB[:, :])
            em.op("vector", ep7, rd=("RSB",), wr=("EPIb1", "RSB"))

            def ep8(t):
                t.matmul(RSB[:, :], b["ONESF"][64:65, :],
                         b["EPI"][64:65, 1024:1536], start=True, stop=True)
                return t.drain()
            em.op("tensor", ep8, rd=("EPIrr", "consts2"), wr=("RSB",))

            def ep9(ve):
                ve.tensor_copy(b["EPI"][:, 2048:2560], RSB[:, :])
                ve.drain()
                ve.tensor_tensor(v["TPS"][0:64, 0:512],
                                 b["EPI"][0:64, 2560:3072],
                                 b["EPI"][0:64, 1536:2048], op=ALU.mult)
                ve.drain()
                return ve.tensor_tensor(v["TPS"][64:128, 0:512],
                                        b["EPI"][64:128, 2560:3072],
                                        b["EPI"][64:128, 2048:2560],
                                        op=ALU.mult)
            em.op("vector", ep9, rd=("EPIb1", "EPIb2", "EPIo"),
                  wr=("TPS", "RSB", "EPIb2"))

            m = (qc * QCH) // c.TOK
            lt = (qc * QCH) % c.TOK

            def fst(sy, m=m, lt=lt, p=p):
                return sy.dma_start(
                    out=self.d[f"st{m}"][p * 128:(p + 1) * 128, lt:lt + QCH],
                    in_=v["TPS"][:, 0:512])
            em.op("sync", fst, rd=("TPS",), wr=(f"st{m}",), dma=True)
            if p == c.PAIRS - 1 and lt + QCH == c.TOK:
                def fagm(g, m=m):
                    rg = [list(range(g0, g0 + c.GS))
                          for g0 in range(0, NCORE, c.GS)]
                    g.collective_compute(
                        "AllGather", ALU.bypass, replica_groups=rg,
                        ins=[self.d[f"st{m}"]],
                        outs=[self.d["ao_all"][m * c.GS * c.SR:(m + 1) * c.GS * c.SR, :]])
                em.op("gpsimd", fagm, rd=(f"st{m}",), wr=(f"aoall{m}",),
                      cc=True)
            a0 += kmax
            first = False
        em.note_write("PT", sex, sex.n)
        for bank in range(6):
            em.note_write(f"PS{bank}", smm, smm.n)
            em.note_read(f"PS{bank}", sex, sex.n)
        em.note_read("QT", smm, smm.n)
        em.note_read("KT", smm, smm.n)
        em.note_read("VV", spv, spv.n)
        em.note_read("SMcb", sex, sex.n)

    def proj_and_residual(self, lay):
        c, b, v, d, em = self.c, self.b, self.v, self.d, self.em
        do_lora = lay == "o"

        def fidx(ve):
            for j in range(c.DCH):
                ve.tensor_scalar(b["SM"][:, 208 + j:209 + j], b["IOTS"][:, :],
                                 1.0, float(128 * j), op0=ALU.mult, op1=ALU.add)
            ve.drain()
            for j in range(c.DCH):
                ve.tensor_tensor(b["SM"][:, 208 + j:209 + j],
                                 b["SM"][:, 208 + j:209 + j], b["MBS"][:, :],
                                 op=ALU.add)
            ve.drain()
            return ve.tensor_copy(b["EIDX"][:, 16:16 + c.DCH],
                                  b["SM"][:, 208:208 + c.DCH])
        em.op("vector", fidx, rd=("consts",), wr=("SM", "EIDX"))

        def fgat(g):
            last = None
            for j in range(c.DCH):
                last = g.indirect_dma_start(
                    out=v["PIN"][:, j * c.TOK:(j + 1) * c.TOK], out_offset=None,
                    in_=d["ao_all"],
                    in_offset=IndirectOffsetOnAxis(ap=b["EIDX"][:, 16 + j:17 + j],
                                                   axis=0))
            return last
        em.op("gpsimd", fgat, rd=("ao_all", "aoall0", "aoall1", "aoall2", "aoall3", "EIDX"), wr=("PIN",), dma=True)

        def fwp(g, lay=lay):
            last = g.dma_start(out=v["WP"][:, 0:c.DCH * c.D], in_=d[f"{lay}_wp"])
            if lay == "o":
                g.dma_start(out=v["LB"][:, 1024:1024 + c.DCH * 8], in_=d["o_Ap"])
                last = g.dma_start(out=v["LB"][0:8, 0:c.D], in_=d["o_Bp"])
            return last
        em.op("gpsimd", fwp, wr=("WP", "LB"), dma=True)

        if do_lora:
            for ch in range(c.TOK // 512):
                def fha(t, ch=ch):
                    for cch in range(c.DCH):
                        t.matmul(b["PS6"][0:8, 2560:3072],
                                 v["LB"][:, 1024 + cch * 8:1024 + (cch + 1) * 8],
                                 v["PIN"][:, cch * c.TOK + ch * 512:cch * c.TOK + (ch + 1) * 512],
                                 start=(cch == 0), stop=(cch == c.DCH - 1))
                    return t.drain()
                em.op("tensor", fha, rd=("PIN", "LB"), wr=("PS5",))

                def fhe(s, ch=ch):
                    return s.activation(
                        b["LHAP"][0:8, ch * 512:(ch + 1) * 512],
                        b["PS6"][0:8, 2560:3072], AF.Copy, scale=2.0)
                em.op("scalar", fhe, rd=("PS5",), wr=("LHAP", "PS5"))

        for co in range(c.DCH):
            for ch in range(c.TOK // 512):
                bank = (co * (c.TOK // 512) + ch) % 2

                def fpj(t, co=co, ch=ch, bank=bank):
                    for cch in range(c.DCH):
                        t.matmul(
                            b["PS6"][:, bank * 512:bank * 512 + 512],
                            v["WP"][:, (cch * c.DCH + co) * 128:(cch * c.DCH + co + 1) * 128],
                            v["PIN"][:, cch * c.TOK + ch * 512:cch * c.TOK + (ch + 1) * 512],
                            start=(cch == 0),
                            stop=(not do_lora and cch == c.DCH - 1))
                    if do_lora:
                        t.matmul(b["PS6"][:, bank * 512:bank * 512 + 512],
                                 v["LB"][0:8, co * 128:(co + 1) * 128],
                                 b["LHAP"][0:8, ch * 512:(ch + 1) * 512],
                                 start=False, stop=True)
                    return t.drain()
                rdl = ("PIN", "WP", "LB", "LHAP") if do_lora else ("PIN", "WP")
                em.op("tensor", fpj, rd=rdl, wr=(f"PS{bank}",))

                def fze(s, co=co, bank=bank, lay=lay):
                    if lay == "o":
                        return s.activation(
                            v["TPS"][:, 0:512],
                            b["PS6"][:, bank * 512:bank * 512 + 512],
                            AF.Identity, bias=b["BPS"][:, co:co + 1])
                    return s.copy(v["TPS"][:, 0:512],
                                  b["PS6"][:, bank * 512:bank * 512 + 512])
                em.op("scalar", fze, rd=(f"PS{bank}", "consts"),
                      wr=("TPS", f"PS{bank}"))

                def fzt(t):
                    pb = b["RSB"].ap().bitcast(BF16)
                    for k in range(4):
                        t.transpose(pb[:, 128 * k:128 * (k + 1)],
                                    v["TPS"][:, 128 * k:128 * (k + 1)],
                                    b["IDB"][:, :])
                    return t.drain()
                em.op("tensor", fzt, rd=("TPS", "consts"), wr=("RSB",))

                def fzv(ve, co=co, ch=ch, lay=lay):
                    pb = b["RSB"].ap().bitcast(BF16)
                    last = None
                    for k in range(4):
                        ti = ch * 4 + k
                        dst = v["H2"][:, ti * c.D + co * 128:ti * c.D + (co + 1) * 128]
                        if lay == "o":
                            last = ve.tensor_copy(dst, pb[:, 128 * k:128 * (k + 1)])
                        else:
                            last = ve.tensor_tensor(
                                dst, dst, pb[:, 128 * k:128 * (k + 1)],
                                op=ALU.add)
                        ve.drain()
                    return last
                em.op("vector", fzv, rd=("RSB", "H2"), wr=("H2", "RSB"))

        if lay == "o":
            for i in range(c.TT):
                mb_, reg = self.ln_stats(v["H2"][:, i * c.D:(i + 1) * c.D],
                                         ("H2",), slot=i % 2)

                def fadd(ve, i=i, mb_=mb_):
                    sm = b["SM"]
                    ve.tensor_scalar(b["ROUT"][:, 0:c.D],
                                     v["H2"][:, i * c.D:(i + 1) * c.D],
                                     sm[:, mb_:mb_ + 1], sm[:, mb_ + 3:mb_ + 4],
                                     op0=ALU.subtract, op1=ALU.mult)
                    ve.drain()
                    return ve.tensor_tensor(b["R"][:, i * c.D:(i + 1) * c.D],
                                            b["R"][:, i * c.D:(i + 1) * c.D],
                                            b["ROUT"][:, 0:c.D], op=ALU.add)
                em.op("vector", fadd, rd=("H2", reg), wr=("R", "ROUT"))

    def h2_u_stage(self):
        c, b, v, em = self.c, self.b, self.v, self.em

        def fgb(g):
            g.dma_start(out=v["LB"][0:1, 0:c.D],
                        in_=self.d["i_g1"].rearrange("(a x) -> a x", a=1))
            return g.dma_start(out=v["LB"][0:1, c.D:2 * c.D],
                               in_=self.d["i_b1"].rearrange("(a x) -> a x", a=1))
        em.op("gpsimd", fgb, wr=("LB",), dma=True)

        def fob(g):
            return g.memset(b["ONEBR"][0:1, :], 1.0)
        em.op("gpsimd", fob, wr=("CSL0",))
        for j in range(2 * c.DCH):
            half = j // c.DCH
            col = (j % c.DCH) * 128

            def fb2(t, half=half, col=col):
                t.matmul(b["PS6"][:, 0:128], b["ONEBR"][0:1, :],
                         v["LB"][0:1, half * c.D + col:half * c.D + col + 128],
                         start=True, stop=True)
                return t.drain()
            em.op("tensor", fb2, rd=("LB", "CSL0"), wr=("PS0",))

            def fe2(ve, j=j):
                return ve.tensor_copy(v["GBB"][:, j * 128:(j + 1) * 128],
                                      b["PS6"][:, 0:128])
            em.op("vector", fe2, rd=("PS0",), wr=("GBB", "PS0"))
        for i in range(c.TT):
            mb_, reg = self.ln_stats(b["R"][:, i * c.D:(i + 1) * c.D], ("R",),
                                     slot=i % 2)

            def fa(ve, i=i, mb_=mb_, reg=reg):
                sm = b["SM"]
                ve.tensor_scalar(v["H2"][:, i * c.D:(i + 1) * c.D],
                                 b["R"][:, i * c.D:(i + 1) * c.D],
                                 sm[:, mb_:mb_ + 1], sm[:, mb_ + 3:mb_ + 4],
                                 op0=ALU.subtract, op1=ALU.mult)
                ve.drain()
                ve.tensor_tensor(v["TPS"][:, 0:c.D],
                                 v["H2"][:, i * c.D:(i + 1) * c.D],
                                 v["GBB"][:, 0:c.D], op=ALU.mult)
                ve.drain()
                return ve.tensor_tensor(v["TPS"][:, 0:c.D], v["TPS"][:, 0:c.D],
                                        v["GBB"][:, c.D:2 * c.D],
                                        op=ALU.add)
            em.op("vector", fa, rd=("R", reg, "GBB"), wr=("H2", "TPS"))
            self.transpose_stage_tile(i, "ag_in")

    def spill_and_flat(self):
        c, b, v, d, em = self.c, self.b, self.v, self.d, self.em

        def fsp(sy):
            xr = d["sp_x1"].rearrange("(i p) dd -> i p dd", p=128)
            yr = d["sp_y"].rearrange("(i p) dd -> i p dd", p=128)
            last = None
            for i in range(c.TT):
                sy.dma_start(out=xr[i, :, :], in_=b["R"][:, i * c.D:(i + 1) * c.D])
                last = sy.dma_start(out=yr[i, :, :],
                                    in_=v["H2"][:, i * c.D:(i + 1) * c.D])
            return last
        em.op("sync", fsp, rd=("R", "H2"), wr=("sp_x1", "sp_y"), dma=True)

        for i in range(c.TT):
            mb_, reg = self.ln_stats(v["H2"][:, i * c.D:(i + 1) * c.D], ("H2",),
                                     slot=i % 2)

            def fa(ve, i=i, mb_=mb_):
                sm = b["SM"]
                ve.tensor_scalar(b["R"][:, i * c.D:(i + 1) * c.D],
                                 v["H2"][:, i * c.D:(i + 1) * c.D],
                                 sm[:, mb_:mb_ + 1], sm[:, mb_ + 3:mb_ + 4],
                                 op0=ALU.subtract, op1=ALU.mult)
                ve.drain()
                ve.tensor_tensor(b["R"][:, i * c.D:(i + 1) * c.D],
                                 b["R"][:, i * c.D:(i + 1) * c.D],
                                 b["GB2"][:, 0:c.D], op=ALU.mult)
                ve.drain()
                ve.tensor_tensor(b["R"][:, i * c.D:(i + 1) * c.D],
                                 b["R"][:, i * c.D:(i + 1) * c.D],
                                 b["GB2"][:, c.D:2 * c.D], op=ALU.add)
                ve.drain()
                return ve.tensor_copy(v["TPS"][:, 0:c.D],
                                      b["R"][:, i * c.D:(i + 1) * c.D])
            em.op("vector", fa, rd=("H2", reg, "GB2", "sp_x1"),
                  wr=("R", "TPS"))

            def fdm(sy, i=i):
                return sy.dma_start(
                    out=d["flat_st"].rearrange("(a p) dd -> a p dd", p=128)[i, :, :],
                    in_=v["TPS"][:, 0:c.D])
            em.op("sync", fdm, rd=("TPS",), wr=("flat_st",), dma=True)

    def routing(self):
        c, b, v, d, em = self.c, self.b, self.v, self.d, self.em

        def fz(ve):
            return ve.memset(b["ROUT"][:, 0:c.E * c.CAP // 128], 0.0)
        em.op("vector", fz, wr=("ROUT",))

        def fzd(sy):
            return sy.dma_start(
                out=d["tmap_in"].rearrange("(p f) o -> p (f o)", p=128),
                in_=b["ROUT"][:, 0:c.E * c.CAP // 128])
        em.op("sync", fzd, rd=("ROUT",), wr=("tmap_in",), dma=True)

        for i in range(c.TT):
            for g0 in range(0, c.DCH, 4):
                def ftt(t, i=i, g0=g0):
                    for k in range(4):
                        cch = g0 + k
                        t.transpose(
                            b["RSB"][:, 128 * k:128 * (k + 1)],
                            b["R"][:, i * c.D + cch * 128:i * c.D + (cch + 1) * 128],
                            b["IDF"][:, :])
                    return t.drain()
                em.op("tensor", ftt, rd=("R", "consts"), wr=("RSB",))

                def fte(ve, g0=g0):
                    return ve.tensor_copy(b["ROUT"][:, g0 * 128:(g0 + 4) * 128],
                                          b["RSB"][:, :])
                em.op("vector", fte, rd=("RSB",), wr=("ROUT", "RSB"))

            def flg(t):
                for cch in range(c.DCH):
                    t.matmul(b["PS6"][0:8, 2560:2688],
                             b["WG"][:, cch * 8:(cch + 1) * 8],
                             b["ROUT"][:, cch * 128:(cch + 1) * 128],
                             start=(cch == 0), stop=(cch == c.DCH - 1))
                return t.drain()
            em.op("tensor", flg, rd=("ROUT", "consts"), wr=("PS5",))

            def fle(ve):
                return ve.tensor_copy(b["ROUT"][0:8, 2048:2176],
                                      b["PS6"][0:8, 2560:2688])
            em.op("vector", fle, rd=("PS5",), wr=("ROUT", "PS5"))

            def flt(t):
                t.transpose(b["RSB"][:, 0:8], b["ROUT"][0:8, 2048:2176],
                            b["IDF"][0:8, 0:8])
                return t.drain()
            em.op("tensor", flt, rd=("ROUT", "consts"), wr=("RSB",))

            def fsm(ve):
                L = b["ROUT"][:, 2304:2312]
                ve.tensor_copy(L, b["RSB"][:, 0:8])
                ve.drain()
                ve.tensor_reduce(b["ROUT"][:, 2312:2313], L, axis=AXX, op=ALU.max)
                ve.drain()
                return ve.tensor_scalar(b["ROUT"][:, 2313:2314],
                                        b["ROUT"][:, 2312:2313], -1.0, None,
                                        op0=ALU.mult)
            em.op("vector", fsm, rd=("RSB",), wr=("ROUT", "RSB"))

            def fex(s):
                return s.activation(b["ROUT"][:, 2320:2328],
                                    b["ROUT"][:, 2304:2312], AF.Exp,
                                    bias=b["ROUT"][:, 2313:2314])
            em.op("scalar", fex, rd=("ROUT",), wr=("ROUT",))

            def ftop(ve, i=i):
                E8 = b["ROUT"][:, 2320:2328]
                ve.tensor_reduce(b["ROUT"][:, 2328:2329], E8, axis=AXX,
                                 op=ALU.add)
                ve.drain()
                ve.reciprocal(b["ROUT"][:, 2329:2330], b["ROUT"][:, 2328:2329])
                ve.drain()
                G = b["GTS"][:, 14 * i:14 * i + 8]
                ve.tensor_scalar(G, E8, b["ROUT"][:, 2329:2330], None,
                                 op0=ALU.mult)
                ve.drain()
                ve.max(b["ROUT"][:, 2330:2338], G)
                ve.drain()
                ve.max_index(b["EIDX"][:, 8:16], b["ROUT"][:, 2330:2338], G)
                ve.drain()
                ve.tensor_copy(b["ROUT"][:, 2338:2346], b["EIDX"][:, 8:16])
                ve.drain()
                ve.tensor_copy(b["GTS"][:, 14 * i + 8:14 * i + 10],
                               b["ROUT"][:, 2338:2340])
                ve.tensor_copy(b["GTS"][:, 14 * i + 10:14 * i + 12],
                               b["ROUT"][:, 2330:2332])
                ve.drain()
                ve.tensor_scalar(b["ROUT"][:, 2346:2354], b["IO8"][:, :],
                                 b["GTS"][:, 14 * i + 8:14 * i + 9], None,
                                 op0=ALU.is_equal)
                ve.drain()
                ve.tensor_scalar(b["ROUT"][:, 2354:2362], b["IO8"][:, :],
                                 b["GTS"][:, 14 * i + 9:14 * i + 10], None,
                                 op0=ALU.is_equal)
                ve.drain()
                ve.tensor_tensor(b["ROUT"][:, 2362:2370],
                                 b["ROUT"][:, 2346:2354],
                                 b["ROUT"][:, 2354:2362], op=ALU.add)
                ve.drain()
                return ve.tensor_copy(v["TPS"][:, 0:8], b["ROUT"][:, 2362:2370])
            em.op("vector", ftop, rd=("ROUT", "consts"),
                  wr=("ROUT", "GTS", "TPS", "EIDX"))

            def fmt(t):
                pb = b["RSB"].ap().bitcast(BF16)
                t.transpose(pb[0:8, 0:128], v["TPS"][:, 0:8], b["IDB"][:, :])
                return t.drain()
            em.op("tensor", fmt, rd=("TPS", "consts"), wr=("RSB",))

            def fmte(ve, i=i):
                pb = b["RSB"].ap().bitcast(BF16)
                return ve.tensor_copy(b["MKT"][0:8, i * 128:(i + 1) * 128],
                                      pb[0:8, 0:128])
            em.op("vector", fmte, rd=("RSB",), wr=("MKT", "RSB"))

        def fscan(ve):
            ve.tensor_tensor_scan(b["CUM"][0:8, :], b["MKT"][0:8, :],
                                  b["MKT"][0:8, :], 0.0, op0=ALU.add,
                                  op1=ALU.bypass)
            ve.drain()
            return ve.tensor_copy(b["SM"][0:8, 120:121],
                                  b["CUM"][0:8, c.TOK - 1:c.TOK])
        em.op("vector", fscan, rd=("MKT",), wr=("CUM", "SM"))

        def fcdm(sy):
            return sy.dma_start(out=d["cnt_in"].rearrange("o e -> e o"),
                                in_=b["SM"][0:8, 120:121],
                                allow_slow_non_contiguous=True)
        em.op("sync", fcdm, rd=("SM",), wr=("cnt_in",), dma=True)
        self.allgather_global("cnt_in", "cnt_all")

        def fcld(sy):
            return sy.dma_start(out=b["CNT"][:, :], in_=d["cnt_all"])
        em.op("sync", fcld, rd=("cnt_all",), wr=("CNT",), dma=True)

        def fbase(t):
            t.matmul(b["RSB"][0:1, 0:8], b["PFXS"][0:8, 0:1], b["CNT"][0:8, :],
                     start=True, stop=True)
            return t.drain()
        em.op("tensor", fbase, rd=("CNT",), wr=("RSB",))

        def fbse(ve):
            return ve.tensor_copy(b["SM"][0:1, 128:136], b["RSB"][0:1, 0:8])
        em.op("vector", fbse, rd=("RSB",), wr=("SM", "RSB"))

        def fbbc(t):
            t.matmul(b["RSB"][:, 0:8], b["ONESF"][0:1, :],
                     b["SM"][0:1, 128:136], start=True, stop=True)
            return t.drain()
        em.op("tensor", fbbc, rd=("SM", "consts2"), wr=("RSB",))

        def fbm1(ve):
            return ve.tensor_scalar(b["SM"][:, 136:144], b["RSB"][:, 0:8],
                                    -1.0, None, op0=ALU.add)
        em.op("vector", fbm1, rd=("RSB",), wr=("SM", "RSB"))

        for i in range(c.TT):
            def fcb(t, i=i):
                t.transpose(b["RSB"][:, 0:8],
                            b["CUM"][0:8, i * 128:(i + 1) * 128],
                            b["IDF"][0:8, 0:8])
                return t.drain()
            em.op("tensor", fcb, rd=("CUM", "consts"), wr=("RSB",))

            def fpos(ve, i=i):
                P = b["ROUT"][:, 2304:2312]
                ve.tensor_tensor(P, b["RSB"][:, 0:8], b["SM"][:, 136:144],
                                 op=ALU.add)
                ve.drain()
                last = None
                for rr in range(2):
                    oh = b["ROUT"][:, 2346 + 8 * rr:2354 + 8 * rr]
                    ve.tensor_scalar(oh, b["IO8"][:, :],
                                     b["GTS"][:, 14 * i + 8 + rr:14 * i + 9 + rr],
                                     None, op0=ALU.is_equal)
                    ve.drain()
                    ve.tensor_tensor(b["ROUT"][:, 2376:2384], P, oh, op=ALU.mult)
                    ve.drain()
                    ve.tensor_reduce(b["ROUT"][:, 2384 + rr:2385 + rr],
                                     b["ROUT"][:, 2376:2384], axis=AXX,
                                     op=ALU.add)
                    ve.drain()
                for rr in range(2):
                    pr = b["ROUT"][:, 2384 + rr:2385 + rr]
                    kp = b["ROUT"][:, 2386 + rr:2387 + rr]
                    ve.tensor_scalar(kp, pr, float(c.CAP - 1), None,
                                     op0=ALU.is_le)
                    ve.drain()
                    ve.tensor_tensor(
                        b["GTS"][:, 14 * i + 12 + rr:14 * i + 13 + rr],
                        b["GTS"][:, 14 * i + 10 + rr:14 * i + 11 + rr],
                        kp, op=ALU.mult)
                    ve.drain()
                    dv = b["ROUT"][:, 2388 + rr:2389 + rr]
                    ve.tensor_scalar(dv,
                                     b["GTS"][:, 14 * i + 8 + rr:14 * i + 9 + rr],
                                     float(c.CAP), None, op0=ALU.mult)
                    ve.drain()
                    ve.tensor_tensor(dv, dv, pr, op=ALU.add)
                    ve.drain()
                    ve.tensor_scalar(b["ROUT"][:, 2390:2391], kp, -3.0e7, 3.0e7,
                                     op0=ALU.mult, op1=ALU.add)
                    ve.drain()
                    ve.tensor_tensor(dv, dv, b["ROUT"][:, 2390:2391],
                                     op=ALU.add)
                    ve.drain()
                    ve.tensor_copy(b["EIDX"][:, 20 + rr:21 + rr], dv)
                    ve.drain()
                    mn = b["ROUT"][:, 2391:2392]
                    ve.tensor_scalar(mn, pr, float(c.CAP - 1), None,
                                     op0=ALU.min)
                    ve.drain()
                    ve.tensor_scalar(b["ROUT"][:, 2393:2394], mn,
                                     float(c.CAP // 2), None, op0=ALU.is_ge)
                    ve.drain()
                    ve.tensor_scalar(b["ROUT"][:, 2393:2394],
                                     b["ROUT"][:, 2393:2394],
                                     float((c.E - 1) * (c.CAP // 2)), None,
                                     op0=ALU.mult)
                    ve.drain()
                    ve.tensor_scalar(b["ROUT"][:, 2392:2393],
                                     b["GTS"][:, 14 * i + 8 + rr:14 * i + 9 + rr],
                                     float(c.CAP // 2), None, op0=ALU.mult)
                    ve.drain()
                    ve.tensor_tensor(mn, mn, b["ROUT"][:, 2392:2393],
                                     op=ALU.add)
                    ve.drain()
                    ve.tensor_tensor(mn, mn, b["ROUT"][:, 2393:2394],
                                     op=ALU.add)
                    ve.drain()
                    last = ve.tensor_copy(b["CIX"][:, 2 * i + rr:2 * i + rr + 1],
                                          mn)
                    ve.drain()
                return last
            em.op("vector", fpos, rd=("RSB", "ROUT", "GTS", "consts"),
                  wr=("ROUT", "GTS", "EIDX", "CIX", "RSB"))

            def fsc(g, i=i):
                last = None
                for rr in range(2):
                    last = g.indirect_dma_start(
                        out=d["tmap_in"],
                        out_offset=IndirectOffsetOnAxis(
                            ap=b["EIDX"][:, 20 + rr:21 + rr], axis=0),
                        in_=b["TVS"][:, i:i + 1], in_offset=None,
                        bounds_check=c.E * c.CAP - 1, oob_is_err=False)
                return last
            em.op("gpsimd", fsc, rd=("EIDX", "consts", "tmap_in"),
                  wr=("tmap_in",), dma=True)

        self.allgather_global("tmap_in", "tmap_all", op="AllReduce",
                              alu=ALU.add)
        self.dbg_dram("tmap", "tmap_all")

    def expert_ffn(self):
        c, b, v, d, em = self.c, self.b, self.v, self.d, self.em

        def fti(ve):
            ve.tensor_scalar(b["SM"][:, 216:217], b["IOTS"][:, :],
                             float(c.CAPT - 1), None, op0=ALU.min)
            ve.drain()
            ve.tensor_tensor(b["SM"][:, 216:217], b["SM"][:, 216:217],
                             b["EBD"][:, :], op=ALU.add)
            ve.drain()
            return ve.tensor_copy(b["EIDX"][:, 0:1], b["SM"][:, 216:217])
        em.op("vector", fti, rd=("consts",), wr=("SM", "EIDX"))

        def ftg(g):
            return g.indirect_dma_start(
                out=b["ROUT"][:, 2048:2176], out_offset=None,
                in_=d["tmap_all"].rearrange("(a p) o -> a (p o)", p=128),
                in_offset=IndirectOffsetOnAxis(ap=b["EIDX"][:, 0:1], axis=0))
        em.op("gpsimd", ftg, rd=("tmap_all", "EIDX"), wr=("ROUT",), dma=True)

        def ftt(t):
            t.transpose(b["RSB"][:, 0:c.CAPT], b["ROUT"][0:c.CAPT, 2048:2176],
                        b["IDF"][0:c.CAPT, 0:c.CAPT])
            return t.drain()
        em.op("tensor", ftt, rd=("ROUT", "consts"), wr=("RSB",))

        def fte(ve):
            ve.tensor_copy(b["ROUT"][:, 0:c.CAPT], b["RSB"][:, 0:c.CAPT])
            ve.drain()
            return ve.tensor_copy(b["EIDX"][:, 0:c.CAPT],
                                  b["ROUT"][:, 0:c.CAPT])
        em.op("vector", fte, rd=("RSB",), wr=("ROUT", "EIDX", "RSB"))

        for j in range(1 if "gather" in ABLATE else c.CAPT):
            gb = (v["EIG"], "EIG") if j % 2 == 0 else (v["MO1"], "MO1")

            def fgi(g, j=j, gb=gb):
                return g.indirect_dma_start(
                    out=gb[0][:, 0:c.D], out_offset=None, in_=d["flat_all"],
                    in_offset=IndirectOffsetOnAxis(ap=b["EIDX"][:, j:j + 1],
                                                   axis=0))
            em.op("gpsimd", fgi, rd=("flat_all", "EIDX"), wr=(gb[1],),
                  dma=True)

            def fei(sy, j=j, gb=gb):
                return sy.dma_start(out=d["ei_tok"][j * 128:(j + 1) * 128, :],
                                    in_=gb[0][:, 0:c.D])
            em.op("sync", fei, rd=(gb[1],), wr=("ei_tok",), dma=True)

        for cch in range(c.DCH):
            def ftx(sy, cch=cch):
                return sy.dma_start(
                    out=v["EIT"][:, cch * c.CAP:(cch + 1) * c.CAP],
                    in_=d["ei_tok"][:, cch * 128:(cch + 1) * 128],
                    transpose=True)
            em.op("sync", ftx, rd=("ei_tok",), wr=("EIT",), dma=True)

        for s_i in range(1 if "moe" in ABLATE else c.NSW):
            r0 = s_i * 512
            for ft in range(c.FT):
                par = ft % 2

                def fw1(g, ft=ft, par=par):
                    w = c.DCH * 128
                    return g.dma_start(
                        out=v["W1B"][:, par * w:(par + 1) * w],
                        in_=d["w1e"][ft, :, :])
                em.op("gpsimd", fw1, wr=(f"W1B{par}",), dma=True)

                def fmm(t, par=par, r0=r0):
                    w = c.DCH * 128
                    for cch in range(c.DCH):
                        t.matmul(
                            b["PS6"][:, par * 512:par * 512 + 512],
                            v["W1B"][:, par * w + cch * 128:par * w + (cch + 1) * 128],
                            v["EIT"][:, cch * c.CAP + r0:cch * c.CAP + r0 + 512],
                            start=(cch == 0), stop=(cch == c.DCH - 1))
                    return t.drain()
                em.op("tensor", fmm, rd=(f"W1B{par}", "EIT"), wr=(f"PS{par}",))

                def fgl(s, ft=ft, par=par):
                    return s.activation(v["MID"][:, ft * 512:(ft + 1) * 512],
                                        b["PS6"][:, par * 512:par * 512 + 512],
                                        AF.Gelu, bias=b["B1S"][:, ft:ft + 1])
                em.op("scalar", fgl, rd=(f"PS{par}", "consts"),
                      wr=("MID", f"PS{par}"))

            for dt in range(c.DCH):
                par = dt % 2
                w2v, w2r = ((v["W2B"], "W2B") if par == 0 else
                            (v["W2B2"], "W2B2"))

                def fw2(g, dt=dt, w2v=w2v):
                    return g.dma_start(out=w2v[:, 0:c.FT * 128],
                                       in_=d["w2e"][dt, :, :])
                em.op("gpsimd", fw2, wr=(w2r,), dma=True)

                def fm2(t, par=par, w2v=w2v):
                    for fc in range(c.FT):
                        t.matmul(b["PS6"][:, (2 + par) * 512:(3 + par) * 512],
                                 w2v[:, fc * 128:(fc + 1) * 128],
                                 v["MID"][:, fc * 512:(fc + 1) * 512],
                                 start=(fc == 0), stop=(fc == c.FT - 1))
                    return t.drain()
                em.op("tensor", fm2, rd=(w2r, "MID"), wr=(f"PS{2 + par}",))

                def fo(s, dt=dt, par=par):
                    return s.activation(
                        v["EIG"][:, 0:512],
                        b["PS6"][:, (2 + par) * 512:(3 + par) * 512],
                        AF.Identity, bias=b["B2S"][:, dt:dt + 1])
                em.op("scalar", fo, rd=(f"PS{2 + par}", "consts"),
                      wr=("EIG", f"PS{2 + par}"))

                def ftr2(t, par=par):
                    pbank = b["RSB"] if par == 0 else b["PVB"]
                    pb = pbank.ap().bitcast(BF16)
                    for k in range(4):
                        t.transpose(pb[:, 128 * k:128 * (k + 1)],
                                    v["EIG"][:, 128 * k:128 * (k + 1)],
                                    b["IDB"][:, :])
                    return t.drain()
                em.op("tensor", ftr2, rd=("EIG", "consts"),
                      wr=("RSB" if par == 0 else "PVB",))

                def fcv2(ve, par=par):
                    pbank = b["RSB"] if par == 0 else b["PVB"]
                    pb = pbank.ap().bitcast(BF16)
                    return ve.tensor_copy(v["EIG"][:, 512:1024], pb[:, 0:512])
                em.op("vector", fcv2, rd=("RSB" if par == 0 else "PVB",),
                      wr=("EIG", "RSB" if par == 0 else "PVB"))

                def fst2(sy, dt=dt, r0=r0):
                    return sy.dma_start(
                        out=d["eo_st"][r0:r0 + 512, dt * 128:(dt + 1) * 128]
                        .rearrange("(k p) e -> p k e", p=128),
                        in_=v["EIG"][:, 512:1024]
                        .rearrange("p (k e) -> p k e", e=128))
                em.op("sync", fst2, rd=("EIG",),
                      wr=("eo_stA" if r0 < c.CAP // 2 else "eo_stB",),
                      dma=True)

            if s_i == c.NSW // 2 - 1:
                def fag1(g):
                    g.collective_compute(
                        "AllGather", ALU.bypass,
                        replica_groups=[list(range(NCORE))],
                        ins=[self.d["eo_st"][0:c.CAP // 2, :]],
                        outs=[self.d["eo_all"][0:c.E * c.CAP // 2, :]])
                em.op("gpsimd", fag1, rd=("eo_stA",), wr=("eo_allA",), cc=True)

        def fag2(g):
            g.collective_compute(
                "AllGather", ALU.bypass, replica_groups=[list(range(NCORE))],
                ins=[self.d["eo_st"][c.CAP // 2:c.CAP, :]],
                outs=[self.d["eo_all"][c.E * c.CAP // 2:c.E * c.CAP, :]])
        em.op("gpsimd", fag2, rd=("eo_stB",), wr=("eo_allB",), cc=True)

    def combine_final(self):
        c, b, v, d, em = self.c, self.b, self.v, self.d, self.em
        for i in range(c.TT):
            def fr(sy, i=i):
                xr = d["sp_x1"].rearrange("(a p) dd -> a p dd", p=128)
                yr = d["sp_y"].rearrange("(a p) dd -> a p dd", p=128)
                sy.dma_start(out=b["R"][:, i * c.D:(i + 1) * c.D],
                             in_=xr[i, :, :])
                return sy.dma_start(out=v["MID"][:, 0:c.D], in_=yr[i, :, :])
            em.op("sync", fr, rd=("sp_x1", "sp_y"), wr=("R", "MID"), dma=True)

            def fg0(g, i=i):
                g.indirect_dma_start(
                    out=v["EIG"][:, 0:c.D], out_offset=None, in_=d["eo_all"],
                    in_offset=IndirectOffsetOnAxis(
                        ap=b["CIX"][:, 2 * i:2 * i + 1], axis=0))
                return g.indirect_dma_start(
                    out=v["MO1"][:, 0:c.D], out_offset=None, in_=d["eo_all"],
                    in_offset=IndirectOffsetOnAxis(
                        ap=b["CIX"][:, 2 * i + 1:2 * i + 2], axis=0))
            em.op("gpsimd", fg0, rd=("eo_allA", "eo_allB", "CIX"),
                  wr=("EIG", "MO1"), dma=True)

            def fmo(ve, i=i):
                ve.tensor_scalar(b["ROUT"][:, 0:c.D], v["EIG"][:, 0:c.D],
                                 b["GTS"][:, 14 * i + 12:14 * i + 13], None,
                                 op0=ALU.mult)
                ve.drain()
                ve.scalar_tensor_tensor(b["ROUT"][:, 1024:1024 + c.D],
                                        v["MO1"][:, 0:c.D],
                                        b["GTS"][:, 14 * i + 13:14 * i + 14],
                                        b["ROUT"][:, 0:c.D],
                                        op0=ALU.mult, op1=ALU.add)
                ve.drain()
                return ve.tensor_tensor(b["ROUT"][:, 0:c.D],
                                        b["ROUT"][:, 1024:1024 + c.D],
                                        v["MID"][:, 0:c.D], op=ALU.add)
            em.op("vector", fmo, rd=("EIG", "MO1", "MID", "GTS"), wr=("ROUT",))

            mb_, reg = self.ln_stats(b["ROUT"][:, 0:c.D], ("ROUT",),
                                     slot=i % 2)

            def fo(ve, i=i, mb_=mb_, reg=reg):
                sm = b["SM"]
                ve.tensor_scalar(b["ROUT"][:, 1024:1024 + c.D],
                                 b["ROUT"][:, 0:c.D], sm[:, mb_:mb_ + 1],
                                 sm[:, mb_ + 3:mb_ + 4],
                                 op0=ALU.subtract, op1=ALU.mult)
                ve.drain()
                return ve.tensor_tensor(b["ROUT"][:, 1024:1024 + c.D],
                                        b["ROUT"][:, 1024:1024 + c.D],
                                        b["R"][:, i * c.D:(i + 1) * c.D],
                                        op=ALU.add)
            em.op("vector", fo, rd=("ROUT", reg, "R"), wr=("ROUT",))

            def fdo(sy, i=i):
                return sy.dma_start(
                    out=d["xout"].rearrange("(a p) dd -> a p dd", p=128)[i, :, :],
                    in_=b["ROUT"][:, 1024:1024 + c.D])
            em.op("sync", fdo, rd=("ROUT",), wr=("xout",), dma=True)

        pairs = self._fence_pairs()

        def fin(sy):
            for s, n in pairs:
                sy.wait_ge(s.h, n)
        self.blk.sync(fin)


def _lhsT_tiles(w, colsel):
    Din = w.shape[0]
    dch = Din // 128
    out = np.empty((128, dch * 128), np.float32)
    sub = w[:, colsel]
    for cc in range(dch):
        out[:, cc * 128:(cc + 1) * 128] = sub[cc * 128:(cc + 1) * 128, :]
    return out


def make_inputs(c, ins, r):
    D, TB = c.D, c.TB
    m = {}
    x = np.asarray(ins["L0_attn_x"], np.float32).reshape(c.N, D)
    m["xin"] = np.ascontiguousarray(x[r * c.TOK:(r + 1) * c.TOK])
    Wq = np.asarray(ins["L0_attn_W_qkv"], np.float32)
    bq = np.asarray(ins["L0_attn_b_qkv"], np.float32)
    Aq = np.asarray(ins["L0_attn_A_qkv"], np.float32)
    Bq = np.asarray(ins["L0_attn_B_qkv"], np.float32)
    Wpo = np.asarray(ins["L0_attn_W_proj"], np.float32)
    bpo = np.asarray(ins["L0_attn_b_proj"], np.float32)
    Ap = np.asarray(ins["L0_attn_A_proj"], np.float32)
    Bp = np.asarray(ins["L0_attn_B_proj"], np.float32)
    Wqkv_i = np.asarray(ins["L0_ffn_Wqkv"], np.float32)
    Wp_i = np.asarray(ins["L0_ffn_Wproj"], np.float32)

    for p in range(c.PAIRS):
        h1 = c.UN * (r % c.GS) + 2 * p
        h2 = h1 + 1
        for ki, knd in enumerate("qkv"):
            base = ki * D
            csel = np.r_[base + h1 * 64:base + h1 * 64 + 64,
                         base + h2 * 64:base + h2 * 64 + 64]
            m[f"o_w{knd}{p}"] = _lhsT_tiles(Wq, csel)
            m[f"o_b{knd}{p}"] = np.ascontiguousarray(bq[csel])
            m[f"o_B{knd}{p}"] = np.ascontiguousarray(Bq[:, csel])
            csel_i = np.r_[h1 * 192 + ki * 64:h1 * 192 + ki * 64 + 64,
                           h2 * 192 + ki * 64:h2 * 192 + ki * 64 + 64]
            m[f"i_w{knd}{p}"] = _lhsT_tiles(Wqkv_i, csel_i)
    Ao = np.empty((128, c.DCH * 8), np.float32)
    App = np.empty((128, c.DCH * 8), np.float32)
    for cc in range(c.DCH):
        Ao[:, cc * 8:(cc + 1) * 8] = Aq[cc * 128:(cc + 1) * 128, :]
        App[:, cc * 8:(cc + 1) * 8] = Ap[cc * 128:(cc + 1) * 128, :]
    m["o_A"] = Ao
    m["o_Ap"] = App
    m["o_Bp"] = np.ascontiguousarray(Bp)
    for lay, W in (("o", Wpo), ("i", Wp_i)):
        out = np.empty((128, c.DCH * D), np.float32)
        for cc in range(c.DCH):
            for co in range(c.DCH):
                out[:, (cc * c.DCH + co) * 128:(cc * c.DCH + co + 1) * 128] = \
                    W[cc * 128:(cc + 1) * 128, co * 128:(co + 1) * 128]
        m[f"{lay}_wp"] = out
    m["o_bp"] = bpo
    m["i_g1"] = np.asarray(ins["L0_ffn_ln1_g"], np.float32)
    m["i_b1"] = np.asarray(ins["L0_ffn_ln1_b"], np.float32)
    m["i_g2"] = np.asarray(ins["L0_ffn_ln2_g"], np.float32)
    m["i_b2"] = np.asarray(ins["L0_ffn_ln2_b"], np.float32)
    Wg = np.asarray(ins["L0_ffn_Wgate"], np.float32)
    wg = np.empty((128, c.DCH * 8), np.float32)
    for cc in range(c.DCH):
        wg[:, cc * 8:(cc + 1) * 8] = Wg[cc * 128:(cc + 1) * 128, :]
    m["wgate"] = wg
    W1 = np.asarray(ins["L0_ffn_W1"], np.float32)[r]
    W2 = np.asarray(ins["L0_ffn_W2"], np.float32)[r]
    w1r = np.empty((c.FT, 128, c.DCH * 128), np.float32)
    for ftl in range(c.FT):
        for cc in range(c.DCH):
            w1r[ftl, :, cc * 128:(cc + 1) * 128] = \
                W1[cc * 128:(cc + 1) * 128, ftl * 128:(ftl + 1) * 128]
    m["w1e"] = w1r
    w2r = np.empty((c.DCH, 128, c.FT * 128), np.float32)
    for dtl in range(c.DCH):
        for fc in range(c.FT):
            w2r[dtl, :, fc * 128:(fc + 1) * 128] = \
                W2[fc * 128:(fc + 1) * 128, dtl * 128:(dtl + 1) * 128]
    m["w2e"] = w2r
    m["b1e"] = np.asarray(ins["L0_ffn_b1"], np.float32)[r]
    m["b2e"] = np.asarray(ins["L0_ffn_b2"], np.float32)[r]
    m["ident_f"] = np.eye(128, dtype=np.float32)
    inv = 1.0 / 10000.0 ** (np.arange(0, HD, 2, dtype=np.float64) / HD)
    ang = np.arange(TB, dtype=np.float64)[:, None] * inv[None, :]
    cos = np.cos(ang).T.astype(np.float32)
    sin = np.sin(ang).T.astype(np.float32)
    m["cc_tab"] = np.ascontiguousarray(np.concatenate([cos] * 4, 0))
    m["ss_tab"] = np.ascontiguousarray(np.concatenate([-sin, sin, -sin, sin], 0))
    cm = np.zeros((128, 4 * QCH), np.float32)
    for j in range(4):
        kpos = j * 128 + np.arange(128)[:, None]
        qpos = np.arange(QCH)[None, :]
        cm[:, j * QCH:(j + 1) * QCH] = (kpos <= qpos).astype(np.float32)
    m["cmask"] = cm
    tv = np.empty((128, c.TT), np.float32)
    for i in range(c.TT):
        tv[:, i] = r * c.TOK + i * 128 + np.arange(128)
    m["tvals"] = tv
    m["iot"] = np.arange(128, dtype=np.float32).reshape(128, 1)
    m["io8"] = np.tile(np.arange(8, dtype=np.float32), (128, 1))
    pfx = np.zeros(8, np.float32)
    pfx[:r] = 1.0
    m["pfx"] = pfx
    m["ebd"] = np.full((128, 1), (c.CAP * r) // 128, np.float32)
    m["mb"] = np.full((128, 1), (r % c.GS) * c.GS * c.SR, np.float32)
    return m


_PROG_CACHE = {}


def kernel(**inputs):
    c = FULL
    if "full" not in _PROG_CACHE:
        _PROG_CACHE["full"] = K(c).build()
    nc = _PROG_CACHE["full"]
    in_maps = [make_inputs(c, inputs, r) for r in range(NCORE)]
    res = run_bass_kernel_spmd(nc, in_maps, core_ids=list(range(NCORE)))
    out = np.concatenate([np.asarray(res.results[r]["xout"], np.float32)
                          for r in range(NCORE)], 0)
    return out.reshape(c.B, c.TB, c.D)


# version 04 (final)
# speedup vs baseline: 1.1548x; 1.0637x over previous
import math
from contextlib import ExitStack

import numpy as np
import ml_dtypes

import concourse.bass as bass
import concourse.mybir as mybir
from concourse.bass import IndirectOffsetOnAxis
from concourse.bass_utils import run_bass_kernel_spmd

F32 = mybir.dt.float32
BF16 = mybir.dt.bfloat16
U32 = mybir.dt.uint32
AF = mybir.ActivationFunctionType
ALU = mybir.AluOpType
AXX = mybir.AxisListType.X

NCORE = 8
PTD = 4
HD = 64
ABLATE = set()
QCH = 512
EPS = 1e-5


class Cfg:
    def __init__(self, D=1024, TB=8192, H=16, E=8, FFM=4):
        self.B = 2
        self.D, self.TB, self.H, self.E = D, TB, H, E
        self.FF = FFM * D
        assert D == H * HD and E == NCORE
        self.N = self.B * TB
        self.TOK = self.N // NCORE
        self.GS = NCORE // self.B
        self.UN = self.B * H // NCORE
        self.PAIRS = self.UN // 2
        self.DCH = D // 128
        self.TT = self.TOK // 128
        self.NQC = TB // QCH
        self.NKT = TB // 128
        self.CAP = int(math.ceil(2.0 * self.N / E))
        self.FT = self.FF // 128
        self.NSW = self.CAP // 512
        self.CAPT = self.CAP // 128
        self.SR = self.PAIRS * 128
        assert self.TOK % QCH == 0 and self.CAP % 512 == 0


FULL = Cfg()
MINI = Cfg(D=512, TB=2048, H=8)


class Sem:
    def __init__(self, h):
        self.h = h
        self.n = 0

    def inc(self, k=1):
        self.n += k
        return self.n


_EPIR = {"EPIs", "EPIr", "EPIrr", "EPIb1", "EPIb2", "EPIo", "ROUT"}
ALIAS = {
    "QT": {"PIN", "EIT"}, "KT": {"PIN", "EIT"}, "VV": {"WP", "EIT"},
    "XS0": {"SQ", "EIT"}, "XS1": {"SQ", "EIT"},
    "PIN": {"QT", "KT", "EIT"}, "WP": {"VV", "EIT"},
    "SQ": {"XS0", "XS1", "EIT"},
    "EIT": {"QT", "KT", "VV", "XS0", "XS1", "PIN", "WP", "SQ"},
    "H2": {"MID", "LHA"}, "MID": {"H2", "LHA"}, "LHA": {"H2", "MID"},
    "WQKV": {"W1B"}, "W1B": {"WQKV"},
    "LB": {"EIG", "MO1"}, "EIG": {"LB"}, "MO1": {"LB"},
    "GB2": {"CUM"}, "CUM": {"GB2"},
    "SM": {"SMln0", "SMln1", "SMcb"}, "SMln0": {"SM"}, "SMln1": {"SM"},
    "SMcb": {"SM"},
    "GBB": {"PT", "LHAP", "W2B"},
    "LHAP": {"PT", "GBB", "W2B"}, "PT": {"LHAP", "GBB", "W2B", "CSL0", "CSL1", "MKT"},
    "MKT": {"CSL0", "CSL1", "W2B", "PT"}, "CSL0": {"MKT", "W2B", "PT"},
    "CSL1": {"MKT", "W2B", "PT"},
        "W2B": {"PT", "CSL0", "CSL1", "MKT", "LHAP", "GBB"},
    "W2B2": {"CM", "TPS", "TPS2"},
    "CM": {"W2B2"}, "TPS": {"W2B2"}, "TPS2": {"W2B2"},
}
for _r in _EPIR:
    ALIAS[_r] = (ALIAS.get(_r, set()) | _EPIR) - {_r}


class EM:
    ESEM = {"tensor": "eT", "scalar": "eS", "vector": "eV", "gpsimd": "eG",
            "sync": "eY"}

    def __init__(self, blk, sems, pools):
        self.blk = blk
        self.s = sems
        self.pools = pools
        self.pidx = {q: 0 for q in pools}
        self.w = {}
        self.r = {}

    def _pick(self, q):
        pool = self.pools[q]
        i = self.pidx[q]
        self.pidx[q] = (i + 1) % len(pool)
        return pool[i]

    def _expand(self, regs):
        out = set(regs)
        for reg in regs:
            out |= ALIAS.get(reg, set())
        return out

    def _waits(self, rd, wr):
        acc = {}

        def add(pair):
            if pair is None:
                return
            s, n = pair
            if acc.get(s, 0) < n:
                acc[s] = n
        for reg in rd:
            add(self.w.get(reg))
        for reg in self._expand(wr):
            add(self.w.get(reg))
            for s, n in self.r.get(reg, {}).items():
                add((s, n))
        return acc

    def op(self, eng, fn, rd=(), wr=(), dma=False, cc=False):
        if dma:
            sem = self._pick({"sync": "qsy", "scalar": "qsc",
                              "gpsimd": "qg"}[eng])
            step = 16
        elif cc:
            sem = self._pick("qcc")
            step = 1
        else:
            sem = self.s[self.ESEM[eng]]
            step = 1
        waits = dict(self._waits(rd, wr))
        if (dma or cc) and sem.n > 0:
            if waits.get(sem, 0) < sem.n:
                waits[sem] = sem.n
        cnt = [0]

        class _DmaProxy:
            def __init__(self, e):
                self._e = e

            def dma_start(self, *a, **k):
                inst = self._e.dma_start(*a, **k)
                inst.then_inc(sem.h, step)
                cnt[0] += step
                return inst

            def indirect_dma_start(self, *a, **k):
                inst = self._e.indirect_dma_start(*a, **k)
                inst.then_inc(sem.h, step)
                cnt[0] += step
                return inst

            def collective_compute(self, *a, **k):
                inst = self._e.collective_compute(*a, **k)
                inst.then_inc(sem.h, step)
                cnt[0] += step
                return inst

            def __getattr__(self, name):
                return getattr(self._e, name)

        def body(e):
            for s, n in waits.items():
                e.wait_ge(s.h, n)
            if dma or cc:
                fn(_DmaProxy(e))
            else:
                inst = fn(e)
                assert inst is not None, "op fn must return its last instruction"
                if isinstance(inst, (list, tuple)):
                    for i2 in inst:
                        i2.then_inc(sem.h, 1)
                    cnt[0] = len(inst)
                else:
                    inst.then_inc(sem.h, 1)
                    cnt[0] = 1
        getattr(self.blk, eng)(body)
        n = sem.inc(cnt[0])
        for reg in self._expand(wr):
            self.w[reg] = (sem, n)
            self.r[reg] = {}
        for reg in rd:
            self.r.setdefault(reg, {})[sem] = n
        return sem, n

    def note_write(self, reg, sem, n):
        for rg in self._expand((reg,)):
            self.w[rg] = (sem, n)
            self.r[rg] = {}

    def note_read(self, reg, sem, n):
        self.r.setdefault(reg, {})[sem] = n

    def need(self, *regs):
        acc = {}
        for reg in regs:
            p = self.w.get(reg)
            if p and acc.get(p[0], 0) < p[1]:
                acc[p[0]] = p[1]
        return list(acc.items())

    def need_w(self, *regs):
        acc = {}
        for reg in self._expand(regs):
            pairs = []
            if self.w.get(reg):
                pairs.append(self.w[reg])
            pairs += list(self.r.get(reg, {}).items())
            for s, n in pairs:
                if acc.get(s, 0) < n:
                    acc[s] = n
        return list(acc.items())


class K:
    def __init__(self, cfg=FULL, dbg=()):
        self.c = cfg
        self.dbg = set(dbg)
        self.nc = bass.Bass("TRN2", target_bir_lowering=False)
        self.d = {}
        self.b = {}
        self.s = {}
        self.dbg_keys = []

    def declare(self):
        nc, c, d = self.nc, self.c, self.d

        def inp(name, shape, dt=F32):
            d[name] = nc.dram_tensor(name, shape, dt, kind="ExternalInput").ap()

        inp("xin", [c.TOK, c.D])
        for p in range(c.PAIRS):
            for lay in ("o", "i"):
                for knd in "qkv":
                    inp(f"{lay}_w{knd}{p}", [128, c.DCH * 128])
            for knd in "qkv":
                inp(f"o_b{knd}{p}", [128])
                inp(f"o_B{knd}{p}", [8, 128])
        inp("o_A", [128, c.DCH * 8])
        for nm in ("i_g1", "i_b1", "i_g2", "i_b2", "o_bp", "b2e"):
            inp(nm, [c.D])
        inp("b1e", [c.FF])
        for lay in ("o", "i"):
            inp(f"{lay}_wp", [128, c.DCH * c.D])
        inp("o_Ap", [128, c.DCH * 8])
        inp("o_Bp", [8, c.D])
        inp("wgate", [128, c.DCH * 8])
        inp("w1e", [c.FT, 128, c.DCH * 128])
        inp("w2e", [c.DCH, 128, c.FT * 128])
        inp("ident_f", [128, 128])
        inp("cc_tab", [128, c.TB])
        inp("ss_tab", [128, c.TB])
        inp("cmask", [128, 4 * QCH])
        inp("tvals", [128, c.TT])
        inp("iot", [128, 1])
        inp("io8", [128, 8])
        inp("pfx", [8])
        inp("ebd", [128, 1])
        inp("mb", [128, 1])

        d["xout"] = nc.dram_tensor("xout", [c.TOK, c.D], F32,
                                   kind="ExternalOutput").ap()

        def internal(name, shape, dt):
            d[name] = nc.dram_tensor(name, shape, dt, kind="Internal").ap()

        internal("ag_in", [c.D, c.TOK], BF16)
        internal("ag_out", [c.GS * c.D, c.TOK], BF16)
        for m in range(c.GS):
            internal(f"st{m}", [c.SR, c.TOK], BF16)
        internal("ao_all", [c.GS * c.GS * c.SR, c.TOK], BF16)
        internal("flat_st", [c.TOK, c.D], BF16)
        internal("flat_all", [c.N, c.D], BF16)
        internal("cnt_in", [1, 8], F32)
        internal("cnt_all", [8, 8], F32)
        internal("tmap_in", [c.E * c.CAP, 1], F32)
        internal("tmap_all", [c.E * c.CAP, 1], F32)
        internal("ei_tok", [c.CAP, c.D], BF16)
        internal("eo_st", [c.CAP, c.D], BF16)
        internal("eo_all", [c.E * c.CAP, c.D], BF16)
        internal("sp_x1", [c.TOK, c.D], F32)
        internal("sp_y", [c.TOK, c.D], BF16)

    def alloc(self, st):
        nc, c, b = self.nc, self.c, self.b

        def sb(name, shape, dt):
            b[name] = st.enter_context(nc.sbuf_tensor(name, shape, dt))

        sb("R", [128, c.TT * c.D], F32)
        sb("GB2", [128, 2 * c.D], F32)
        sb("SM", [128, 256], F32)
        sb("ROUT", [128, 3072], F32)
        sb("AR1", [128, 32768], BF16)
        sb("AR2", [128, 8192], BF16)
        sb("AR3", [128, 20480], BF16)
        sb("ONES", [128, 1], BF16)
        sb("ONESF", [128, 128], F32)
        sb("IDB", [128, 128], BF16)
        sb("IDF", [128, 128], F32)
        sb("WG", [128, c.DCH * 8], F32)
        sb("EIDX", [128, max(c.CAPT, 24)], U32)
        sb("CIX", [128, 2 * c.TT], U32)
        sb("GTS", [128, 14 * c.TT], F32)
        sb("TVS", [128, c.TT], F32)
        sb("CNT", [8, 8], F32)
        sb("PFXS", [8, 1], F32)
        b["PS6"] = st.enter_context(nc.psum_tensor("PS6", [128, 3072], F32))
        b["PVB"] = st.enter_context(nc.psum_tensor("PVB", [128, 512], F32))
        b["RSB"] = st.enter_context(nc.psum_tensor("RSB", [128, 512], F32))

        A1, A2, A3 = b["AR1"], b["AR2"], b["AR3"]
        v = self.v = {}
        v["QT"] = A1[:, 0:8192]
        v["KT"] = A1[:, 8192:16384]
        v["VV"] = A1[:, 16384:24576]
        v["XS"] = A1[:, 24576:32768]
        v["PIN"] = A1[:, 0:16384]
        v["WP"] = A1[:, 16384:24576]
        v["SQ"] = A1[:, 24576:32768]
        v["EIT"] = A1[:, 0:c.DCH * c.CAP]
        v["PT"] = A2[:, 0:4096]
        v["CS"] = A2[:, 2048:4096]
        v["CM"] = A2[:, 4096:6144]
        v["GBB"] = A2[:, 0:2048]
        v["TPS"] = A2[:, 6144:8192]
        v["H2"] = A3[:, 0:16384]
        v["WQKV"] = A3[:, 16384:18432]
        v["LB"] = A3[:, 18432:20480]
        v["MID"] = A3[:, 0:16384]
        v["W1B"] = A3[:, 16384:18432]
        v["W2B"] = A2[:, 0:c.FT * 128]
        v["W2B2"] = A2[:, 4096:4096 + c.FT * 128]
        v["EIG"] = A3[:, 18432:19456]
        v["MO1"] = A3[:, 19456:20480]
        b["EPI"] = b["ROUT"][:, 0:3072]
        b["CUM"] = b["GB2"][:, 0:c.TOK]
        b["MKT"] = A2[:, 2048:2048 + c.TOK]
        b["LHA"] = A3[:, 0:c.TB]
        b["LHAP"] = A2[:, 0:c.TOK]
        b["ONEBR"] = A2[:, 2048:2176]
        b["IO8"] = b["SM"][:, 144:152]
        b["B2S"] = b["SM"][:, 152:160]
        b["BPS"] = b["SM"][:, 160:168]
        b["BQB"] = b["SM"][:, 168:174]
        b["B1S"] = b["SM"][:, 176:176 + c.FT]
        b["IOTS"] = b["SM"][:, 224:225]
        b["EBD"] = b["SM"][:, 225:226]
        b["MBS"] = b["SM"][:, 226:227]
        b["EPSC"] = b["SM"][:, 227:228]

    def build(self):
        self.declare()
        with ExitStack() as st:
            self.alloc(st)
            for nm in ("smm", "sex", "spv", "smsk",
                       "eT", "eS", "eV", "eG", "eY"):
                self.s[nm] = Sem(st.enter_context(self.nc.semaphore(nm)))
            pools = {}
            for q, npool in (("qsy", 24), ("qsc", 12), ("qg", 24), ("qcc", 8)):
                pools[q] = [Sem(st.enter_context(self.nc.semaphore(f"{q}{i}")))
                            for i in range(npool)]
            with self.nc.Block() as blk:
                self.blk = blk
                self.em = EM(blk, self.s, pools)
                self.emit()
        return self.nc

    def _fence_pairs(self):
        pairs = [(s, s.n) for s in self.s.values() if s.n]
        for pool in self.em.pools.values():
            pairs += [(s, s.n) for s in pool if s.n]
        return pairs

    def dbg_sbuf(self, name, view, regs=()):
        if name not in self.dbg:
            return
        h = self.nc.dram_tensor("dbg_" + name, [128, view.shape[-1]],
                                view.tensor.dtype, kind="ExternalOutput").ap()
        self.dbg_keys.append("dbg_" + name)
        pairs = self._fence_pairs()

        def f(sy):
            for s, n in pairs:
                sy.wait_ge(s.h, n)
            return sy.dma_start(out=h, in_=view)
        self.em.op("sync", f, rd=tuple(regs), dma=True)

    def dbg_dram(self, name, key):
        if name not in self.dbg:
            return
        src = self.d[key]
        h = self.nc.dram_tensor("dbg_" + name, list(src.tensor.shape),
                                src.tensor.dtype, kind="ExternalOutput").ap()
        self.dbg_keys.append("dbg_" + name)
        pairs = self._fence_pairs()

        def f(sy):
            for s, n in pairs:
                sy.wait_ge(s.h, n)
            return sy.dma_start(out=h, in_=src)
        self.em.op("sync", f, rd=(key,), dma=True)

    def emit(self):
        c = self.c
        self.p0_load()
        self.ln_stage_xln()
        self.allgather_group("ag_in", "ag_out")
        self.dbg_dram("xlnT", "ag_out")
        for p in range(c.PAIRS):
            self.qkv_pair("o", p)
            self.dbg_sbuf(f"o_qt{p}", self.v["QT"][:, 0:c.TB], regs=("QT",))
            self.dbg_sbuf(f"o_kt{p}", self.v["KT"][:, 0:c.TB], regs=("KT",))
            self.dbg_sbuf(f"o_vv{p}", self.v["VV"][:, 0:c.TB], regs=("VV",))
            self.bounds_pair("o", p)
            self.attention_pair("o", p, causal=False)
        self.proj_and_residual("o")
        self.dbg_sbuf("x1", self.b["R"][:, 0:c.TT * c.D], regs=("R",))
        self.h2_u_stage()
        self.allgather_group("ag_in", "ag_out")
        self.dbg_dram("uT", "ag_out")
        for p in range(c.PAIRS):
            self.qkv_pair("i", p)
            self.dbg_sbuf(f"i_qt{p}", self.v["QT"][:, 0:c.TB], regs=("QT",))
            self.bounds_pair("i", p)
            self.attention_pair("i", p, causal=True)
        self.proj_and_residual("i")
        self.dbg_sbuf("ybuf", self.v["H2"][:, 0:c.TT * c.D], regs=("H2",))
        self.spill_and_flat()
        self.allgather_global("flat_st", "flat_all")
        self.routing()
        self.expert_ffn()
        self.combine_final()

    def p0_load(self):
        c, b, v, d, em = self.c, self.b, self.v, self.d, self.em

        def fm(g):
            g.memset(b["SM"][:, :], 0.0)
            g.memset(b["ROUT"][:, :], 0.0)
            g.drain()
            g.memset(b["EPSC"][:, 0:1], float(EPS))
            g.memset(b["ONES"][:, :], 1.0)
            g.memset(b["ONEBR"][0:1, :], 1.0)
            return g.memset(b["ONESF"][:, :], 1.0)
        em.op("gpsimd", fm, wr=("consts2", "SM", "CSL0", "ROUT"))

        def f1(sy):
            xr = d["xin"].rearrange("(i p) dd -> i p dd", p=128)
            for i in range(c.TT):
                sy.dma_start(out=b["R"][:, i * c.D:(i + 1) * c.D], in_=xr[i, :, :])
            sy.dma_start(out=b["IDF"][:, :], in_=d["ident_f"])
            sy.dma_start(out=b["PFXS"][:, :],
                         in_=d["pfx"].rearrange("(a o) -> a o", o=1))
            sy.dma_start(out=b["EBD"][:, 0:1], in_=d["ebd"])
            sy.dma_start(out=b["MBS"][:, 0:1], in_=d["mb"])
            sy.dma_start(out=b["IOTS"][:, 0:1], in_=d["iot"])
            sy.dma_start(out=b["SM"][:, 144:152], in_=d["io8"])
            sy.dma_start(out=b["TVS"][:, :], in_=d["tvals"])
            sy.dma_start(out=b["B1S"][:, 0:c.FT],
                         in_=d["b1e"].rearrange("(f p) -> p f", p=128), allow_slow_non_contiguous=True)
            sy.dma_start(out=b["B2S"][:, 0:c.DCH],
                         in_=d["b2e"].rearrange("(t p) -> p t", p=128), allow_slow_non_contiguous=True)
            sy.dma_start(out=b["BPS"][:, 0:c.DCH],
                         in_=d["o_bp"].rearrange("(t p) -> p t", p=128), allow_slow_non_contiguous=True)
            sy.dma_start(out=b["WG"][:, :], in_=d["wgate"])
            for p in range(c.PAIRS):
                for ki, knd in enumerate("qkv"):
                    sy.dma_start(out=b["BQB"][:, 3 * p + ki:3 * p + ki + 1],
                                 in_=d[f"o_b{knd}{p}"].rearrange("(p o) -> p o", o=1), allow_slow_non_contiguous=True)
            sy.dma_start(out=b["ROUT"][0:1, 0:c.D],
                         in_=d["i_g2"].rearrange("(a x) -> a x", a=1))
            return sy.dma_start(out=b["ROUT"][0:1, c.D:2 * c.D],
                                in_=d["i_b2"].rearrange("(a x) -> a x", a=1))
        em.op("sync", f1, wr=("R", "consts", "ROUT", "SM"), dma=True)

        def fg(g):
            g.dma_start(out=b["IDB"][:, :], in_=d["ident_f"])
            return g.dma_start(out=v["CM"], in_=d["cmask"])
        em.op("gpsimd", fg, wr=("CM", "consts"), dma=True)

        for j in range(2 * c.DCH):
            half = j // c.DCH
            col = (j % c.DCH) * 128

            def fb(t, half=half, col=col):
                src = b["ROUT"][0:1, half * c.D + col:half * c.D + col + 128]
                return t.matmul(b["PS6"][:, 0:128], b["ONESF"][0:1, :], src,
                                start=True, stop=True)
            em.op("tensor", fb, rd=("ROUT", "consts2"), wr=("PS0",))

            def fe(ve, j=j):
                return ve.tensor_copy(b["GB2"][:, j * 128:(j + 1) * 128],
                                      b["PS6"][:, 0:128])
            em.op("vector", fe, rd=("PS0",), wr=("GB2", "PS0"))

    def ln_stats(self, src_view, rd_regs, slot=0):
        c, b, em = self.c, self.b, self.em
        base = 24 * slot
        mb_ = 48 + 8 * slot
        reg = f"SMln{slot}"

        def fv(ve):
            sm = b["SM"]
            for j in range(c.D // 512):
                ve.bn_stats(sm[:, base + 6 * j:base + 6 * j + 6],
                            src_view[:, 512 * j:512 * (j + 1)])
            ve.drain()
            return ve.bn_aggr(sm[:, mb_:mb_ + 2],
                              sm[:, base:base + 6 * (c.D // 512)])
        em.op("vector", fv, rd=rd_regs, wr=(reg,))

        def fs(s):
            sm = b["SM"]
            s.activation(sm[:, mb_ + 2:mb_ + 3], sm[:, mb_ + 1:mb_ + 2], AF.Ln,
                         bias=b["EPSC"][:, 0:1])
            s.drain()
            return s.activation(sm[:, mb_ + 3:mb_ + 4], sm[:, mb_ + 2:mb_ + 3],
                                AF.Exp, scale=-0.5)
        em.op("scalar", fs, rd=(reg,), wr=(reg,))
        return mb_, reg

    def ln_stage_xln(self):
        c, b, v, em = self.c, self.b, self.v, self.em
        for i in range(1 if "lnst" in ABLATE else c.TT):
            mb_, reg = self.ln_stats(b["R"][:, i * c.D:(i + 1) * c.D], ("R",),
                                     slot=i % 2)

            def fa(ve, i=i, mb_=mb_):
                sm = b["SM"]
                return ve.tensor_scalar(v["TPS"][:, 0:c.D],
                                        b["R"][:, i * c.D:(i + 1) * c.D],
                                        sm[:, mb_:mb_ + 1], sm[:, mb_ + 3:mb_ + 4],
                                        op0=ALU.subtract, op1=ALU.mult)
            em.op("vector", fa, rd=("R", reg), wr=("TPS",))
            self.transpose_stage_tile(i, "ag_in")

    def transpose_stage_tile(self, i, dram):
        c, b, v, em = self.c, self.b, self.v, self.em
        for g0 in range(0, c.DCH, 4):
            def ft(t, g0=g0):
                pb = b["PVB"].ap().bitcast(BF16)
                last = None
                for k in range(4):
                    cch = g0 + k
                    last = t.transpose(pb[:, 128 * k:128 * (k + 1)],
                                       v["TPS"][:, cch * 128:(cch + 1) * 128],
                                       b["IDB"][:, :])
                return last
            em.op("tensor", ft, rd=("TPS", "consts"), wr=("PVB",))

            def fe(ve):
                pb = b["PVB"].ap().bitcast(BF16)
                return ve.tensor_copy(v["TPS"][:, 1024:1536], pb[:, 0:512])
            em.op("vector", fe, rd=("PVB",), wr=("TPS2", "PVB"))

            def fd(sy, i=i, g0=g0):
                last = None
                for k in range(4):
                    cch = g0 + k
                    last = sy.dma_start(
                        out=self.d[dram][cch * 128:(cch + 1) * 128,
                                         i * 128:(i + 1) * 128],
                        in_=v["TPS"][:, 1024 + 128 * k:1024 + 128 * (k + 1)])
                return last
            em.op("sync", fd, rd=("TPS2",), wr=(dram,), dma=True)

    def allgather_group(self, src, dst):
        c, em = self.c, self.em

        def fg(g):
            rg = [list(range(g0, g0 + c.GS)) for g0 in range(0, NCORE, c.GS)]
            g.collective_compute("AllGather", ALU.bypass, replica_groups=rg,
                                 ins=[self.d[src]], outs=[self.d[dst]])
        em.op("gpsimd", fg, rd=(src,), wr=(dst,), cc=True)

    def allgather_global(self, src, dst, op="AllGather", alu=ALU.bypass):
        em = self.em

        def fg(g):
            g.collective_compute(op, alu, replica_groups=[list(range(NCORE))],
                                 ins=[self.d[src]], outs=[self.d[dst]])
        em.op("gpsimd", fg, rd=(src,), wr=(dst,), cc=True)

    def chunk_ags(self):
        c, em = self.c, self.em
        for m in range(c.GS):
            def fg(g, m=m):
                rg = [list(range(g0, g0 + c.GS))
                      for g0 in range(0, NCORE, c.GS)]
                g.collective_compute(
                    "AllGather", ALU.bypass, replica_groups=rg,
                    ins=[self.d[f"st{m}"]],
                    outs=[self.d["ao_all"][m * c.GS * c.SR:(m + 1) * c.GS * c.SR, :]])
            em.op("gpsimd", fg, rd=(f"st{m}",), wr=(f"aoall{m}",), cc=True)

    def qkv_pair(self, lay, p):
        c, b, v, d, em = self.c, self.b, self.v, self.d, self.em
        do_lora = lay == "o"
        do_hA = do_lora and p == 0

        if do_lora:
            def fwl(g, p=p):
                g.dma_start(out=v["LB"][0:8, 0:128], in_=d[f"o_Bq{p}"])
                g.dma_start(out=v["LB"][0:8, 128:256], in_=d[f"o_Bk{p}"])
                g.dma_start(out=v["LB"][0:8, 256:384], in_=d[f"o_Bv{p}"])
                return g.dma_start(out=v["LB"][:, 1024:1024 + c.DCH * 8],
                                   in_=d["o_A"])
            em.op("gpsimd", fwl, wr=("LB",), dma=True)

        if do_hA:
            for tc in range(1 if "qkv" in ABLATE else c.NQC):
                par = tc % 2
                self._qkv_load_chunk(tc, par)

                def fha(t, tc=tc, par=par):
                    last = None
                    for cch in range(c.DCH):
                        last = t.matmul(
                            b["PS6"][0:8, 2560:3072],
                            v["LB"][:, 1024 + cch * 8:1024 + (cch + 1) * 8],
                            v["XS"][:, 4096 * par + cch * 512:4096 * par + (cch + 1) * 512],
                            start=(cch == 0), stop=(cch == c.DCH - 1))
                    return last
                em.op("tensor", fha, rd=(f"XS{par}", "LB"), wr=("PS5",))

                def fhe(s, tc=tc):
                    return s.activation(b["LHA"][0:8, tc * QCH:(tc + 1) * QCH],
                                        b["PS6"][0:8, 2560:3072], AF.Copy,
                                        scale=2.0)
                em.op("scalar", fhe, rd=("PS5",), wr=("LHA", "PS5"))

        for ki, knd in enumerate("qkv"):
            def fw(g, ki=ki, knd=knd, lay=lay, p=p):
                return g.dma_start(out=v["WQKV"][:, 0:c.DCH * 128],
                                   in_=d[f"{lay}_w{knd}{p}"])
            em.op("gpsimd", fw, wr=("WQKV",), dma=True)
            dstv = (v["QT"], v["KT"], v["VV"])[ki]
            dreg = ("QT", "KT", "VV")[ki]
            for tc in range(1 if "qkv" in ABLATE else c.NQC):
                par = tc % 2
                self._qkv_load_chunk(tc, par)
                if lay == "i" and ki == 0:
                    bt = (tc * QCH) % c.TB

                    def ftab(g, par=par, bt=bt):
                        g.dma_start(out=v["CS"][:, par * 1024:par * 1024 + 512],
                                    in_=d["cc_tab"][:, bt:bt + QCH])
                        return g.dma_start(
                            out=v["CS"][:, par * 1024 + 512:par * 1024 + 1024],
                            in_=d["ss_tab"][:, bt:bt + QCH])
                    em.op("gpsimd", ftab, wr=(f"CSL{par}",), dma=True)
                bank = tc % 2

                def fmm(t, tc=tc, par=par, ki=ki, bank=bank):
                    last = None
                    for cch in range(c.DCH):
                        last = t.matmul(
                            b["PS6"][:, bank * 512:bank * 512 + 512],
                            v["WQKV"][:, cch * 128:(cch + 1) * 128],
                            v["XS"][:, 4096 * par + cch * 512:4096 * par + (cch + 1) * 512],
                            start=(cch == 0),
                            stop=(not do_lora and cch == c.DCH - 1))
                    if do_lora:
                        last = t.matmul(b["PS6"][:, bank * 512:bank * 512 + 512],
                                        v["LB"][0:8, ki * 128:(ki + 1) * 128],
                                        b["LHA"][0:8, tc * QCH:(tc + 1) * QCH],
                                        start=False, stop=True)
                    return last
                rdl = (f"XS{par}", "WQKV", "LB", "LHA") if do_lora else \
                      (f"XS{par}", "WQKV")
                em.op("tensor", fmm, rd=rdl, wr=(f"PS{bank}",))

                def fev(s, tc=tc, ki=ki, bank=bank, p=p, lay=lay, dstv=dstv):
                    if lay == "o":
                        return s.activation(
                            dstv[:, tc * QCH:(tc + 1) * QCH],
                            b["PS6"][:, bank * 512:bank * 512 + 512],
                            AF.Identity,
                            bias=b["BQB"][:, 3 * p + ki:3 * p + ki + 1])
                    return s.copy(dstv[:, tc * QCH:(tc + 1) * QCH],
                                  b["PS6"][:, bank * 512:bank * 512 + 512])
                em.op("scalar", fev, rd=(f"PS{bank}", "consts"),
                      wr=(dreg, f"PS{bank}"))

                if lay == "i" and ki < 2:
                    def frope(ve, tc=tc, par=par, dstv=dstv, dreg=dreg):
                        cs = v["CS"][:, par * 1024:par * 1024 + 512]
                        ss = v["CS"][:, par * 1024 + 512:par * 1024 + 1024]
                        q = dstv[:, tc * QCH:(tc + 1) * QCH]
                        sw = v["PT"][:, 0:512]
                        for hh in (0, 64):
                            ve.tensor_copy(sw[hh:hh + 32, :], q[hh + 32:hh + 64, :])
                            ve.tensor_copy(sw[hh + 32:hh + 64, :], q[hh:hh + 32, :])
                        ve.drain()
                        ve.tensor_tensor(q, q, cs, op=ALU.mult)
                        ve.tensor_tensor(sw[:, :], sw[:, :], ss, op=ALU.mult)
                        ve.drain()
                        return ve.tensor_tensor(q, q, sw[:, :], op=ALU.add)
                    em.op("vector", frope, rd=(dreg, f"CSL{par}"),
                          wr=(dreg, "PT"))

        NB = c.TB // 128
        for j0 in range(0, NB, 4):
            def ftv(t, j0=j0):
                pb = b["RSB"].ap().bitcast(BF16)
                last = None
                for k in range(4):
                    j = j0 + k
                    last = t.transpose(pb[:, 128 * k:128 * (k + 1)],
                                       v["VV"][:, j * 128:(j + 1) * 128],
                                       b["IDB"][:, :])
                return last
            em.op("tensor", ftv, rd=("VV", "consts"), wr=("RSB",))

            def fev2(ve, j0=j0):
                pb = b["RSB"].ap().bitcast(BF16)
                last = None
                for k in range(4):
                    j = j0 + k
                    last = ve.tensor_copy(v["VV"][:, j * 128:(j + 1) * 128],
                                          pb[:, 128 * k:128 * (k + 1)])
                return last
            em.op("vector", fev2, rd=("RSB",), wr=("VV", "RSB"))

    def _qkv_load_chunk(self, tc, par):
        c, b, v, d, em = self.c, self.b, self.v, self.d, self.em
        m = (tc * QCH) // c.TOK
        lt = (tc * QCH) % c.TOK

        def fld(s, par=par, m=m, lt=lt):
            src = d["ag_out"][m * c.D:(m + 1) * c.D, lt:lt + QCH]
            return s.dma_start(
                out=v["XS"][:, 4096 * par:4096 * par + c.DCH * 512]
                .rearrange("p (cc t) -> p cc t", t=QCH),
                in_=src.rearrange("(cc p) t -> p cc t", p=128))
        em.op("scalar", fld, rd=("ag_out",), wr=(f"XS{par}",), dma=True)

    def bounds_pair(self, lay, p):
        c, b, v, em = self.c, self.b, self.v, self.em
        SM = b["SM"]
        for t_i, (srcv, sreg) in enumerate(((v["QT"], "QT"), (v["KT"], "KT"))):
            def fsq(ve, srcv=srcv):
                return ve.tensor_tensor(v["SQ"][:, 0:c.TB], srcv[:, 0:c.TB],
                                        srcv[:, 0:c.TB], op=ALU.mult)
            em.op("vector", fsq, rd=(sreg,), wr=("SQ",))
            for ch in range(1 if "bounds" in ABLATE else c.TB // 512):
                def fm(t, ch=ch):
                    m1 = t.matmul(b["RSB"][0:1, 0:512], b["ONES"][0:64, 0:1],
                                  v["SQ"][0:64, ch * 512:(ch + 1) * 512],
                                  start=True, stop=True, tile_position=(0, 0))
                    m2 = t.matmul(b["PVB"][0:1, 0:512], b["ONES"][64:128, 0:1],
                                  v["SQ"][64:128, ch * 512:(ch + 1) * 512],
                                  start=True, stop=True, tile_position=(64, 0))
                    return [m1, m2]
                em.op("tensor", fm, rd=("SQ", "consts2"), wr=("RSB", "PVB"))

                def facc(ve, ch=ch, t_i=t_i):
                    col = 96 + 2 * t_i
                    out = []
                    for q, ps in ((0, b["RSB"]), (1, b["PVB"])):
                        scr = b["EPI"][0:1, 512 * q:512 * q + 512]
                        if ch == 0:
                            out.append(ve.tensor_scalar(
                                scr, ps[0:1, 0:512], 1.0, None,
                                op0=ALU.mult, op1=ALU.max,
                                accum_out=SM[0:1, col + q:col + q + 1]))
                        else:
                            out.append(ve.tensor_scalar(
                                scr, ps[0:1, 0:512], 1.0,
                                SM[0:1, 110 + 2 * t_i + q:111 + 2 * t_i + q],
                                op0=ALU.mult, op1=ALU.max,
                                accum_out=SM[0:1, col + q:col + q + 1]))
                    return out
                em.op("vector", facc, rd=("RSB", "PVB", "SM"),
                      wr=("SM", "EPIs", "EPIr", "RSB", "PVB"))

                if ch < c.TB // 512 - 1:
                    def fcp(ve, t_i=t_i):
                        col = 96 + 2 * t_i
                        i1 = ve.tensor_copy(
                            SM[0:1, 110 + 2 * t_i:111 + 2 * t_i],
                            SM[0:1, col:col + 1])
                        i2 = ve.tensor_copy(
                            SM[0:1, 111 + 2 * t_i:112 + 2 * t_i],
                            SM[0:1, col + 1:col + 2])
                        return [i1, i2]
                    em.op("vector", fcp, rd=("SM",), wr=("SM",))

        def fc(ve):
            ve.tensor_tensor(SM[0:1, 104:105], SM[0:1, 96:97], SM[0:1, 97:98],
                             op=ALU.max)
            ve.tensor_tensor(SM[0:1, 105:106], SM[0:1, 98:99], SM[0:1, 99:100],
                             op=ALU.max)
            ve.drain()
            return ve.tensor_tensor(SM[0:1, 106:107], SM[0:1, 104:105],
                                    SM[0:1, 105:106], op=ALU.mult)
        em.op("vector", fc, rd=("SM",), wr=("SM",))

        def fsr(s):
            return s.activation(SM[0:1, 107:108], SM[0:1, 106:107], AF.Sqrt)
        em.op("scalar", fsr, rd=("SM",), wr=("SM",))

        def fneg(ve):
            return ve.tensor_scalar(SM[0:1, 108:109], SM[0:1, 107:108],
                                    -0.125, None, op0=ALU.mult)
        em.op("vector", fneg, rd=("SM",), wr=("SM",))

        def fbc(t):
            return t.matmul(b["RSB"][:, 0:1], b["ONESF"][0:1, :],
                            SM[0:1, 108:109], start=True, stop=True)
        em.op("tensor", fbc, rd=("SM", "consts2"), wr=("RSB",))

        def fcb(ve):
            return ve.tensor_copy(SM[:, 80:81], b["RSB"][:, 0:1])
        em.op("vector", fcb, rd=("RSB",), wr=("SMcb", "RSB"))

    def attention_pair(self, lay, p, causal):
        c, b, v, em = self.c, self.b, self.v, self.em
        smm, sex, spv, smsk = (self.s[x] for x in ("smm", "sex", "spv", "smsk"))
        eV = self.s["eV"]
        cb = b["SM"][:, 80:81]
        QT, KTv, VV, PT = v["QT"], v["KT"], v["VV"], v["PT"]
        PS6, PVB, RSB = b["PS6"], b["PVB"], b["RSB"]
        w_pe = em.need("QT", "KT", "VV", "SMcb") + \
            em.need_w("PVB", "RSB", "PS0", "PS1", "PS2", "PS3", "PS4", "PS5")
        w_act = em.need("SMcb") + em.need_w("PT")
        w_dve = em.need("CM")
        a0 = sex.n
        first = True
        for qc in range(c.NQC):
            kmax = c.NKT if not causal else 4 * (qc + 1)
            if "attn" in ABLATE:
                kmax = 1
            diag0 = kmax - 4 if causal else kmax + 1
            smm0, spv0, sex0, smsk0 = smm.n, spv.n, sex.n, smsk.n
            nep = eV.n

            def fpe(t, qc=qc, kmax=kmax, diag0=diag0, a0=a0, spv0=spv0,
                    sex0=sex0, smsk0=smsk0, nep=nep, first=first, causal=causal):
                if first:
                    for s, n in w_pe:
                        t.wait_ge(s.h, n)
                for kt in range(kmax + 1):
                    if kt < kmax:
                        a = a0 + kt
                        j3 = a % 3
                        if kt >= 3:
                            t.wait_ge(sex.h, sex0 + kt - 2)
                        elif sex0 > 0:
                            t.wait_ge(sex.h, sex0)
                        t.matmul(PS6[:, j3 * 512:(j3 + 1) * 512],
                                 KTv[0:64, kt * 128:(kt + 1) * 128],
                                 QT[0:64, qc * 512:(qc + 1) * 512],
                                 start=True, stop=True, tile_position=(0, 0))
                        t.matmul(PS6[:, (3 + j3) * 512:(4 + j3) * 512],
                                 KTv[64:128, kt * 128:(kt + 1) * 128],
                                 QT[64:128, qc * 512:(qc + 1) * 512],
                                 start=True, stop=True, tile_position=(64, 0)
                                 ).then_inc(smm.h, 1)
                    if kt >= 1:
                        kp = kt - 1
                        ap = a0 + kp
                        t.wait_ge(sex.h, sex0 + kp + 1)
                        if kp == 0:
                            t.wait_ge(eV.h, nep)
                        if causal and kp >= diag0:
                            t.wait_ge(smsk.h, smsk0 + 2 * (kp - diag0) + 2)
                        par = ap % PTD
                        stt = kp == 0
                        t.matmul(PVB[0:64, :], VV[:, kp * 128:kp * 128 + 64],
                                 PT[:, par * 1024:par * 1024 + 512],
                                 start=stt, stop=False, tile_position=(0, 0),
                                 skip_group_check=True)
                        t.matmul(PVB[64:128, :],
                                 VV[:, kp * 128 + 64:kp * 128 + 128],
                                 PT[:, par * 1024 + 512:par * 1024 + 1024],
                                 start=stt, stop=False, tile_position=(0, 64),
                                 skip_group_check=True)
                        t.matmul(RSB[0:1, :], b["ONES"][:, 0:1],
                                 PT[:, par * 1024:par * 1024 + 512],
                                 start=stt, stop=False, tile_position=(0, 0),
                                 skip_group_check=True)
                        t.matmul(RSB[64:65, :], b["ONES"][:, 0:1],
                                 PT[:, par * 1024 + 512:par * 1024 + 1024],
                                 start=stt, stop=False, tile_position=(0, 64),
                                 skip_group_check=True).then_inc(spv.h, 1)
            self.blk.tensor(fpe)
            smm.inc(kmax)
            spv.inc(kmax)

            def fex(s, kmax=kmax, a0=a0, smm0=smm0, spv0=spv0, first=first):
                if first:
                    for ss, n in w_act:
                        s.wait_ge(ss.h, n)
                for kt in range(kmax):
                    a = a0 + kt
                    j3, par = a % 3, a % PTD
                    s.wait_ge(smm.h, smm0 + kt + 1)
                    if kt >= 4:
                        s.wait_ge(spv.h, spv0 + kt - 3)
                    elif spv0 > 0:
                        s.wait_ge(spv.h, spv0)
                    src = PS6.ap()[:, j3 * 512:j3 * 512 + 2048].rearrange(
                        "q (a bb) -> q a bb", bb=512)[:, 0:4:3, :]
                    dst = PT[:, par * 1024:par * 1024 + 1024].rearrange(
                        "q (a bb) -> q a bb", bb=512)
                    s.activation(dst, src, AF.Exp, bias=cb, scale=0.125
                                 ).then_inc(sex.h, 1)
            self.blk.scalar(fex)
            sex.inc(kmax)

            if causal:
                def fmk(ve, kmax=kmax, diag0=diag0, a0=a0, sex0=sex0,
                        first=first):
                    if first:
                        for ss, n in w_dve:
                            ve.wait_ge(ss.h, n)
                    for j in range(4):
                        kt = diag0 + j
                        par = (a0 + kt) % PTD
                        ve.wait_ge(sex.h, sex0 + kt + 1)
                        ve.tensor_tensor(PT[:, par * 1024:par * 1024 + 512],
                                         PT[:, par * 1024:par * 1024 + 512],
                                         v["CM"][:, j * 512:(j + 1) * 512],
                                         op=ALU.mult).then_inc(smsk.h, 1)
                        ve.tensor_tensor(
                            PT[:, par * 1024 + 512:par * 1024 + 1024],
                            PT[:, par * 1024 + 512:par * 1024 + 1024],
                            v["CM"][:, j * 512:(j + 1) * 512],
                            op=ALU.mult).then_inc(smsk.h, 1)
                self.blk.vector(fmk)
                smsk.inc(8)

            npv = spv.n

            def epA(ve, npv=npv):
                ve.wait_ge(spv.h, npv)
                i1 = ve.reciprocal(b["EPI"][0:1, 512:1024], RSB[0:1, :])
                i2 = ve.reciprocal(b["EPI"][64:65, 512:1024], RSB[64:65, :])
                i3 = ve.tensor_copy(b["EPI"][:, 2560:3072], PVB[:, :])
                return [i1, i2, i3]
            em.op("vector", epA, rd=(), wr=("EPIr", "EPIo", "RSB", "PVB"))

            def epB(t):
                m1 = t.matmul(RSB[:, :], b["ONESF"][0:1, :],
                              b["EPI"][0:1, 512:1024], start=True, stop=True)
                m2 = t.matmul(PVB[:, :], b["ONESF"][64:65, :],
                              b["EPI"][64:65, 512:1024], start=True, stop=True)
                return [m1, m2]
            em.op("tensor", epB, rd=("EPIr", "consts2"), wr=("RSB", "PVB"))

            def epC(ve):
                i1 = ve.tensor_tensor(v["TPS"][0:64, 0:512],
                                      b["EPI"][0:64, 2560:3072],
                                      RSB[0:64, :], op=ALU.mult)
                i2 = ve.tensor_tensor(v["TPS"][64:128, 0:512],
                                      b["EPI"][64:128, 2560:3072],
                                      PVB[64:128, :], op=ALU.mult)
                return [i1, i2]
            em.op("vector", epC, rd=("EPIo",), wr=("TPS", "RSB", "PVB"))

            m = (qc * QCH) // c.TOK
            lt = (qc * QCH) % c.TOK

            def fst(sy, m=m, lt=lt, p=p):
                return sy.dma_start(
                    out=self.d[f"st{m}"][p * 128:(p + 1) * 128, lt:lt + QCH],
                    in_=v["TPS"][:, 0:512])
            em.op("sync", fst, rd=("TPS",), wr=(f"st{m}",), dma=True)
            if p == c.PAIRS - 1 and lt + QCH == c.TOK:
                def fagm(g, m=m):
                    rg = [list(range(g0, g0 + c.GS))
                          for g0 in range(0, NCORE, c.GS)]
                    g.collective_compute(
                        "AllGather", ALU.bypass, replica_groups=rg,
                        ins=[self.d[f"st{m}"]],
                        outs=[self.d["ao_all"][m * c.GS * c.SR:(m + 1) * c.GS * c.SR, :]])
                em.op("gpsimd", fagm, rd=(f"st{m}",), wr=(f"aoall{m}",),
                      cc=True)
            a0 += kmax
            first = False
        em.note_write("PT", sex, sex.n)
        for bank in range(6):
            em.note_write(f"PS{bank}", smm, smm.n)
            em.note_read(f"PS{bank}", sex, sex.n)
        em.note_read("QT", smm, smm.n)
        em.note_read("KT", smm, smm.n)
        em.note_read("VV", spv, spv.n)
        em.note_read("SMcb", sex, sex.n)

    def proj_and_residual(self, lay):
        c, b, v, d, em = self.c, self.b, self.v, self.d, self.em
        do_lora = lay == "o"

        def fidx(ve):
            for j in range(c.DCH):
                ve.tensor_scalar(b["SM"][:, 208 + j:209 + j], b["IOTS"][:, :],
                                 1.0, float(128 * j), op0=ALU.mult, op1=ALU.add)
            ve.drain()
            for j in range(c.DCH):
                ve.tensor_tensor(b["SM"][:, 208 + j:209 + j],
                                 b["SM"][:, 208 + j:209 + j], b["MBS"][:, :],
                                 op=ALU.add)
            ve.drain()
            return ve.tensor_copy(b["EIDX"][:, 16:16 + c.DCH],
                                  b["SM"][:, 208:208 + c.DCH])
        em.op("vector", fidx, rd=("consts",), wr=("SM", "EIDX"))

        def fgat(g):
            last = None
            for j in range(c.DCH):
                last = g.indirect_dma_start(
                    out=v["PIN"][:, j * c.TOK:(j + 1) * c.TOK], out_offset=None,
                    in_=d["ao_all"],
                    in_offset=IndirectOffsetOnAxis(ap=b["EIDX"][:, 16 + j:17 + j],
                                                   axis=0))
            return last
        em.op("gpsimd", fgat, rd=("ao_all", "aoall0", "aoall1", "aoall2", "aoall3", "EIDX"), wr=("PIN",), dma=True)

        def fwp(g, lay=lay):
            last = g.dma_start(out=v["WP"][:, 0:c.DCH * c.D], in_=d[f"{lay}_wp"])
            if lay == "o":
                g.dma_start(out=v["LB"][:, 1024:1024 + c.DCH * 8], in_=d["o_Ap"])
                last = g.dma_start(out=v["LB"][0:8, 0:c.D], in_=d["o_Bp"])
            return last
        em.op("gpsimd", fwp, wr=("WP", "LB"), dma=True)

        if do_lora:
            for ch in range(c.TOK // 512):
                def fha(t, ch=ch):
                    last = None
                    for cch in range(c.DCH):
                        last = t.matmul(
                            b["PS6"][0:8, 2560:3072],
                            v["LB"][:, 1024 + cch * 8:1024 + (cch + 1) * 8],
                            v["PIN"][:, cch * c.TOK + ch * 512:cch * c.TOK + (ch + 1) * 512],
                            start=(cch == 0), stop=(cch == c.DCH - 1))
                    return last
                em.op("tensor", fha, rd=("PIN", "LB"), wr=("PS5",))

                def fhe(s, ch=ch):
                    return s.activation(
                        b["LHAP"][0:8, ch * 512:(ch + 1) * 512],
                        b["PS6"][0:8, 2560:3072], AF.Copy, scale=2.0)
                em.op("scalar", fhe, rd=("PS5",), wr=("LHAP", "PS5"))

        for co in range(1 if "proj" in ABLATE else c.DCH):
            for ch in range(c.TOK // 512):
                bank = (co * (c.TOK // 512) + ch) % 2

                def fpj(t, co=co, ch=ch, bank=bank):
                    last = None
                    for cch in range(c.DCH):
                        last = t.matmul(
                            b["PS6"][:, bank * 512:bank * 512 + 512],
                            v["WP"][:, (cch * c.DCH + co) * 128:(cch * c.DCH + co + 1) * 128],
                            v["PIN"][:, cch * c.TOK + ch * 512:cch * c.TOK + (ch + 1) * 512],
                            start=(cch == 0),
                            stop=(not do_lora and cch == c.DCH - 1))
                    if do_lora:
                        last = t.matmul(b["PS6"][:, bank * 512:bank * 512 + 512],
                                        v["LB"][0:8, co * 128:(co + 1) * 128],
                                        b["LHAP"][0:8, ch * 512:(ch + 1) * 512],
                                        start=False, stop=True)
                    return last
                rdl = ("PIN", "WP", "LB", "LHAP") if do_lora else ("PIN", "WP")
                em.op("tensor", fpj, rd=rdl, wr=(f"PS{bank}",))

                def fze(s, co=co, bank=bank, lay=lay):
                    if lay == "o":
                        return s.activation(
                            v["TPS"][:, 0:512],
                            b["PS6"][:, bank * 512:bank * 512 + 512],
                            AF.Identity, bias=b["BPS"][:, co:co + 1])
                    return s.copy(v["TPS"][:, 0:512],
                                  b["PS6"][:, bank * 512:bank * 512 + 512])
                em.op("scalar", fze, rd=(f"PS{bank}", "consts"),
                      wr=("TPS", f"PS{bank}"))

                def fzt(t):
                    pb = b["RSB"].ap().bitcast(BF16)
                    last = None
                    for k in range(4):
                        last = t.transpose(pb[:, 128 * k:128 * (k + 1)],
                                           v["TPS"][:, 128 * k:128 * (k + 1)],
                                           b["IDB"][:, :])
                    return last
                em.op("tensor", fzt, rd=("TPS", "consts"), wr=("RSB",))

                def fzv(ve, co=co, ch=ch, lay=lay):
                    pb = b["RSB"].ap().bitcast(BF16)
                    out = []
                    for k in range(4):
                        ti = ch * 4 + k
                        dst = v["H2"][:, ti * c.D + co * 128:ti * c.D + (co + 1) * 128]
                        if lay == "o":
                            out.append(ve.tensor_copy(
                                dst, pb[:, 128 * k:128 * (k + 1)]))
                        else:
                            out.append(ve.tensor_tensor(
                                dst, dst, pb[:, 128 * k:128 * (k + 1)],
                                op=ALU.add))
                    return out
                em.op("vector", fzv, rd=("RSB", "H2"), wr=("H2", "RSB"))

        if lay == "o":
            for i in range(c.TT):
                mb_, reg = self.ln_stats(v["H2"][:, i * c.D:(i + 1) * c.D],
                                         ("H2",), slot=i % 2)

                def fadd(ve, i=i, mb_=mb_):
                    sm = b["SM"]
                    ve.tensor_scalar(b["ROUT"][:, 0:c.D],
                                     v["H2"][:, i * c.D:(i + 1) * c.D],
                                     sm[:, mb_:mb_ + 1], sm[:, mb_ + 3:mb_ + 4],
                                     op0=ALU.subtract, op1=ALU.mult)
                    ve.drain()
                    return ve.tensor_tensor(b["R"][:, i * c.D:(i + 1) * c.D],
                                            b["R"][:, i * c.D:(i + 1) * c.D],
                                            b["ROUT"][:, 0:c.D], op=ALU.add)
                em.op("vector", fadd, rd=("H2", reg), wr=("R", "ROUT"))

    def h2_u_stage(self):
        c, b, v, em = self.c, self.b, self.v, self.em

        def fgb(g):
            g.dma_start(out=v["LB"][0:1, 0:c.D],
                        in_=self.d["i_g1"].rearrange("(a x) -> a x", a=1))
            return g.dma_start(out=v["LB"][0:1, c.D:2 * c.D],
                               in_=self.d["i_b1"].rearrange("(a x) -> a x", a=1))
        em.op("gpsimd", fgb, wr=("LB",), dma=True)

        def fob(g):
            return g.memset(b["ONEBR"][0:1, :], 1.0)
        em.op("gpsimd", fob, wr=("CSL0",))
        for j in range(2 * c.DCH):
            half = j // c.DCH
            col = (j % c.DCH) * 128

            def fb2(t, half=half, col=col):
                return t.matmul(
                    b["PS6"][:, 0:128], b["ONEBR"][0:1, :],
                    v["LB"][0:1, half * c.D + col:half * c.D + col + 128],
                    start=True, stop=True)
            em.op("tensor", fb2, rd=("LB", "CSL0"), wr=("PS0",))

            def fe2(ve, j=j):
                return ve.tensor_copy(v["GBB"][:, j * 128:(j + 1) * 128],
                                      b["PS6"][:, 0:128])
            em.op("vector", fe2, rd=("PS0",), wr=("GBB", "PS0"))
        for i in range(c.TT):
            mb_, reg = self.ln_stats(b["R"][:, i * c.D:(i + 1) * c.D], ("R",),
                                     slot=i % 2)

            def fa(ve, i=i, mb_=mb_, reg=reg):
                sm = b["SM"]
                ve.tensor_scalar(v["H2"][:, i * c.D:(i + 1) * c.D],
                                 b["R"][:, i * c.D:(i + 1) * c.D],
                                 sm[:, mb_:mb_ + 1], sm[:, mb_ + 3:mb_ + 4],
                                 op0=ALU.subtract, op1=ALU.mult)
                ve.drain()
                ve.tensor_tensor(v["TPS"][:, 0:c.D],
                                 v["H2"][:, i * c.D:(i + 1) * c.D],
                                 v["GBB"][:, 0:c.D], op=ALU.mult)
                ve.drain()
                return ve.tensor_tensor(v["TPS"][:, 0:c.D], v["TPS"][:, 0:c.D],
                                        v["GBB"][:, c.D:2 * c.D],
                                        op=ALU.add)
            em.op("vector", fa, rd=("R", reg, "GBB"), wr=("H2", "TPS"))
            self.transpose_stage_tile(i, "ag_in")

    def spill_and_flat(self):
        c, b, v, d, em = self.c, self.b, self.v, self.d, self.em

        def fsp(sy):
            xr = d["sp_x1"].rearrange("(i p) dd -> i p dd", p=128)
            yr = d["sp_y"].rearrange("(i p) dd -> i p dd", p=128)
            last = None
            for i in range(c.TT):
                sy.dma_start(out=xr[i, :, :], in_=b["R"][:, i * c.D:(i + 1) * c.D])
                last = sy.dma_start(out=yr[i, :, :],
                                    in_=v["H2"][:, i * c.D:(i + 1) * c.D])
            return last
        em.op("sync", fsp, rd=("R", "H2"), wr=("sp_x1", "sp_y"), dma=True)

        for i in range(c.TT):
            mb_, reg = self.ln_stats(v["H2"][:, i * c.D:(i + 1) * c.D], ("H2",),
                                     slot=i % 2)

            def fa(ve, i=i, mb_=mb_):
                sm = b["SM"]
                ve.tensor_scalar(b["R"][:, i * c.D:(i + 1) * c.D],
                                 v["H2"][:, i * c.D:(i + 1) * c.D],
                                 sm[:, mb_:mb_ + 1], sm[:, mb_ + 3:mb_ + 4],
                                 op0=ALU.subtract, op1=ALU.mult)
                ve.drain()
                ve.tensor_tensor(b["R"][:, i * c.D:(i + 1) * c.D],
                                 b["R"][:, i * c.D:(i + 1) * c.D],
                                 b["GB2"][:, 0:c.D], op=ALU.mult)
                ve.drain()
                ve.tensor_tensor(b["R"][:, i * c.D:(i + 1) * c.D],
                                 b["R"][:, i * c.D:(i + 1) * c.D],
                                 b["GB2"][:, c.D:2 * c.D], op=ALU.add)
                ve.drain()
                return ve.tensor_copy(v["TPS"][:, 0:c.D],
                                      b["R"][:, i * c.D:(i + 1) * c.D])
            em.op("vector", fa, rd=("H2", reg, "GB2", "sp_x1"),
                  wr=("R", "TPS"))

            def fdm(sy, i=i):
                return sy.dma_start(
                    out=d["flat_st"].rearrange("(a p) dd -> a p dd", p=128)[i, :, :],
                    in_=v["TPS"][:, 0:c.D])
            em.op("sync", fdm, rd=("TPS",), wr=("flat_st",), dma=True)

    def routing(self):
        c, b, v, d, em = self.c, self.b, self.v, self.d, self.em

        def fz(ve):
            return ve.memset(b["ROUT"][:, 0:c.E * c.CAP // 128], 0.0)
        em.op("vector", fz, wr=("ROUT",))

        def fzd(sy):
            return sy.dma_start(
                out=d["tmap_in"].rearrange("(p f) o -> p (f o)", p=128),
                in_=b["ROUT"][:, 0:c.E * c.CAP // 128])
        em.op("sync", fzd, rd=("ROUT",), wr=("tmap_in",), dma=True)

        for i in range(1 if "routing" in ABLATE else c.TT):
            for g0 in range(0, c.DCH, 4):
                def ftt(t, i=i, g0=g0):
                    last = None
                    for k in range(4):
                        cch = g0 + k
                        last = t.transpose(
                            b["RSB"][:, 128 * k:128 * (k + 1)],
                            b["R"][:, i * c.D + cch * 128:i * c.D + (cch + 1) * 128],
                            b["IDF"][:, :])
                    return last
                em.op("tensor", ftt, rd=("R", "consts"), wr=("RSB",))

                def fte(ve, g0=g0):
                    return ve.tensor_copy(b["ROUT"][:, g0 * 128:(g0 + 4) * 128],
                                          b["RSB"][:, :])
                em.op("vector", fte, rd=("RSB",), wr=("ROUT", "RSB"))

            def flg(t):
                last = None
                for cch in range(c.DCH):
                    last = t.matmul(b["PS6"][0:8, 2560:2688],
                                    b["WG"][:, cch * 8:(cch + 1) * 8],
                                    b["ROUT"][:, cch * 128:(cch + 1) * 128],
                                    start=(cch == 0), stop=(cch == c.DCH - 1))
                return last
            em.op("tensor", flg, rd=("ROUT", "consts"), wr=("PS5",))

            def fle(ve):
                return ve.tensor_copy(b["ROUT"][0:8, 2048:2176],
                                      b["PS6"][0:8, 2560:2688])
            em.op("vector", fle, rd=("PS5",), wr=("ROUT", "PS5"))

            def flt(t):
                return t.transpose(b["RSB"][:, 0:8], b["ROUT"][0:8, 2048:2176],
                                   b["IDF"][0:8, 0:8])
            em.op("tensor", flt, rd=("ROUT", "consts"), wr=("RSB",))

            def fsm(ve):
                L = b["ROUT"][:, 2304:2312]
                ve.tensor_copy(L, b["RSB"][:, 0:8])
                ve.drain()
                ve.tensor_reduce(b["ROUT"][:, 2312:2313], L, axis=AXX, op=ALU.max)
                ve.drain()
                return ve.tensor_scalar(b["ROUT"][:, 2313:2314],
                                        b["ROUT"][:, 2312:2313], -1.0, None,
                                        op0=ALU.mult)
            em.op("vector", fsm, rd=("RSB",), wr=("ROUT", "RSB"))

            def fex(s):
                return s.activation(b["ROUT"][:, 2320:2328],
                                    b["ROUT"][:, 2304:2312], AF.Exp,
                                    bias=b["ROUT"][:, 2313:2314])
            em.op("scalar", fex, rd=("ROUT",), wr=("ROUT",))

            def ftop(ve, i=i):
                E8 = b["ROUT"][:, 2320:2328]
                ve.tensor_reduce(b["ROUT"][:, 2328:2329], E8, axis=AXX,
                                 op=ALU.add)
                ve.drain()
                ve.reciprocal(b["ROUT"][:, 2329:2330], b["ROUT"][:, 2328:2329])
                ve.drain()
                G = b["GTS"][:, 14 * i:14 * i + 8]
                ve.tensor_scalar(G, E8, b["ROUT"][:, 2329:2330], None,
                                 op0=ALU.mult)
                ve.drain()
                ve.max(b["ROUT"][:, 2330:2338], G)
                ve.drain()
                ve.max_index(b["EIDX"][:, 8:16], b["ROUT"][:, 2330:2338], G)
                ve.drain()
                ve.tensor_copy(b["ROUT"][:, 2338:2346], b["EIDX"][:, 8:16])
                ve.drain()
                ve.tensor_copy(b["GTS"][:, 14 * i + 8:14 * i + 10],
                               b["ROUT"][:, 2338:2340])
                ve.tensor_copy(b["GTS"][:, 14 * i + 10:14 * i + 12],
                               b["ROUT"][:, 2330:2332])
                ve.drain()
                ve.tensor_scalar(b["ROUT"][:, 2346:2354], b["IO8"][:, :],
                                 b["GTS"][:, 14 * i + 8:14 * i + 9], None,
                                 op0=ALU.is_equal)
                ve.drain()
                ve.tensor_scalar(b["ROUT"][:, 2354:2362], b["IO8"][:, :],
                                 b["GTS"][:, 14 * i + 9:14 * i + 10], None,
                                 op0=ALU.is_equal)
                ve.drain()
                ve.tensor_tensor(b["ROUT"][:, 2362:2370],
                                 b["ROUT"][:, 2346:2354],
                                 b["ROUT"][:, 2354:2362], op=ALU.add)
                ve.drain()
                return ve.tensor_copy(v["TPS"][:, 0:8], b["ROUT"][:, 2362:2370])
            em.op("vector", ftop, rd=("ROUT", "consts"),
                  wr=("ROUT", "GTS", "TPS", "EIDX"))

            def fmt(t):
                pb = b["RSB"].ap().bitcast(BF16)
                return t.transpose(pb[0:8, 0:128], v["TPS"][:, 0:8],
                                   b["IDB"][:, :])
            em.op("tensor", fmt, rd=("TPS", "consts"), wr=("RSB",))

            def fmte(ve, i=i):
                pb = b["RSB"].ap().bitcast(BF16)
                return ve.tensor_copy(b["MKT"][0:8, i * 128:(i + 1) * 128],
                                      pb[0:8, 0:128])
            em.op("vector", fmte, rd=("RSB",), wr=("MKT", "RSB"))

        def fscan(ve):
            ve.tensor_tensor_scan(b["CUM"][0:8, :], b["MKT"][0:8, :],
                                  b["MKT"][0:8, :], 0.0, op0=ALU.add,
                                  op1=ALU.bypass)
            ve.drain()
            return ve.tensor_copy(b["SM"][0:8, 120:121],
                                  b["CUM"][0:8, c.TOK - 1:c.TOK])
        em.op("vector", fscan, rd=("MKT",), wr=("CUM", "SM"))

        def fcdm(sy):
            return sy.dma_start(out=d["cnt_in"].rearrange("o e -> e o"),
                                in_=b["SM"][0:8, 120:121],
                                allow_slow_non_contiguous=True)
        em.op("sync", fcdm, rd=("SM",), wr=("cnt_in",), dma=True)
        self.allgather_global("cnt_in", "cnt_all")

        def fcld(sy):
            return sy.dma_start(out=b["CNT"][:, :], in_=d["cnt_all"])
        em.op("sync", fcld, rd=("cnt_all",), wr=("CNT",), dma=True)

        def fbase(t):
            return t.matmul(b["RSB"][0:1, 0:8], b["PFXS"][0:8, 0:1],
                            b["CNT"][0:8, :], start=True, stop=True)
        em.op("tensor", fbase, rd=("CNT",), wr=("RSB",))

        def fbse(ve):
            return ve.tensor_copy(b["SM"][0:1, 128:136], b["RSB"][0:1, 0:8])
        em.op("vector", fbse, rd=("RSB",), wr=("SM", "RSB"))

        def fbbc(t):
            return t.matmul(b["RSB"][:, 0:8], b["ONESF"][0:1, :],
                            b["SM"][0:1, 128:136], start=True, stop=True)
        em.op("tensor", fbbc, rd=("SM", "consts2"), wr=("RSB",))

        def fbm1(ve):
            return ve.tensor_scalar(b["SM"][:, 136:144], b["RSB"][:, 0:8],
                                    -1.0, None, op0=ALU.add)
        em.op("vector", fbm1, rd=("RSB",), wr=("SM", "RSB"))

        for i in range(1 if "routing" in ABLATE else c.TT):
            def fcb(t, i=i):
                return t.transpose(b["RSB"][:, 0:8],
                                   b["CUM"][0:8, i * 128:(i + 1) * 128],
                                   b["IDF"][0:8, 0:8])
            em.op("tensor", fcb, rd=("CUM", "consts"), wr=("RSB",))

            def fpos(ve, i=i):
                P = b["ROUT"][:, 2304:2312]
                ve.tensor_tensor(P, b["RSB"][:, 0:8], b["SM"][:, 136:144],
                                 op=ALU.add)
                ve.drain()
                last = None
                for rr in range(2):
                    oh = b["ROUT"][:, 2346 + 8 * rr:2354 + 8 * rr]
                    ve.tensor_scalar(oh, b["IO8"][:, :],
                                     b["GTS"][:, 14 * i + 8 + rr:14 * i + 9 + rr],
                                     None, op0=ALU.is_equal)
                    ve.drain()
                    ve.tensor_tensor(b["ROUT"][:, 2376:2384], P, oh, op=ALU.mult)
                    ve.drain()
                    ve.tensor_reduce(b["ROUT"][:, 2384 + rr:2385 + rr],
                                     b["ROUT"][:, 2376:2384], axis=AXX,
                                     op=ALU.add)
                    ve.drain()
                for rr in range(2):
                    pr = b["ROUT"][:, 2384 + rr:2385 + rr]
                    kp = b["ROUT"][:, 2386 + rr:2387 + rr]
                    ve.tensor_scalar(kp, pr, float(c.CAP - 1), None,
                                     op0=ALU.is_le)
                    ve.drain()
                    ve.tensor_tensor(
                        b["GTS"][:, 14 * i + 12 + rr:14 * i + 13 + rr],
                        b["GTS"][:, 14 * i + 10 + rr:14 * i + 11 + rr],
                        kp, op=ALU.mult)
                    ve.drain()
                    dv = b["ROUT"][:, 2388 + rr:2389 + rr]
                    ve.tensor_scalar(dv,
                                     b["GTS"][:, 14 * i + 8 + rr:14 * i + 9 + rr],
                                     float(c.CAP), None, op0=ALU.mult)
                    ve.drain()
                    ve.tensor_tensor(dv, dv, pr, op=ALU.add)
                    ve.drain()
                    ve.tensor_scalar(b["ROUT"][:, 2390:2391], kp, -3.0e7, 3.0e7,
                                     op0=ALU.mult, op1=ALU.add)
                    ve.drain()
                    ve.tensor_tensor(dv, dv, b["ROUT"][:, 2390:2391],
                                     op=ALU.add)
                    ve.drain()
                    ve.tensor_copy(b["EIDX"][:, 20 + rr:21 + rr], dv)
                    ve.drain()
                    mn = b["ROUT"][:, 2391:2392]
                    ve.tensor_scalar(mn, pr, float(c.CAP - 1), None,
                                     op0=ALU.min)
                    ve.drain()
                    ve.tensor_scalar(b["ROUT"][:, 2393:2394], mn,
                                     float(c.CAP // 2), None, op0=ALU.is_ge)
                    ve.drain()
                    ve.tensor_scalar(b["ROUT"][:, 2393:2394],
                                     b["ROUT"][:, 2393:2394],
                                     float((c.E - 1) * (c.CAP // 2)), None,
                                     op0=ALU.mult)
                    ve.drain()
                    ve.tensor_scalar(b["ROUT"][:, 2392:2393],
                                     b["GTS"][:, 14 * i + 8 + rr:14 * i + 9 + rr],
                                     float(c.CAP // 2), None, op0=ALU.mult)
                    ve.drain()
                    ve.tensor_tensor(mn, mn, b["ROUT"][:, 2392:2393],
                                     op=ALU.add)
                    ve.drain()
                    ve.tensor_tensor(mn, mn, b["ROUT"][:, 2393:2394],
                                     op=ALU.add)
                    ve.drain()
                    last = ve.tensor_copy(b["CIX"][:, 2 * i + rr:2 * i + rr + 1],
                                          mn)
                    ve.drain()
                return last
            em.op("vector", fpos, rd=("RSB", "ROUT", "GTS", "consts"),
                  wr=("ROUT", "GTS", "EIDX", "CIX", "RSB"))

            def fsc(g, i=i):
                last = None
                for rr in range(2):
                    last = g.indirect_dma_start(
                        out=d["tmap_in"],
                        out_offset=IndirectOffsetOnAxis(
                            ap=b["EIDX"][:, 20 + rr:21 + rr], axis=0),
                        in_=b["TVS"][:, i:i + 1], in_offset=None,
                        bounds_check=c.E * c.CAP - 1, oob_is_err=False)
                return last
            em.op("gpsimd", fsc, rd=("EIDX", "consts", "tmap_in"),
                  wr=("tmap_in",), dma=True)

        self.allgather_global("tmap_in", "tmap_all", op="AllReduce",
                              alu=ALU.add)
        self.dbg_dram("tmap", "tmap_all")

    def expert_ffn(self):
        c, b, v, d, em = self.c, self.b, self.v, self.d, self.em

        def fti(ve):
            ve.tensor_scalar(b["SM"][:, 216:217], b["IOTS"][:, :],
                             float(c.CAPT - 1), None, op0=ALU.min)
            ve.drain()
            ve.tensor_tensor(b["SM"][:, 216:217], b["SM"][:, 216:217],
                             b["EBD"][:, :], op=ALU.add)
            ve.drain()
            return ve.tensor_copy(b["EIDX"][:, 0:1], b["SM"][:, 216:217])
        em.op("vector", fti, rd=("consts",), wr=("SM", "EIDX"))

        def ftg(g):
            return g.indirect_dma_start(
                out=b["ROUT"][:, 2048:2176], out_offset=None,
                in_=d["tmap_all"].rearrange("(a p) o -> a (p o)", p=128),
                in_offset=IndirectOffsetOnAxis(ap=b["EIDX"][:, 0:1], axis=0))
        em.op("gpsimd", ftg, rd=("tmap_all", "EIDX"), wr=("ROUT",), dma=True)

        def ftt(t):
            return t.transpose(b["RSB"][:, 0:c.CAPT],
                               b["ROUT"][0:c.CAPT, 2048:2176],
                               b["IDF"][0:c.CAPT, 0:c.CAPT])
        em.op("tensor", ftt, rd=("ROUT", "consts"), wr=("RSB",))

        def fte(ve):
            ve.tensor_copy(b["ROUT"][:, 0:c.CAPT], b["RSB"][:, 0:c.CAPT])
            ve.drain()
            return ve.tensor_copy(b["EIDX"][:, 0:c.CAPT],
                                  b["ROUT"][:, 0:c.CAPT])
        em.op("vector", fte, rd=("RSB",), wr=("ROUT", "EIDX", "RSB"))

        for j in range(1 if "gather" in ABLATE else c.CAPT):
            gb = (v["EIG"], "EIG") if j % 2 == 0 else (v["MO1"], "MO1")

            def fgi(g, j=j, gb=gb):
                return g.indirect_dma_start(
                    out=gb[0][:, 0:c.D], out_offset=None, in_=d["flat_all"],
                    in_offset=IndirectOffsetOnAxis(ap=b["EIDX"][:, j:j + 1],
                                                   axis=0))
            em.op("gpsimd", fgi, rd=("flat_all", "EIDX"), wr=(gb[1],),
                  dma=True)

            def fei(sy, j=j, gb=gb):
                return sy.dma_start(out=d["ei_tok"][j * 128:(j + 1) * 128, :],
                                    in_=gb[0][:, 0:c.D])
            em.op("sync", fei, rd=(gb[1],), wr=("ei_tok",), dma=True)

        for cch in range(c.DCH):
            def ftx(sy, cch=cch):
                return sy.dma_start(
                    out=v["EIT"][:, cch * c.CAP:(cch + 1) * c.CAP],
                    in_=d["ei_tok"][:, cch * 128:(cch + 1) * 128],
                    transpose=True)
            em.op("sync", ftx, rd=("ei_tok",), wr=("EIT",), dma=True)

        eS = self.s["eS"]
        for s_i in range(1 if "moe" in ABLATE else c.NSW):
            r0 = s_i * 512
            n0 = eS.n
            acts = []
            w_mid = em.need_w("MID")
            for ft in range(c.FT):
                par = ft % 2

                def fw1(g, ft=ft, par=par):
                    w = c.DCH * 128
                    return g.dma_start(
                        out=v["W1B"][:, par * w:(par + 1) * w],
                        in_=d["w1e"][ft, :, :])
                em.op("gpsimd", fw1, wr=(f"W1B{par}",), dma=True)

                def fmm(t, par=par, r0=r0):
                    w = c.DCH * 128
                    last = None
                    for cch in range(c.DCH):
                        last = t.matmul(
                            b["PS6"][:, par * 512:par * 512 + 512],
                            v["W1B"][:, par * w + cch * 128:par * w + (cch + 1) * 128],
                            v["EIT"][:, cch * c.CAP + r0:cch * c.CAP + r0 + 512],
                            start=(cch == 0), stop=(cch == c.DCH - 1))
                    return last
                semT, nT = em.op("tensor", fmm, rd=(f"W1B{par}", "EIT"),
                                 wr=(f"PS{par}",))
                wl = [(semT, nT)]
                if ft == 0:
                    wl += w_mid
                    if s_i == 0:
                        wl += em.need("consts")
                acts.append((wl, "gelu", ft, par))
                em.note_read(f"PS{par}", eS, n0 + ft + 1)
            em.note_write("MID", eS, n0 + c.FT)

            for dt in range(c.DCH):
                par = dt % 2
                w2v, w2r = ((v["W2B"], "W2B") if par == 0 else
                            (v["W2B2"], "W2B2"))

                def fw2(g, dt=dt, w2v=w2v):
                    return g.dma_start(out=w2v[:, 0:c.FT * 128],
                                       in_=d["w2e"][dt, :, :])
                em.op("gpsimd", fw2, wr=(w2r,), dma=True)

                def fm2(t, par=par, w2v=w2v):
                    last = None
                    for fc in range(c.FT):
                        last = t.matmul(
                            b["PS6"][:, (2 + par) * 512:(3 + par) * 512],
                            w2v[:, fc * 128:(fc + 1) * 128],
                            v["MID"][:, fc * 512:(fc + 1) * 512],
                            start=(fc == 0), stop=(fc == c.FT - 1))
                    return last
                semT2, nT2 = em.op("tensor", fm2, rd=(w2r, "MID"),
                                   wr=(f"PS{2 + par}",))
                ci = n0 + c.FT + dt + 1
                acts.append(([(semT2, nT2)] + em.need_w("EIG"),
                             "out", dt, 2 + par))
                em.note_read(f"PS{2 + par}", eS, ci)
                em.note_write("EIG", eS, ci)

                def ftr2(t, par=par):
                    pbank = b["RSB"] if par == 0 else b["PVB"]
                    pb = pbank.ap().bitcast(BF16)
                    last = None
                    for k in range(4):
                        last = t.transpose(pb[:, 128 * k:128 * (k + 1)],
                                           v["EIG"][:, 128 * k:128 * (k + 1)],
                                           b["IDB"][:, :])
                    return last
                em.op("tensor", ftr2, rd=("EIG", "consts"),
                      wr=("RSB" if par == 0 else "PVB",))

                def fcv2(ve, par=par):
                    pbank = b["RSB"] if par == 0 else b["PVB"]
                    pb = pbank.ap().bitcast(BF16)
                    return ve.tensor_copy(v["EIG"][:, 512:1024], pb[:, 0:512])
                em.op("vector", fcv2, rd=("RSB" if par == 0 else "PVB",),
                      wr=("EIG", "RSB" if par == 0 else "PVB"))

                def fst2(sy, dt=dt, r0=r0):
                    return sy.dma_start(
                        out=d["eo_st"][r0:r0 + 512, dt * 128:(dt + 1) * 128]
                        .rearrange("(k p) e -> p k e", p=128),
                        in_=v["EIG"][:, 512:1024]
                        .rearrange("p (k e) -> p k e", e=128))
                em.op("sync", fst2, rd=("EIG",),
                      wr=("eo_stA" if r0 < c.CAP // 2 else "eo_stB",),
                      dma=True)

            def fsw(s, acts=acts):
                for wl, kind, idx, bank in acts:
                    for sm_, n_ in wl:
                        s.wait_ge(sm_.h, n_)
                    if kind == "gelu":
                        s.activation(v["MID"][:, idx * 512:(idx + 1) * 512],
                                     b["PS6"][:, bank * 512:bank * 512 + 512],
                                     AF.Gelu,
                                     bias=b["B1S"][:, idx:idx + 1]
                                     ).then_inc(eS.h, 1)
                    else:
                        s.activation(v["EIG"][:, 0:512],
                                     b["PS6"][:, bank * 512:bank * 512 + 512],
                                     AF.Identity,
                                     bias=b["B2S"][:, idx:idx + 1]
                                     ).then_inc(eS.h, 1)
            self.blk.scalar(fsw)
            eS.inc(c.FT + c.DCH)

            if s_i == c.NSW // 2 - 1:
                def fag1(g):
                    g.collective_compute(
                        "AllGather", ALU.bypass,
                        replica_groups=[list(range(NCORE))],
                        ins=[self.d["eo_st"][0:c.CAP // 2, :]],
                        outs=[self.d["eo_all"][0:c.E * c.CAP // 2, :]])
                em.op("gpsimd", fag1, rd=("eo_stA",), wr=("eo_allA",), cc=True)

        def fag2(g):
            g.collective_compute(
                "AllGather", ALU.bypass, replica_groups=[list(range(NCORE))],
                ins=[self.d["eo_st"][c.CAP // 2:c.CAP, :]],
                outs=[self.d["eo_all"][c.E * c.CAP // 2:c.E * c.CAP, :]])
        em.op("gpsimd", fag2, rd=("eo_stB",), wr=("eo_allB",), cc=True)

    def combine_final(self):
        c, b, v, d, em = self.c, self.b, self.v, self.d, self.em
        for i in range(1 if "comb" in ABLATE else c.TT):
            def fr(sy, i=i):
                xr = d["sp_x1"].rearrange("(a p) dd -> a p dd", p=128)
                yr = d["sp_y"].rearrange("(a p) dd -> a p dd", p=128)
                sy.dma_start(out=b["R"][:, i * c.D:(i + 1) * c.D],
                             in_=xr[i, :, :])
                return sy.dma_start(out=v["MID"][:, 0:c.D], in_=yr[i, :, :])
            em.op("sync", fr, rd=("sp_x1", "sp_y"), wr=("R", "MID"), dma=True)

            def fg0(g, i=i):
                g.indirect_dma_start(
                    out=v["EIG"][:, 0:c.D], out_offset=None, in_=d["eo_all"],
                    in_offset=IndirectOffsetOnAxis(
                        ap=b["CIX"][:, 2 * i:2 * i + 1], axis=0))
                return g.indirect_dma_start(
                    out=v["MO1"][:, 0:c.D], out_offset=None, in_=d["eo_all"],
                    in_offset=IndirectOffsetOnAxis(
                        ap=b["CIX"][:, 2 * i + 1:2 * i + 2], axis=0))
            em.op("gpsimd", fg0, rd=("eo_allA", "eo_allB", "CIX"),
                  wr=("EIG", "MO1"), dma=True)

            def fmo(ve, i=i):
                ve.tensor_scalar(b["ROUT"][:, 0:c.D], v["EIG"][:, 0:c.D],
                                 b["GTS"][:, 14 * i + 12:14 * i + 13], None,
                                 op0=ALU.mult)
                ve.drain()
                ve.scalar_tensor_tensor(b["ROUT"][:, 1024:1024 + c.D],
                                        v["MO1"][:, 0:c.D],
                                        b["GTS"][:, 14 * i + 13:14 * i + 14],
                                        b["ROUT"][:, 0:c.D],
                                        op0=ALU.mult, op1=ALU.add)
                ve.drain()
                return ve.tensor_tensor(b["ROUT"][:, 0:c.D],
                                        b["ROUT"][:, 1024:1024 + c.D],
                                        v["MID"][:, 0:c.D], op=ALU.add)
            em.op("vector", fmo, rd=("EIG", "MO1", "MID", "GTS"), wr=("ROUT",))

            mb_, reg = self.ln_stats(b["ROUT"][:, 0:c.D], ("ROUT",),
                                     slot=i % 2)

            def fo(ve, i=i, mb_=mb_, reg=reg):
                sm = b["SM"]
                ve.tensor_scalar(b["ROUT"][:, 1024:1024 + c.D],
                                 b["ROUT"][:, 0:c.D], sm[:, mb_:mb_ + 1],
                                 sm[:, mb_ + 3:mb_ + 4],
                                 op0=ALU.subtract, op1=ALU.mult)
                ve.drain()
                return ve.tensor_tensor(b["ROUT"][:, 1024:1024 + c.D],
                                        b["ROUT"][:, 1024:1024 + c.D],
                                        b["R"][:, i * c.D:(i + 1) * c.D],
                                        op=ALU.add)
            em.op("vector", fo, rd=("ROUT", reg, "R"), wr=("ROUT",))

            def fdo(sy, i=i):
                return sy.dma_start(
                    out=d["xout"].rearrange("(a p) dd -> a p dd", p=128)[i, :, :],
                    in_=b["ROUT"][:, 1024:1024 + c.D])
            em.op("sync", fdo, rd=("ROUT",), wr=("xout",), dma=True)

        pairs = self._fence_pairs()

        def fin(sy):
            for s, n in pairs:
                sy.wait_ge(s.h, n)
        self.blk.sync(fin)


def _lhsT_tiles(w, colsel):
    Din = w.shape[0]
    dch = Din // 128
    out = np.empty((128, dch * 128), np.float32)
    sub = w[:, colsel]
    for cc in range(dch):
        out[:, cc * 128:(cc + 1) * 128] = sub[cc * 128:(cc + 1) * 128, :]
    return out


def make_inputs(c, ins, r):
    D, TB = c.D, c.TB
    m = {}
    x = np.asarray(ins["L0_attn_x"], np.float32).reshape(c.N, D)
    m["xin"] = np.ascontiguousarray(x[r * c.TOK:(r + 1) * c.TOK])
    Wq = np.asarray(ins["L0_attn_W_qkv"], np.float32)
    bq = np.asarray(ins["L0_attn_b_qkv"], np.float32)
    Aq = np.asarray(ins["L0_attn_A_qkv"], np.float32)
    Bq = np.asarray(ins["L0_attn_B_qkv"], np.float32)
    Wpo = np.asarray(ins["L0_attn_W_proj"], np.float32)
    bpo = np.asarray(ins["L0_attn_b_proj"], np.float32)
    Ap = np.asarray(ins["L0_attn_A_proj"], np.float32)
    Bp = np.asarray(ins["L0_attn_B_proj"], np.float32)
    Wqkv_i = np.asarray(ins["L0_ffn_Wqkv"], np.float32)
    Wp_i = np.asarray(ins["L0_ffn_Wproj"], np.float32)

    for p in range(c.PAIRS):
        h1 = c.UN * (r % c.GS) + 2 * p
        h2 = h1 + 1
        for ki, knd in enumerate("qkv"):
            base = ki * D
            csel = np.r_[base + h1 * 64:base + h1 * 64 + 64,
                         base + h2 * 64:base + h2 * 64 + 64]
            m[f"o_w{knd}{p}"] = _lhsT_tiles(Wq, csel)
            m[f"o_b{knd}{p}"] = np.ascontiguousarray(bq[csel])
            m[f"o_B{knd}{p}"] = np.ascontiguousarray(Bq[:, csel])
            csel_i = np.r_[h1 * 192 + ki * 64:h1 * 192 + ki * 64 + 64,
                           h2 * 192 + ki * 64:h2 * 192 + ki * 64 + 64]
            m[f"i_w{knd}{p}"] = _lhsT_tiles(Wqkv_i, csel_i)
    Ao = np.empty((128, c.DCH * 8), np.float32)
    App = np.empty((128, c.DCH * 8), np.float32)
    for cc in range(c.DCH):
        Ao[:, cc * 8:(cc + 1) * 8] = Aq[cc * 128:(cc + 1) * 128, :]
        App[:, cc * 8:(cc + 1) * 8] = Ap[cc * 128:(cc + 1) * 128, :]
    m["o_A"] = Ao
    m["o_Ap"] = App
    m["o_Bp"] = np.ascontiguousarray(Bp)
    for lay, W in (("o", Wpo), ("i", Wp_i)):
        out = np.empty((128, c.DCH * D), np.float32)
        for cc in range(c.DCH):
            for co in range(c.DCH):
                out[:, (cc * c.DCH + co) * 128:(cc * c.DCH + co + 1) * 128] = \
                    W[cc * 128:(cc + 1) * 128, co * 128:(co + 1) * 128]
        m[f"{lay}_wp"] = out
    m["o_bp"] = bpo
    m["i_g1"] = np.asarray(ins["L0_ffn_ln1_g"], np.float32)
    m["i_b1"] = np.asarray(ins["L0_ffn_ln1_b"], np.float32)
    m["i_g2"] = np.asarray(ins["L0_ffn_ln2_g"], np.float32)
    m["i_b2"] = np.asarray(ins["L0_ffn_ln2_b"], np.float32)
    Wg = np.asarray(ins["L0_ffn_Wgate"], np.float32)
    wg = np.empty((128, c.DCH * 8), np.float32)
    for cc in range(c.DCH):
        wg[:, cc * 8:(cc + 1) * 8] = Wg[cc * 128:(cc + 1) * 128, :]
    m["wgate"] = wg
    W1 = np.asarray(ins["L0_ffn_W1"], np.float32)[r]
    W2 = np.asarray(ins["L0_ffn_W2"], np.float32)[r]
    w1r = np.empty((c.FT, 128, c.DCH * 128), np.float32)
    for ftl in range(c.FT):
        for cc in range(c.DCH):
            w1r[ftl, :, cc * 128:(cc + 1) * 128] = \
                W1[cc * 128:(cc + 1) * 128, ftl * 128:(ftl + 1) * 128]
    m["w1e"] = w1r
    w2r = np.empty((c.DCH, 128, c.FT * 128), np.float32)
    for dtl in range(c.DCH):
        for fc in range(c.FT):
            w2r[dtl, :, fc * 128:(fc + 1) * 128] = \
                W2[fc * 128:(fc + 1) * 128, dtl * 128:(dtl + 1) * 128]
    m["w2e"] = w2r
    m["b1e"] = np.asarray(ins["L0_ffn_b1"], np.float32)[r]
    m["b2e"] = np.asarray(ins["L0_ffn_b2"], np.float32)[r]
    m["ident_f"] = np.eye(128, dtype=np.float32)
    inv = 1.0 / 10000.0 ** (np.arange(0, HD, 2, dtype=np.float64) / HD)
    ang = np.arange(TB, dtype=np.float64)[:, None] * inv[None, :]
    cos = np.cos(ang).T.astype(np.float32)
    sin = np.sin(ang).T.astype(np.float32)
    m["cc_tab"] = np.ascontiguousarray(np.concatenate([cos] * 4, 0))
    m["ss_tab"] = np.ascontiguousarray(np.concatenate([-sin, sin, -sin, sin], 0))
    cm = np.zeros((128, 4 * QCH), np.float32)
    for j in range(4):
        kpos = j * 128 + np.arange(128)[:, None]
        qpos = np.arange(QCH)[None, :]
        cm[:, j * QCH:(j + 1) * QCH] = (kpos <= qpos).astype(np.float32)
    m["cmask"] = cm
    tv = np.empty((128, c.TT), np.float32)
    for i in range(c.TT):
        tv[:, i] = r * c.TOK + i * 128 + np.arange(128)
    m["tvals"] = tv
    m["iot"] = np.arange(128, dtype=np.float32).reshape(128, 1)
    m["io8"] = np.tile(np.arange(8, dtype=np.float32), (128, 1))
    pfx = np.zeros(8, np.float32)
    pfx[:r] = 1.0
    m["pfx"] = pfx
    m["ebd"] = np.full((128, 1), (c.CAP * r) // 128, np.float32)
    m["mb"] = np.full((128, 1), (r % c.GS) * c.GS * c.SR, np.float32)
    return m


_PROG_CACHE = {}


def kernel(**inputs):
    c = FULL
    if "full" not in _PROG_CACHE:
        _PROG_CACHE["full"] = K(c).build()
    nc = _PROG_CACHE["full"]
    in_maps = [make_inputs(c, inputs, r) for r in range(NCORE)]
    res = run_bass_kernel_spmd(nc, in_maps, core_ids=list(range(NCORE)))
    out = np.concatenate([np.asarray(res.results[r]["xout"], np.float32)
                          for r in range(NCORE)], 0)
    return out.reshape(c.B, c.TB, c.D)
